# version 16
# baseline (speedup 1.0000x reference)
import math
from contextlib import ExitStack

import numpy as np

import concourse.bass as bass
import concourse.mybir as mybir
from concourse.bass_utils import run_bass_kernel_spmd

F32 = mybir.dt.float32
BF16 = mybir.dt.bfloat16
I32 = mybir.dt.int32
AF = mybir.ActivationFunctionType
OP = mybir.AluOpType
AX = mybir.AxisListType

S = 2048
D = 1024
NH = 16
NSL = S // 128
NCH = D // 128
MF = 16384
SBK = 512
NBK = S // SBK
EPS = 1e-5
MSUB = 12.0
TWO_PI = 6.283185307179586
HALF_PI = 1.5707963267948966
CLN = math.log(10000.0) / 512.0

ENGINES = ("sync", "gpsimd", "tensor", "vector", "scalar")


def slope(h):
    return 2.0 ** (-8.0 * (h + 1) / 16.0)


def kc_min_for(h, qc):
    lo = 0
    for d in range(1, 13):
        if -MSUB + slope(h) * (127.0 - 128.0 * d) <= -124.0:
            lo = max(lo, 4 * qc - d + 1)
    return max(0, lo)


class Prog:
    def __init__(self, nc, es):
        self.nc = nc
        self.es = es
        self.ops = {e: [] for e in ENGINES}
        self.sems = {}
        self.cnt = {}
        for name in ("t", "v", "s", "g"):
            self._mk(name)

    def _mk(self, name):
        self.sems[name] = self.es.enter_context(self.nc.semaphore(name))
        self.cnt[name] = 0

    ENG_SEM = {"vector": "v", "scalar": "s", "gpsimd": "g"}

    gate = {}

    def op(self, eng, fn, waits=(), inc=None):
        gw = self.gate.pop(eng, None) if self.gate else None
        if gw:
            waits = list(waits) + gw
        esem = self.ENG_SEM.get(eng)
        pre_self = self.cnt.get(esem, 0) if esem is not None else 0
        incs = []
        is_dma = False
        if inc is not None:
            for snm in ([inc] if isinstance(inc, str) else list(inc)):
                if snm not in self.sems:
                    self._mk(snm)
                n = 16 if snm.startswith("d") else 1
                if snm.startswith("d"):
                    is_dma = True
                self.cnt[snm] += n
                incs.append((snm, n))
        flat = []
        for w in waits:
            if w is None:
                continue
            if isinstance(w, list):
                flat.extend(x for x in w if x is not None)
            else:
                flat.append(w)
        if esem is not None:
            flat.append((esem, pre_self))
            if not is_dma and not any(s == esem for (s, _n) in incs):
                self.cnt[esem] += 1
                incs.append((esem, 1))
        flat = tuple((s, v) for (s, v) in flat if v > 0)
        self.ops[eng].append((fn, flat, incs))
        return (incs[0][0], self.cnt[incs[0][0]]) if incs else None

    def now(self, sem):
        if sem not in self.sems:
            self._mk(sem)
        return (sem, self.cnt[sem])

    def barrier(self):
        gate = [self.now(x) for x in list(self.sems.keys())]
        self.gate = {e: list(gate) for e in ENGINES}

    def build(self, blk):
        sems = self.sems
        for ename in ENGINES:
            eng_ops = self.ops[ename]
            if not eng_ops:
                continue

            def make(ops_):
                def body(eng):
                    lw = {}
                    for fn, waits, incs in ops_:
                        for (snm, v) in waits:
                            if lw.get(snm, 0) < v:
                                eng.wait_ge(sems[snm], v)
                                lw[snm] = v
                        ins = fn(eng)
                        for (snm, n) in incs:
                            ins = ins.then_inc(sems[snm], n)
                return body

            getattr(blk, ename)(make(eng_ops))


def build_nc(phase_limit=99):
    nc = bass.Bass("TRN2", target_bir_lowering=False)

    x_d = nc.dram_tensor("x", [S, D], F32, kind="ExternalInput").ap()
    wqT_d = nc.dram_tensor("wqT", [D, D], F32, kind="ExternalInput").ap()
    wkT_d = nc.dram_tensor("wkT", [D, D], F32, kind="ExternalInput").ap()
    wvT_d = nc.dram_tensor("wvT", [D, D], F32, kind="ExternalInput").ap()
    owT_d = nc.dram_tensor("owT", [D, D], F32, kind="ExternalInput").ap()
    ob_d = nc.dram_tensor("ob", [1, D], F32, kind="ExternalInput").ap()
    blkA_d = nc.dram_tensor("blkA", [4, 8, 128, 128], F32, kind="ExternalInput").ap()
    blkB_d = nc.dram_tensor("blkB", [4, 8, 128, 512], F32, kind="ExternalInput").ap()
    wcT_d = nc.dram_tensor("wcT", [MF, D], F32, kind="ExternalInput").ap()
    out_d = nc.dram_tensor("out", [S, D], F32, kind="ExternalOutput").ap()
    xres_d = nc.dram_tensor("xres", [S, D], F32, kind="Internal").ap()
    btscr_d = nc.dram_tensor("btscr", [NSL, 128, 8, 128], BF16, kind="Internal").ap()
    wcb_d = nc.dram_tensor("wcb", [MF, D], BF16, kind="Internal").ap()
    w2scr_d = nc.dram_tensor("w2scr", [4, 8, 128, 512], BF16, kind="Internal").ap()
    dscr_d = nc.dram_tensor("dscr", [NH, 5, 128, 512], BF16, kind="Internal").ap()

    es = ExitStack()
    P = Prog(nc, es)

    cpool = es.enter_context(nc.sbuf_tensor("cpool", [128, 200], F32))
    ident = es.enter_context(nc.sbuf_tensor("ident", [128, 128], BF16))
    onesF = es.enter_context(nc.sbuf_tensor("onesF", [128, 128], F32))
    onesB = es.enter_context(nc.sbuf_tensor("onesB", [128, 128], BF16))
    xfar = es.enter_context(nc.sbuf_tensor("xfar", [128, 512], BF16))
    xdg = [es.enter_context(nc.sbuf_tensor(f"xdg{i}", [128, 512], BF16)) for i in range(4)]
    rrow = es.enter_context(nc.sbuf_tensor("rrow", [128, 1024], F32))
    sqd = es.enter_context(nc.sbuf_tensor("sqd", [128, 1024], F32))
    lnst = es.enter_context(nc.sbuf_tensor("lnst", [128, 16], F32))
    browb = es.enter_context(nc.sbuf_tensor("browb", [1, D], BF16))

    CP_FAR0 = 0
    CP_M12 = 192
    CP_EPS = 193
    CP_FRQ0 = 194
    cpool_vals = {}
    for h in range(NH):
        for d in range(1, 13):
            cpool_vals[CP_FAR0 + h * 12 + (d - 1)] = -MSUB + slope(h) * (127.0 - 128.0 * d)
    cpool_vals[CP_M12] = -MSUB
    cpool_vals[CP_EPS] = EPS
    for c in range(4):
        cpool_vals[CP_FRQ0 + c] = -128.0 * c * CLN

    items = sorted(cpool_vals.items())
    for j, (idx, val) in enumerate(items):
        P.op("vector", (lambda i=idx, vv=val: (lambda e: e.memset(cpool[:, i:i + 1], vv)))(),
             inc="v" if j == len(items) - 1 else None)
    ev_cpool = P.now("v")
    P.op("vector", lambda e: e.memset(onesF[:], 1.0))
    P.op("vector", lambda e: e.memset(onesB[:], 1.0))
    P.op("vector", lambda e: e.memset(sqd[:], 1.0))
    P.op("vector", lambda e: e.memset(rrow[:], 1.0), inc="v")
    ev_vconst = P.now("v")

    P.op("gpsimd", lambda e: e.affine_select(ident[:], onesB[:], [[-1, 128]],
                                             OP.is_equal, 0.0, base=0,
                                             channel_multiplier=1),
         waits=[ev_vconst], inc="g")
    P.op("gpsimd", lambda e: e.iota(xfar[:], [[-1, 512]], base=-127,
                                    channel_multiplier=1,
                                    allow_small_or_imprecise_dtypes=True), inc="g")
    for sf in range(4):
        P.op("gpsimd", (lambda s_=sf: (lambda e: e.iota(
            xdg[s_][:], [[-1, 512]], base=128 * s_, channel_multiplier=1,
            allow_small_or_imprecise_dtypes=True)))(), inc="g")
    ev_giota = P.now("g")
    for sf in range(4):
        P.op("gpsimd", (lambda s_=sf: (lambda e: e.affine_select(
            xdg[s_][:], xdg[s_][:], [[1, 512]], OP.is_ge, -1.0e5,
            base=-128 * s_, channel_multiplier=-1)))(),
             waits=[ev_giota], inc="g")
    ev_gconst = P.now("g")

    P.op("gpsimd", lambda e: e.dma_start(out=browb[:], in_=ob_d), inc="dbr")
    ev_browb = P.now("dbr")

    dbg_d = None
    if phase_limit < 99:
        dbg_d = nc.dram_tensor("dbg", [128, 16384], F32, kind="ExternalOutput").ap()

    def dump_dbg(src_ap, cols, prior_ev, rows=128):
        if src_ap.dtype == F32:
            P.op("scalar", lambda e: e.dma_start(out=dbg_d[0:rows, 0:cols], in_=src_ap),
                 waits=[prior_ev], inc="ddb")
        else:
            n = 0
            last_st = None
            while n < cols:
                wdt = min(1024, cols - n)
                P.op("vector", (lambda nn=n, ww=wdt, sp=src_ap, r_=rows: (lambda e: e.tensor_copy(
                    sqd[0:r_, 0:ww], sp[:, nn:nn + ww])))(), waits=[prior_ev, last_st], inc="v")
                evc = P.now("v")
                P.op("scalar", (lambda nn=n, ww=wdt, r_=rows: (lambda e: e.dma_start(
                    out=dbg_d[0:r_, nn:nn + ww], in_=sqd[0:r_, 0:ww])))(),
                     waits=[evc], inc="ddb")
                last_st = P.now("ddb")
                n += wdt

    phase_stacks = []

    def finish():
        for snm in list(P.sems.keys()):
            if snm.startswith("d"):
                tot = P.cnt[snm]
                if tot > 0:
                    P.op("sync", (lambda s_=snm, t_=tot: (lambda e: e.wait_ge(
                        P.sems[s_], t_)))())
        for st in reversed(phase_stacks):
            st.close()
        blk = es.enter_context(nc.Block())
        P.build(blk)
        es.close()
        return nc

    def emit_ln(xtile, otile, waits0, col0):
        c = col0
        P.op("vector", lambda e: e.reduce_sum(lnst[:, c + 0:c + 1], xtile[:], axis=AX.X),
             waits=list(waits0), inc="v")
        ev1 = P.now("v")
        P.op("scalar", lambda e: e.activation(lnst[:, c + 1:c + 2], lnst[:, c + 0:c + 1],
                                              AF.Copy, bias=0.0, scale=-1.0 / D),
             waits=[ev1], inc="s")
        P.op("scalar", lambda e: e.activation(sqd[:], xtile[:], AF.Square,
                                              bias=lnst[:, c + 1:c + 2], scale=1.0,
                                              accum_out=lnst[:, c + 2:c + 3]), inc="s")
        P.op("scalar", lambda e: e.activation(lnst[:, c + 3:c + 4], lnst[:, c + 2:c + 3],
                                              AF.Ln, bias=cpool[:, CP_EPS:CP_EPS + 1],
                                              scale=1.0 / D),
             waits=[ev_cpool], inc="s")
        P.op("scalar", lambda e: e.activation(lnst[:, c + 4:c + 5], lnst[:, c + 3:c + 4],
                                              AF.Exp, bias=0.0, scale=-0.5), inc="s")
        ev5 = P.now("s")
        P.op("vector", lambda e: e.tensor_mul(lnst[:, c + 5:c + 6], lnst[:, c + 1:c + 2],
                                              lnst[:, c + 4:c + 5]),
             waits=[ev5], inc="v")
        ev6 = P.now("v")
        P.op("scalar", lambda e: e.activation(otile[:], xtile[:], AF.Identity,
                                              bias=lnst[:, c + 5:c + 6],
                                              scale=lnst[:, c + 4:c + 5]),
             waits=[ev6], inc="s")
        return P.now("s")

    def emit_transposes(src_tile, n_slices, dst_fn, waits0, pair, state):
        evs = state.setdefault("evs", [])
        for i in range(n_slices):
            k = state.get("k", 0)
            bank = pair[k % 2]
            slot = (k // 2) % 8
            wts = list(waits0) + [ev_gconst]
            if len(evs) >= 2:
                wts.append(evs[-2])
            if len(evs) >= 16:
                wts.append(evs[-16])
            P.op("tensor", (lambda b_=bank, sl_=slot, i_=i, sp=src_tile: (lambda e: e.transpose(
                b_[:, 128 * sl_:128 * (sl_ + 1)],
                sp[:, 128 * i_:128 * (i_ + 1)], ident[:])))(),
                 waits=wts, inc="t")
            ev_tr = P.now("t")
            dst = dst_fn(i)
            P.op("vector", (lambda b_=bank, sl_=slot, d_=dst: (lambda e: e.tensor_copy(
                d_, b_[:, 128 * sl_:128 * (sl_ + 1)])))(),
                 waits=[ev_tr], inc="v")
            evs.append(P.now("v"))
            state["k"] = k + 1
        return P.now("t"), P.now("v")

    es_at = ExitStack()
    phase_stacks.append(es_at)
    At = es_at.enter_context(nc.sbuf_tensor("At", [128, S * NCH], BF16, side="right"))
    p1_es = ExitStack()
    phase_stacks.append(p1_es)
    xin = [p1_es.enter_context(nc.sbuf_tensor(f"xin{i}", [128, D], F32, side="right"))
           for i in range(2)]
    atm = [p1_es.enter_context(nc.sbuf_tensor(f"atm{i}", [128, D], BF16, side="right"))
           for i in range(2)]
    pbtA = p1_es.enter_context(nc.psum_tensor("pbtA", [128, 1024], BF16))
    pbtB = p1_es.enter_context(nc.psum_tensor("pbtB", [128, 1024], BF16))

    xin_free = [None, None]
    atm_free = [None, None]
    tst = {}
    for sl in range(NSL):
        bi = sl % 2
        P.op("sync", (lambda s_=sl, b=bi: (lambda e: e.dma_start(
            out=xin[b][:], in_=x_d[128 * s_:128 * (s_ + 1), :])))(),
             waits=[xin_free[bi]], inc=f"dx{bi}")
        ev_x = P.now(f"dx{bi}")
        ev_a = emit_ln(xin[bi], atm[bi], [ev_x, atm_free[bi]], 8 * bi)
        xin_free[bi] = ev_a
        ev_t, ev_v = emit_transposes(
            atm[bi], NCH,
            (lambda s_=sl: (lambda i: At[:, S * i + 128 * s_: S * i + 128 * (s_ + 1)]))(),
            [ev_a], (pbtA, pbtB), tst)
        atm_free[bi] = ev_t
    ev_At = P.now("v")

    if phase_limit <= 1:
        dump_dbg(At[:, 0:16384], 16384, ev_At)
        return finish()

    p1_es.close()
    phase_stacks.remove(p1_es)
    P.barrier()
    es_qkv = ExitStack()
    phase_stacks.append(es_qkv)
    QT = es_qkv.enter_context(nc.sbuf_tensor("QT", [128, S * NCH], BF16))
    KT = es_qkv.enter_context(nc.sbuf_tensor("KT", [128, S * NCH], BF16))
    Vx = es_qkv.enter_context(nc.sbuf_tensor("Vx", [128, NSL * 1040], BF16))

    p2_es = ExitStack()
    phase_stacks.append(p2_es)
    iotaS = p2_es.enter_context(nc.sbuf_tensor("iotaS", [128, S], F32, side="right"))
    scrW = p2_es.enter_context(nc.sbuf_tensor("scrW", [128, S], F32, side="right"))
    scrI = p2_es.enter_context(nc.sbuf_tensor("scrI", [128, S // 4], I32, side="right"))
    scrB = p2_es.enter_context(nc.sbuf_tensor("scrB", [128, S // 4], F32, side="right"))
    sintab = p2_es.enter_context(nc.sbuf_tensor("sintab", [128, S], BF16, side="right"))
    costab = p2_es.enter_context(nc.sbuf_tensor("costab", [128, S], BF16, side="right"))
    rtmp = [p2_es.enter_context(nc.sbuf_tensor(f"rtmp{i}", [128, S], BF16, side="right"))
            for i in range(2)]
    freqs = p2_es.enter_context(nc.sbuf_tensor("freqs", [128, 8], F32, side="right"))
    wqb = [p2_es.enter_context(nc.sbuf_tensor(f"wqb{i}", [128, 2048], BF16, side="right"))
           for i in range(2)]
    vts = [p2_es.enter_context(nc.sbuf_tensor(f"vts{i}", [128, S], BF16, side="right"))
           for i in range(2)]
    pqk = [p2_es.enter_context(nc.psum_tensor(f"pqk{i}", [128, 512], F32)) for i in range(4)]
    pbt2A = p2_es.enter_context(nc.psum_tensor("pbt2A", [128, 1024], BF16))
    pbt2B = p2_es.enter_context(nc.psum_tensor("pbt2B", [128, 1024], BF16))

    dstage = [p2_es.enter_context(nc.sbuf_tensor(f"dstage{i}", [128, 512], BF16, side="right"))
              for i in range(2)]
    qk_evt = {}
    wqb_free = [None, None]
    bank_free = [None] * 4
    for (pname, wd, dst, scl) in (("q", wqT_d, QT, 0.125), ("k", wkT_d, KT, 1.0)):
        for qtr in range(4):
            wi = qtr % 2
            for e8 in range(8):
                P.op("gpsimd", (lambda wd_=wd, q_=qtr, e_=e8, i_=wi: (lambda e: e.dma_start(
                    out=wqb[i_][:, 256 * e_:256 * (e_ + 1)],
                    in_=wd_[128 * e_:128 * (e_ + 1), 256 * q_:256 * (q_ + 1)])))(),
                     waits=[wqb_free[wi]] if e8 == 0 else (), inc=f"dw{wi}")
            ev_w = P.now(f"dw{wi}")
            last_mm = None
            for d2 in range(2):
                ch = 2 * qtr + d2
                for e8 in range(8):
                    for sc in range(4):
                        wts = [ev_w, ev_At]
                        if e8 == 0:
                            wts.append(bank_free[sc])
                        P.op("tensor", (lambda i_=wi, e_=e8, d_=d2, s_=sc: (lambda e: e.matmul(
                            pqk[s_][:],
                            wqb[i_][:, 256 * e_ + 128 * d_:256 * e_ + 128 * (d_ + 1)],
                            At[:, S * e_ + 512 * s_:S * e_ + 512 * (s_ + 1)],
                            start=(e_ == 0), stop=(e_ == 7))))(),
                             waits=wts if e8 == 0 else (), inc="t")
                ev_mm = P.now("t")
                last_mm = ev_mm
                for sc in range(4):
                    P.op("scalar", (lambda s_=sc, ch_=ch, dst_=dst, sc_=scl: (lambda e: e.activation(
                        dst_[:, S * ch_ + 512 * s_:S * ch_ + 512 * (s_ + 1)],
                        pqk[s_][:], AF.Copy, bias=0.0, scale=sc_)))(),
                         waits=[ev_mm], inc="s")
                    bank_free[sc] = P.now("s")
                qk_evt[(pname, ch)] = P.now("s")
            wqb_free[wi] = last_mm

    P.op("gpsimd", lambda e: e.iota(iotaS[:], [[1, S]], base=0, channel_multiplier=0,
                                    allow_small_or_imprecise_dtypes=True), inc="g")
    ev_iotaS = P.now("g")
    P.op("gpsimd", lambda e: e.iota(freqs[:, 0:1], [[0, 1]], base=0, channel_multiplier=1,
                                    allow_small_or_imprecise_dtypes=True), inc="g")
    ev_iotaP = P.now("g")
    for c in range(4):
        P.op("scalar", (lambda c_=c: (lambda e: e.activation(
            freqs[:, 4 + c_:5 + c_], freqs[:, 0:1], AF.Exp,
            bias=cpool[:, CP_FRQ0 + c_:CP_FRQ0 + c_ + 1], scale=-CLN)))(),
             waits=[ev_iotaP, ev_cpool], inc="s")
    ev_freq = P.now("s")
    P.op("vector", lambda e: e.tensor_scalar_mul(freqs[:, 0:4], freqs[:, 4:8],
                                                 1.0 / TWO_PI),
         waits=[ev_freq], inc="v")
    ev_freq2 = P.now("v")

    def emit_tab(tab, c, add_quarter, waits0):
        if add_quarter:
            P.op("vector", (lambda c_=c: (lambda e: e.tensor_scalar(
                scrW[:], iotaS[:], freqs[:, c_:c_ + 1], 0.25, OP.mult, OP.add)))(),
                 waits=list(waits0), inc="v")
        else:
            P.op("vector", (lambda c_=c: (lambda e: e.tensor_scalar_mul(
                scrW[:], iotaS[:], freqs[:, c_:c_ + 1])))(),
                 waits=list(waits0), inc="v")
        for hf in range(4):
            P.op("vector", (lambda h_=hf: (lambda e: e.tensor_copy(
                scrI[:], scrW[:, 512 * h_:512 * (h_ + 1)])))(), inc="v")
            P.op("vector", lambda e: e.tensor_copy(scrB[:], scrI[:]), inc="v")
            P.op("vector", (lambda h_=hf: (lambda e: e.tensor_tensor(
                scrW[:, 512 * h_:512 * (h_ + 1)],
                scrW[:, 512 * h_:512 * (h_ + 1)], scrB[:], OP.subtract)))(), inc="v")
        P.op("vector", lambda e: e.scalar_tensor_tensor(
            scrW[:], scrW[:], 0.5, scrW[:], OP.is_ge, OP.subtract), inc="v")
        ev_w = P.now("v")
        P.op("scalar", (lambda t_=tab: (lambda e: e.activation(
            t_[:], scrW[:], AF.Sin, bias=0.0, scale=-TWO_PI)))(),
             waits=[ev_w, prev_tab_reader[0]], inc="s")
        return P.now("s")

    rope_evt = {}
    prev_tab_reader = [None]
    prev_act = [None]
    for c in range(4):
        prev_act[0] = emit_tab(sintab, c, False, [ev_iotaS, ev_freq2, prev_act[0]])
        ev_cos = emit_tab(costab, c, True, [ev_iotaS, ev_freq2, prev_act[0]])
        prev_act[0] = ev_cos
        for (pname, T) in (("q", QT), ("k", KT)):
            lo = lambda T_=T, o_=S * c: T_[:, o_:o_ + S]
            hi = lambda T_=T, o_=S * (c + 4): T_[:, o_:o_ + S]
            P.op("vector", (lambda hi_=hi(): (lambda e: e.tensor_mul(
                rtmp[0][:], hi_, sintab[:])))(),
                 waits=[ev_cos, qk_evt[(pname, c)], qk_evt[(pname, c + 4)]], inc="v")
            P.op("vector", (lambda hi_=hi(): (lambda e: e.tensor_mul(
                rtmp[1][:], hi_, costab[:])))(), inc="v")
            P.op("vector", (lambda lo_=lo(), hi_=hi(): (lambda e: e.tensor_mul(
                hi_, lo_, sintab[:])))(), inc="v")
            P.op("vector", (lambda hi_=hi(): (lambda e: e.tensor_add(
                hi_, hi_, rtmp[1][:])))(), inc="v")
            P.op("vector", (lambda lo_=lo(): (lambda e: e.tensor_mul(
                lo_, lo_, costab[:])))(), inc="v")
            P.op("vector", (lambda lo_=lo(): (lambda e: e.tensor_sub(
                lo_, lo_, rtmp[0][:])))(), inc="v")
            rope_evt[(pname, c)] = rope_evt[(pname, c + 4)] = P.now("v")
        prev_tab_reader[0] = P.now("v")

    if phase_limit <= 2:
        dump_dbg(QT[:, 0:16384], 16384, P.now("v"))
        return finish()

    for sl in range(NSL):
        P.op("vector", (lambda s_=sl: (lambda e: e.memset(
            Vx[:, 1040 * s_:1040 * (s_ + 1)].rearrange(
                "p (a b) -> p a b", a=16)[:, :, 64:65], 1.0)))(),
             inc="v" if sl == NSL - 1 else None)
    ones_evt = P.now("v")

    dst_free = [None, None]
    k5 = 0
    for h in range(NH):
        sh = slope(h)
        for t5 in range(5):
            di = k5 % 2
            k5 += 1
            xin_t = xfar if t5 == 4 else xdg[t5]
            P.op("scalar", (lambda sh_=sh, x_=xin_t, i_=di: (lambda e: e.activation(
                dstage[i_][:], x_[:], AF.Exp, bias=0.0, scale=sh_)))(),
                 waits=[ev_gconst, dst_free[di]], inc="s")
            ev_dx = P.now("s")
            P.op("scalar", (lambda h_=h, t_=t5, i_=di: (lambda e: e.dma_start(
                out=dscr_d[h_, t_], in_=dstage[i_][:])))(),
                 waits=[ev_dx], inc=f"dds{di}")
            dst_free[di] = P.now(f"dds{di}")
    ev_dscr = [P.now("dds0"), P.now("dds1")]

    for g in range(8):
        for e8 in range(8):
            i = e8 % 2
            P.op("gpsimd", (lambda e_=e8, i_=i, g_=g: (lambda e: e.dma_start(
                out=wqb[i_][:, 0:1024],
                in_=wvT_d[128 * e_:128 * (e_ + 1), :])))(),
                 waits=[wqb_free[i]], inc=f"dw{i}")
            ev_w = P.now(f"dw{i}")
            for s2 in range(2):
                sl = 2 * g + s2
                for dv2 in range(2):
                    bk = s2 * 2 + dv2
                    wts = [ev_w, ev_At]
                    if e8 == 0:
                        wts.append(bank_free[bk])
                    P.op("tensor", (lambda e_=e8, s_=sl, d_=dv2, i_=i, b_=bk: (lambda e: e.matmul(
                        pqk[b_][:],
                        At[:, S * e_ + 128 * s_:S * e_ + 128 * (s_ + 1)],
                        wqb[i_][:, 512 * d_:512 * (d_ + 1)],
                        start=(e_ == 0), stop=(e_ == 7))))(),
                         waits=wts, inc="t")
            wqb_free[i] = P.now("t")
        ev_mm = P.now("t")
        for s2 in range(2):
            sl = 2 * g + s2
            for dv2 in range(2):
                bk = s2 * 2 + dv2
                h0 = 8 * dv2
                P.op("scalar", (lambda s_=sl, b_=bk, h_=h0: (lambda e: e.activation(
                    Vx[:, 1040 * s_ + 65 * h_:1040 * s_ + 65 * h_ + 520].rearrange(
                        "p (a b) -> p a b", a=8)[:, :, 0:64],
                    pqk[b_][:].rearrange("p (a b) -> p a b", a=8),
                    AF.Copy, bias=0.0, scale=1.0)))(),
                     waits=[ev_mm, ones_evt], inc="s")
                bank_free[bk] = P.now("s")
    ev_Vx = P.now("v")

    if phase_limit <= 3:
        dump_dbg(Vx[:, 0:16384], 16384, ev_Vx)
        return finish()

    p2_es.close()
    phase_stacks.remove(p2_es)
    es_at.close()
    phase_stacks.remove(es_at)
    P.barrier()
    p3_es = ExitStack()
    phase_stacks.append(p3_es)
    p3p_es = ExitStack()
    phase_stacks.append(p3p_es)
    Ost = p3_es.enter_context(nc.sbuf_tensor("Ost", [128, NH * S], BF16, side="right"))
    dfar = [p3_es.enter_context(nc.sbuf_tensor(f"dfar{j}", [128, 512], BF16, side="right"))
            for j in range(2)]
    ddg = [[p3_es.enter_context(nc.sbuf_tensor(f"ddg{j}_{i}", [128, 512], BF16, side="right"))
            for i in range(4)] for j in range(2)]
    pbuf = [p3_es.enter_context(nc.sbuf_tensor(f"pbuf{i}", [128, 1024], BF16, side="right"))
            for i in range(4)]
    qstg = [p3_es.enter_context(nc.sbuf_tensor(f"qstg{i}", [128, 512], BF16, side="right"))
            for i in range(2)]
    z128 = p3_es.enter_context(nc.sbuf_tensor("z128", [128, 128], F32, side="right"))
    rbt = [p3_es.enter_context(nc.sbuf_tensor(f"rbt{i}", [128, 512], F32, side="right"))
           for i in range(1)]
    ps = [p3p_es.enter_context(nc.psum_tensor(f"ps{i}", [128, 1024], F32)) for i in range(2)]
    po = [p3p_es.enter_context(nc.psum_tensor(f"po{i}", [128, 512], F32)) for i in range(2)]
    pc = p3p_es.enter_context(nc.psum_tensor("pc", [128, 512], F32))

    ba = [p3_es.enter_context(nc.sbuf_tensor(f"ba{i}", [128, 128], BF16, side="right"))
          for i in range(2)]
    bbuf = [p3_es.enter_context(nc.sbuf_tensor(f"bbuf{i}", [128, 512], BF16, side="right"))
            for i in range(2)]
    w2asm = [p3_es.enter_context(nc.sbuf_tensor(f"w2asm{i}", [128, 512], BF16, side="right"))
             for i in range(2)]
    wcv = [p3_es.enter_context(nc.sbuf_tensor(f"wcv{i}", [128, D // 2], BF16, side="right"))
           for i in range(3)]
    pw2 = p3p_es.enter_context(nc.psum_tensor("pw2", [128, 512], F32))

    ba_free = [None, None]
    bb_free = [None, None]
    w2_free = [None, None]
    pw2_free = None
    for h4 in range(4):
        for oc in range(8):
            i = (h4 * 8 + oc) % 2
            P.op("gpsimd", (lambda h_=h4, o_=oc, i_=i: (lambda e: e.dma_start(
                out=ba[i_][:], in_=blkA_d[h_, o_])))(),
                 waits=[ba_free[i]], inc=f"dba{i}")
            ev_lda = P.now(f"dba{i}")
            P.op("gpsimd", (lambda h_=h4, o_=oc, i_=i: (lambda e: e.dma_start(
                out=bbuf[i_][:], in_=blkB_d[h_, o_])))(),
                 waits=[bb_free[i]], inc=f"dbb{i}")
            ev_ldb = P.now(f"dbb{i}")
            P.op("tensor", (lambda i_=i: (lambda e: e.matmul(
                pw2[:], ba[i_][:], bbuf[i_][:], start=True, stop=True)))(),
                 waits=[ev_lda, ev_ldb, pw2_free], inc="t")
            ev_mm = P.now("t")
            ba_free[i] = ev_mm
            bb_free[i] = ev_mm
            P.op("vector", (lambda i_=i: (lambda e: e.tensor_copy(w2asm[i_][:], pw2[:])))(),
                 waits=[ev_mm, w2_free[i]], inc="v")
            pw2_free = P.now("v")
            P.op("scalar", (lambda h_=h4, o_=oc, i_=i: (lambda e: e.dma_start(
                out=w2scr_d[h_, o_], in_=w2asm[i_][:])))(),
                 waits=[pw2_free], inc=f"dws{i}")
            w2_free[i] = P.now(f"dws{i}")
    ev_w2s = [P.now("dws0"), P.now("dws1")]

    wcv_ld = [None, None, None]
    wcv_st = [None, None, None]
    for mh in range(256):
        mc, hh = divmod(mh, 2)
        i = mh % 3
        P.op("gpsimd", (lambda m_=mc, h_=hh, i_=i: (lambda e: e.dma_start(
            out=wcv[i_][:], in_=wcT_d[128 * m_:128 * (m_ + 1), 512 * h_:512 * (h_ + 1)])))(),
             waits=[wcv_st[i]], inc=f"dcv{i}")
        wcv_ld[i] = P.now(f"dcv{i}")
        P.op("gpsimd", (lambda m_=mc, h_=hh, i_=i: (lambda e: e.dma_start(
            out=wcb_d[128 * m_:128 * (m_ + 1), 512 * h_:512 * (h_ + 1)], in_=wcv[i_][:])))(),
             waits=[wcv_ld[i]], inc=f"dcb{i}")
        wcv_st[i] = P.now(f"dcb{i}")
    ev_wcb_done = [P.now("dcb0"), P.now("dcb1"), P.now("dcb2")]

    ps_free = [None] * 2
    pb_free = [None] * 4
    qs_free = [None, None]
    qs_evt = {}
    pend_s = []
    pair_no = [0]
    po_free = [None, None]
    pc_free = None
    rb_free = None
    drow_free = None
    rrow_free = None
    ost_evt = {}

    LOOK = 6
    jobs = []
    for h in range(NH):
        for qc in range(4):
            kmin = kc_min_for(h, qc)
            diag = [4 * qc + i for i in range(4)]
            far = [k for k in range(kmin, 4 * qc)]
            jobs.append((h, qc, diag + far))
    chunks = []
    for ji, (h, qc, kcs) in enumerate(jobs):
        for i, kc in enumerate(kcs):
            chunks.append((ji, h, qc, kc, i == 0, i == len(kcs) - 1))
    NCK = len(chunks)

    dtile_evt = {}
    dtile_lastuse = {}
    s1evt = [None] * NCK
    tails1 = []
    ev_avlast = {}

    def emit_dtiles(h):
        if h >= NH or h in dtile_evt:
            return
        hp = h % 2
        prev = dtile_lastuse.get(h - 2)
        for sf in range(4):
            P.op("sync", (lambda h_=h, s_=sf, j_=hp: (lambda e: e.dma_start(
                out=ddg[j_][s_][:], in_=dscr_d[h_, s_])))(),
                 waits=ev_dscr + [prev] if sf == 0 else (), inc=f"ddl{hp}")
        P.op("sync", (lambda h_=h, j_=hp: (lambda e: e.dma_start(
            out=dfar[j_][:], in_=dscr_d[h_, 4])))(), inc=f"ddl{hp}")
        dtile_evt[h] = P.now(f"ddl{hp}")

    cur_pair = [0]
    chunk_slot = {}

    def stage1(cj):
        ji, h, qc, kc, first, last = chunks[cj]
        if h not in dtile_evt:
            emit_dtiles(h)
        if qc == 2 and first:
            emit_dtiles(h + 1)
        hb = 64 * (h % 2)
        hc = h // 2
        hp = h % 2
        if first:
            qp = qc % 2
            P.op("vector", (lambda p_=qp, hb_=hb: (lambda e: e.memset(
                qstg[p_][64 - hb_:128 - hb_, :], 0.0)))(),
                 waits=[qs_free[qp]], inc="v")
            P.op("vector", (lambda p_=qp, hb_=hb, hc_=hc, qc_=qc: (lambda e: e.tensor_copy(
                qstg[p_][hb_:hb_ + 64, :],
                QT[hb_:hb_ + 64, S * hc_ + 512 * qc_:S * hc_ + 512 * (qc_ + 1)])))(),
                 waits=[rope_evt[("q", hc)]], inc="v")
            qs_evt[ji] = P.now("v")
        pr = cur_pair[0]
        hf = len(pend_s)
        chunk_slot[cj] = (pr, hf, pair_no[0] % 4)
        P.op("tensor", (lambda p_=pr, f_=hf, kc_=kc, hc_=hc, qp_=qc % 2: (lambda e: e.matmul(
            ps[p_][:, 512 * f_:512 * (f_ + 1)],
            KT[0:128, S * hc_ + 128 * kc_:S * hc_ + 128 * (kc_ + 1)],
            qstg[qp_][:], start=True, stop=True)))(),
             waits=[rope_evt[("k", hc)], qs_evt[ji],
                    ps_free[pr] if hf == 0 else None], inc="t")
        pend_s.append((cj, P.now("t")))
        if last:
            qs_free[qc % 2] = P.now("t")
        if hf == 1 or last:
            flush_pair()

    def cidx_of(cjx):
        jx, hx, qx, kx, _f, _l = chunks[cjx]
        d = 4 * qx - kx
        return CP_M12 if kx >= 4 * qx else CP_FAR0 + hx * 12 + (d - 1)

    def dsel_of(cjx):
        jx, hx, qx, kx, _f, _l = chunks[cjx]
        return ddg[hx % 2][kx - 4 * qx] if kx >= 4 * qx else dfar[hx % 2]

    def flush_pair():
        if not pend_s:
            return
        items = list(pend_s)
        del pend_s[:]
        cur_pair[0] ^= 1
        pair_no[0] += 1
        cj0 = items[0][0]
        pr = chunk_slot[cj0][0]
        n = len(items)
        h = chunks[cj0][1]
        idxs = [cidx_of(c[0]) for c in items]
        evs = [c[1] for c in items]
        pbi = chunk_slot[cj0][2]
        if n == 2 and idxs[0] == idxs[1]:
            P.op("scalar", (lambda p_=pr, b_=pbi, ci=idxs[0]: (lambda e: e.activation(
                pbuf[b_][:], ps[p_][:], AF.Exp,
                bias=cpool[:, ci:ci + 1], scale=1.0)))(),
                 waits=evs + [ev_cpool, pb_free[pbi], dtile_evt[h]], inc="s")
        else:
            for (cjx, evx), ci in zip(items, idxs):
                f = chunk_slot[cjx][1]
                P.op("scalar", (lambda p_=pr, b_=pbi, f_=f, ci_=ci: (lambda e: e.activation(
                    pbuf[b_][:, 512 * f_:512 * (f_ + 1)],
                    ps[p_][:, 512 * f_:512 * (f_ + 1)], AF.Exp,
                    bias=cpool[:, ci_:ci_ + 1], scale=1.0)))(),
                     waits=[evx, ev_cpool,
                            pb_free[pbi] if cjx == cj0 else None,
                            dtile_evt[h]], inc="s")
        ps_free[pr] = P.now("s")
        ev_e = P.now("s")
        for (cjx, _e) in items:
            f = chunk_slot[cjx][1]
            dd = dsel_of(cjx)
            P.op("vector", (lambda b_=pbi, f_=f, d_=dd: (lambda e: e.tensor_mul(
                pbuf[b_][:, 512 * f_:512 * (f_ + 1)],
                pbuf[b_][:, 512 * f_:512 * (f_ + 1)], d_[:])))(),
                 waits=[ev_e], inc="v")
        for (cjx, _e) in items:
            s1evt[cjx] = P.now("v")
        dtile_lastuse[h] = P.now("v")

    P.op("vector", lambda e: e.memset(sqd[64:66, :], 1.0), inc="v")
    ev_sq65 = P.now("v")
    P.op("gpsimd", lambda e: e.affine_select(z128[:], onesF[:], [[0, 128]],
                                             OP.is_equal, 0.0, base=-64,
                                             channel_multiplier=1),
         waits=[ev_vconst], inc="g")
    ev_z128 = P.now("g")
    P.op("vector", lambda e: e.memset(Ost[64:128, :], 0.0), inc="v")
    ev_ost0 = P.now("v")


    s2count = [0]
    t1q = []
    t2q = []

    def stage2(cj):
        ji, h, qc, kc, first, last = chunks[cj]
        pr, hf, pbi = chunk_slot[cj]
        pob = ji % 2
        if first:
            drain_parity(pob)
        P.op("tensor", (lambda b_=pbi, f2=hf, kc_=kc, h_=h, pb_=pob, f_=first, l_=last: (lambda e: e.matmul(
            po[pb_][0:65, :],
            Vx[:, 1040 * kc_ + 65 * h_:1040 * kc_ + 65 * (h_ + 1)],
            pbuf[b_][:, 512 * f2:512 * (f2 + 1)], start=f_, stop=l_)))(),
             waits=[s1evt[cj], ev_Vx] + ([po_free[pob]] if first else []), inc="t")
        if chunks[cj][5] or hf == 1:
            pb_free[pbi] = P.now("t")
        s2count[0] += 1
        if last:
            ev_av = P.now("t")
            P.op("vector", (lambda p_=pob: (lambda e: e.tensor_copy(
                sqd[64:65, 512 * p_:512 * (p_ + 1)], po[p_][64:65, :])))(),
                 waits=[ev_av, ev_vconst, drow_evt[pob]], inc="v")
            t1q.append((s2count[0] + 2, ji, P.now("v")))

    def drain_parity(pob):
        for q, fn in ((t1q, _t1_emit), (t2q, _t2_emit)):
            keep = []
            for item in q:
                if item[1] % 2 == pob:
                    fn(item)
                else:
                    keep.append(item)
            q[:] = keep

    def _t1_emit(item):
        _, ji, ev_dc = item
        _emit_t1(ji, ev_dc)

    def _t2_emit(item):
        _, ji, ev_rr = item
        _emit_t2(ji, ev_rr)

    def _emit_t1(ji, ev_dc):
        pob = ji % 2
        P.op("scalar", (lambda p_=pob: (lambda e: e.activation(
            rrow[64:66, 512 * p_:512 * (p_ + 1)],
            sqd[64:66, 512 * p_:512 * (p_ + 1)],
            AF.Ln, bias=0.0, scale=1.0)))(),
             waits=[ev_dc, rrow_evt[pob]], inc="s")
        drow_evt[pob] = P.now("s")
        P.op("scalar", (lambda p_=pob: (lambda e: e.activation(
            rrow[64:66, 512 * p_:512 * (p_ + 1)],
            rrow[64:66, 512 * p_:512 * (p_ + 1)],
            AF.Exp, bias=0.0, scale=-1.0)))(), inc="s")
        t2q.append((s2count[0] + 4, ji, P.now("s")))

    def _emit_t2(ji, ev_rr):
        h, qc, kcs = jobs[ji]
        pob = ji % 2
        P.op("tensor", (lambda p_=pob: (lambda e: e.matmul(
            pc[:], z128[:],
            rrow[0:128, 512 * p_:512 * (p_ + 1)], start=True, stop=True)))(),
             waits=[ev_rr, ev_vconst, pc_evt[0], ev_z128], inc="t")
        ev_bc = P.now("t")
        rrow_evt[pob] = ev_bc
        P.op("vector", (lambda: (lambda e: e.tensor_copy(rbt[0][:], pc[:])))(),
             waits=[ev_bc, rb_evt[0]], inc="v")
        pc_evt[0] = P.now("v")
        P.op("vector", (lambda h_=h, q_=qc, p_=pob: (lambda e: e.tensor_mul(
            Ost[0:64, S * h_ + 512 * q_:S * h_ + 512 * (q_ + 1)],
            po[p_][0:64, :], rbt[0][0:64, :])))(), inc="v")
        ev = P.now("v")
        po_free[pob] = ev
        rb_evt[0] = ev
        ost_evt[(h, qc)] = ev

    def drain_tails(now, flush=False):
        while t1q and (flush or t1q[0][0] <= now):
            _, ji, ev_dc = t1q.pop(0)
            _emit_t1(ji, ev_dc)
        while t2q and (flush or t2q[0][0] <= now):
            _, ji, ev_rr = t2q.pop(0)
            _emit_t2(ji, ev_rr)

    drow_evt = [None, None]
    rrow_evt = [None, None]
    pc_evt = [None]
    rb_evt = [None]

    for j in range(NCK + LOOK):
        if j >= LOOK:
            stage2(j - LOOK)
            drain_tails(s2count[0])
        if j < NCK:
            stage1(j)
    drain_tails(0, flush=True)

    ev_attn = P.now("v")
    if phase_limit <= 4:
        dump_dbg(Ost[0:64, 0:16384], 16384, ev_attn, rows=64)
        return finish()

    p3p_es.close()
    phase_stacks.remove(p3p_es)
    es_qkv.close()
    phase_stacks.remove(es_qkv)
    P.barrier()
    p3b_es = ExitStack()
    phase_stacks.append(p3b_es)
    owt = p3b_es.enter_context(nc.sbuf_tensor("owt", [128, NH * 1024], BF16, side="right"))
    xrb = [p3b_es.enter_context(nc.sbuf_tensor(f"xrb{i}", [128, D], F32, side="right"))
           for i in range(2)]
    xnw = [p3b_es.enter_context(nc.sbuf_tensor(f"xnw{i}", [128, D], F32, side="right"))
           for i in range(2)]
    btm = [p3b_es.enter_context(nc.sbuf_tensor(f"btm{i}", [128, D], BF16, side="right"))
           for i in range(2)]
    btasm = [p3b_es.enter_context(nc.sbuf_tensor(f"btasm{i}", [128, D], BF16, side="right"))
             for i in range(2)]
    pat = [p3b_es.enter_context(nc.psum_tensor(f"pat{i}", [128, 512], F32)) for i in range(4)]
    pbt3A = p3b_es.enter_context(nc.psum_tensor("pbt3A", [128, 1024], BF16))
    pbt3B = p3b_es.enter_context(nc.psum_tensor("pbt3B", [128, 1024], BF16))

    P.op("vector", lambda e: e.memset(owt[64:128, :], 0.0), inc="v")
    ev_owt0 = P.now("v")
    for h in range(NH):
        P.op("gpsimd", (lambda h_=h: (lambda e: e.dma_start(
            out=owt[0:64, 1024 * h_:1024 * (h_ + 1)],
            in_=owT_d[64 * h_:64 * (h_ + 1), :])))(), inc="dow")
    ev_owt = P.now("dow")

    xrb_free = [None, None]
    xnw_free = [[], []]
    pat_free = [None] * 4
    btm_free = [None, None]
    btasm_free = [None, None]
    bst = {}
    bt_pend = [None]

    def emit_bt(sl, bi, ev_b):
        wts0 = [ev_b] + ([btasm_free[bi]] if btasm_free[bi] else [])
        ev_t, ev_bt = emit_transposes(
            btm[bi], NCH, (lambda b_=bi: (lambda i: btasm[b_][:, 128 * i:128 * (i + 1)]))(),
            wts0, (pbt3A, pbt3B), bst)
        btm_free[bi] = ev_t
        P.op("scalar", (lambda s_=sl, b_=bi: (lambda e: e.dma_start(
            out=btscr_d[s_], in_=btasm[b_][:].rearrange("p (a b) -> p a b", a=8))))(),
             waits=[ev_bt], inc=f"dbt{bi}")
        btasm_free[bi] = P.now(f"dbt{bi}")

    for sl in range(NSL):
        bi = sl % 2
        P.op("sync", (lambda s_=sl, b=bi: (lambda e: e.dma_start(
            out=xrb[b][:], in_=x_d[128 * s_:128 * (s_ + 1), :])))(),
             waits=[xrb_free[bi]], inc=f"dxr{bi}")
        ev_xr = P.now(f"dxr{bi}")
        qcsl = sl // 4
        for e2 in range(2):
            pi = (sl % 2) * 2 + e2
            wts = [ev_owt, ev_owt0, ev_ost0, ev_browb, pat_free[pi]] + [ost_evt[(h, qcsl)] for h in range(NH)]
            for h in range(NH):
                P.op("tensor", (lambda h_=h, e_=e2, s_=sl, p_=pi: (lambda e: e.matmul(
                    pat[p_][:],
                    Ost[0:128, S * h_ + 128 * s_:S * h_ + 128 * (s_ + 1)],
                    owt[0:128, 1024 * h_ + 512 * e_:1024 * h_ + 512 * (e_ + 1)],
                    start=(h_ == 0), stop=False)))(),
                     waits=wts if h == 0 else (), inc="t")
            P.op("tensor", (lambda e_=e2, p_=pi: (lambda e: e.matmul(
                pat[p_][:], onesB[0:1, 0:128], browb[0:1, 512 * e_:512 * (e_ + 1)],
                start=False, stop=True)))(), inc="t")
        if bt_pend[0] is not None:
            emit_bt(*bt_pend[0])
            bt_pend[0] = None
        ev_op = P.now("t")
        for e2 in range(2):
            pi = (sl % 2) * 2 + e2
            P.op("vector", (lambda b=bi, e_=e2, p_=pi: (lambda e: e.tensor_add(
                xnw[b][:, 512 * e_:512 * (e_ + 1)], pat[p_][:],
                xrb[b][:, 512 * e_:512 * (e_ + 1)])))(),
                 waits=[ev_op, ev_xr] + (xnw_free[bi] if e2 == 0 else []), inc="v")
            pat_free[pi] = P.now("v")
        ev_xn = P.now("v")
        xrb_free[bi] = ev_xn
        P.op("scalar", (lambda s_=sl, b=bi: (lambda e: e.dma_start(
            out=xres_d[128 * s_:128 * (s_ + 1), :], in_=xnw[b][:])))(),
             waits=[ev_xn], inc=f"dsp{bi}")
        ev_spill = P.now(f"dsp{bi}")
        ev_b = emit_ln(xnw[bi], btm[bi], [ev_xn, btm_free[bi]], 8 * bi)
        xnw_free[bi] = [ev_spill, ev_b]
        bt_pend[0] = (sl, bi, ev_b)
    if bt_pend[0] is not None:
        emit_bt(*bt_pend[0])
        bt_pend[0] = None

    ev_spills = [P.now("dsp0"), P.now("dsp1")]
    ev_btspill = [P.now("dbt0"), P.now("dbt1")]

    if phase_limit <= 5:
        return finish()

    p3b_es.close()
    phase_stacks.remove(p3b_es)
    p3_es.close()
    phase_stacks.remove(p3_es)
    P.barrier()
    p4_es = ExitStack()
    phase_stacks.append(p4_es)
    SB2 = 256
    NB2 = S // SB2
    Hb2 = [p4_es.enter_context(nc.sbuf_tensor(f"Hb2{i}", [128, 128 * SB2], BF16))
           for i in range(2)]
    btb = [p4_es.enter_context(nc.sbuf_tensor(f"btb{i}", [128, 8 * SB2], BF16))
           for i in range(2)]
    w2b = [p4_es.enter_context(nc.sbuf_tensor(f"w2b{i}", [128, 512], BF16)) for i in range(2)]
    wcs = [p4_es.enter_context(nc.sbuf_tensor(f"wcs{i}", [128, D], BF16)) for i in range(10)]
    xr2 = [p4_es.enter_context(nc.sbuf_tensor(f"xr2{i}", [128, D], F32)) for i in range(2)]
    ob2 = [p4_es.enter_context(nc.sbuf_tensor(f"ob2{i}", [128, D], F32)) for i in range(2)]
    pf = [p4_es.enter_context(nc.psum_tensor(f"pf{i}", [128, 512], F32)) for i in range(8)]

    wcs_free = [None] * 10
    w2b_free = [None, None]
    xr2_free = [None, None]
    ob2_free = [None, None]
    pf_free = [None] * 8
    btb_free = [None, None]
    hb2_free = [None, None]
    rails_done = {}
    cp_done = {-1: None}

    def load_btb(bk):
        if bk >= NB2:
            return
        p = bk % 2
        for oc in range(8):
            P.op("sync", (lambda o_=oc, b_=bk, p_=p: (lambda e: e.dma_start(
                out=btb[p_][:, SB2 * o_:SB2 * (o_ + 1)].rearrange("p (a b) -> p a b", a=2),
                in_=btscr_d[2 * b_:2 * (b_ + 1), :, o_, :].rearrange("a p b -> p a b"))))(),
                 waits=(ev_btspill + [btb_free[p]]) if oc == 0 else (), inc=f"dtb{p}")
        btb_free[p] = None
        return P.now(f"dtb{p}")

    btb_evt = {}
    btb_evt[0] = load_btb(0)

    def rails_unit(bk, u, standalone=False):
        p = bk % 2
        h4, rem = divmod(u, 32)
        oc, j = divmod(rem, 4)
        i = (u // 4) % 2
        if j == 0:
            P.op("sync", (lambda h_=h4, o_=oc, i_=i: (lambda e: e.dma_start(
                out=w2b[i_][:], in_=w2scr_d[h_, o_])))(),
                 waits=[ev_w2s[0], ev_w2s[1], w2b_free[i]], inc=f"dw2{i}")
        ev_wl = P.now(f"dw2{i}")
        zb = 4 + (u % 4)
        mwts = [ev_wl, btb_evt[bk], pf_free[zb]]
        P.op("tensor", (lambda i_=i, j_=j, o_=oc, z_=zb, p_=p: (lambda e: e.matmul(
            pf[z_][:, 0:SB2], w2b[i_][:, 128 * j_:128 * (j_ + 1)],
            btb[p_][:, SB2 * o_:SB2 * (o_ + 1)], start=True, stop=True)))(),
             waits=mwts, inc="t")
        ev_z = P.now("t")
        if j == 3:
            w2b_free[i] = ev_z
        if u == 127:
            btb_free[p] = ev_z
        mc = u
        gw = [ev_z]
        if u == 0:
            gw.append(hb2_free[p])
        P.op("scalar", (lambda m_=mc, z_=zb, p_=p: (lambda e: e.activation(
            Hb2[p_][:, SB2 * m_:SB2 * (m_ + 1)], pf[z_][:, 0:SB2], AF.Gelu,
            bias=0.0, scale=1.0)))(),
             waits=gw, inc="s")
        pf_free[zb] = P.now("s")
        if u == 127:
            rails_done[bk] = P.now("s")

    btb_evt[1] = load_btb(1)
    for u in range(128):
        rails_unit(0, u)

    for bk in range(NB2):
        p = bk % 2
        if bk + 2 < NB2:
            btb_evt[bk + 2] = load_btb(bk + 2)
        xr_evts = {}
        for sl2 in (1, 0):
            q4 = sl2
            sl = 2 * bk + sl2
            P.op("sync", (lambda s_=sl, q_=q4: (lambda e: e.dma_start(
                out=xr2[q_][:], in_=xres_d[128 * s_:128 * (s_ + 1), :])))(),
                 waits=[xr2_free[q4], ev_spills[0], ev_spills[1]], inc=f"dr2{q4}")
            xr_evts[sl2] = P.now(f"dr2{q4}")
        for mc in range(128):
            i = mc % 10
            P.op("sync", (lambda m_=mc, i_=i: (lambda e: e.dma_start(
                out=wcs[i_][:], in_=wcb_d[128 * m_:128 * (m_ + 1), :])))(),
                 waits=[wcs_free[i]] + (ev_wcb_done if (bk == 0 and mc < 10) else []),
                 inc=f"dwc{i}")
            ev_wc = P.now(f"dwc{i}")
            for sl2 in range(2):
                for e2 in range(2):
                    pb_i = sl2 * 2 + e2
                    mwts = [ev_wc, rails_done[bk]]
                    if mc == 0:
                        mwts.append(pf_free[pb_i])
                    P.op("tensor", (lambda m_=mc, s_=sl2, e_=e2, pi=pb_i, i_=i, p_=p: (lambda e: e.matmul(
                        pf[pi][:], Hb2[p_][:, SB2 * m_ + 128 * s_:SB2 * m_ + 128 * (s_ + 1)],
                        wcs[i_][:, 512 * e_:512 * (e_ + 1)],
                        start=(m_ == 0), stop=(m_ == 127))))(),
                         waits=mwts, inc="t")
            wcs_free[i] = P.now("t")
            if bk + 1 < NB2:
                rails_unit(bk + 1, mc)
        ev_cp = P.now("t")
        cp_done[bk] = ev_cp
        hb2_free[(bk + 2) % 2] = ev_cp
        for sl2 in (1, 0):
            sl = 2 * bk + sl2
            q4 = sl2
            ev_xr = xr_evts[sl2]
            for e2 in range(2):
                pb_i = sl2 * 2 + e2
                P.op("vector", (lambda q_=q4, e_=e2, pi=pb_i: (lambda e: e.tensor_add(
                    ob2[q_][:, 512 * e_:512 * (e_ + 1)], pf[pi][:],
                    xr2[q_][:, 512 * e_:512 * (e_ + 1)])))(),
                     waits=[ev_cp, ev_xr] + ([ob2_free[q4]] if e2 == 0 else []), inc="v")
                pf_free[pb_i] = P.now("v")
            ev_ot = P.now("v")
            xr2_free[q4] = ev_ot
            P.op("scalar", (lambda s_=sl, q_=q4: (lambda e: e.dma_start(
                out=out_d[128 * s_:128 * (s_ + 1), :], in_=ob2[q_][:])))(),
                 waits=[ev_ot], inc=f"dot{q4}")
            ob2_free[q4] = P.now(f"dot{q4}")

    return finish()



_NC_CACHE = {}


def _get_nc(phase_limit=99):
    if phase_limit not in _NC_CACHE:
        _NC_CACHE[phase_limit] = build_nc(phase_limit)
    return _NC_CACHE[phase_limit]


def kernel(L0_attn_x, L0_attn_wq, L0_attn_wk, L0_attn_wv, L0_attn_out_w,
           L0_attn_out_b, L0_ffn_rail_maker, L0_ffn_rail_up_proj, L0_ffn_w_cproj,
           _phase_limit=99, _cores=8):
    x = np.ascontiguousarray(np.asarray(L0_attn_x, dtype=np.float32))
    wq = np.asarray(L0_attn_wq, dtype=np.float32)
    wk = np.asarray(L0_attn_wk, dtype=np.float32)
    wv = np.asarray(L0_attn_wv, dtype=np.float32)
    ow = np.asarray(L0_attn_out_w, dtype=np.float32)
    ob = np.ascontiguousarray(np.asarray(L0_attn_out_b, dtype=np.float32).reshape(1, D))
    rm = np.asarray(L0_ffn_rail_maker, dtype=np.float32)
    ru = np.asarray(L0_ffn_rail_up_proj, dtype=np.float32)
    wc = np.asarray(L0_ffn_w_cproj, dtype=np.float32)

    wqT = np.ascontiguousarray(wq.T)
    wkT = np.ascontiguousarray(wk.T)
    wvT = np.ascontiguousarray(wv.T)
    owT = np.ascontiguousarray(ow.T)
    wcT = np.ascontiguousarray(wc.T)

    blkA = np.zeros((4, 8, 128, 128), dtype=np.float32)
    blkB = np.zeros((4, 8, 128, 512), dtype=np.float32)
    for h in range(4):
        for oc in range(8):
            for r in range(8):
                Mb = rm[h, 8 * oc + r]
                Ub = ru[h, 8 * oc + r]
                blkA[h, oc, 16 * r:16 * (r + 1), 16 * r:16 * (r + 1)] = Mb.T
                blkB[h, oc, 16 * r:16 * (r + 1), 64 * r:64 * (r + 1)] = Ub.T

    nc = _get_nc(_phase_limit)
    shared = dict(wqT=wqT, wkT=wkT, wvT=wvT, owT=owT, ob=ob,
                  blkA=blkA, blkB=blkB, wcT=wcT)
    in_maps = [dict(shared, x=np.ascontiguousarray(x[i])) for i in range(_cores)]
    res = run_bass_kernel_spmd(nc, in_maps, core_ids=list(range(_cores)))
    if _phase_limit < 99:
        return res
    return np.stack([np.asarray(r["out"]) for r in res.results], axis=0)


# version 17
# speedup vs baseline: 1.0425x; 1.0425x over previous
import math
from contextlib import ExitStack

import numpy as np

import concourse.bass as bass
import concourse.mybir as mybir
from concourse.bass_utils import run_bass_kernel_spmd

F32 = mybir.dt.float32
BF16 = mybir.dt.bfloat16
I32 = mybir.dt.int32
AF = mybir.ActivationFunctionType
OP = mybir.AluOpType
AX = mybir.AxisListType

S = 2048
D = 1024
NH = 16
NSL = S // 128
NCH = D // 128
MF = 16384
SBK = 512
NBK = S // SBK
EPS = 1e-5
MSUB = 12.0
TWO_PI = 6.283185307179586
HALF_PI = 1.5707963267948966
CLN = math.log(10000.0) / 512.0

ENGINES = ("sync", "gpsimd", "tensor", "vector", "scalar")


def slope(h):
    return 2.0 ** (-8.0 * (h + 1) / 16.0)


def kc_min_for(h, qc):
    lo = 0
    for d in range(1, 13):
        if -MSUB + slope(h) * (127.0 - 128.0 * d) <= -124.0:
            lo = max(lo, 4 * qc - d + 1)
    return max(0, lo)


class Prog:
    def __init__(self, nc, es):
        self.nc = nc
        self.es = es
        self.ops = {e: [] for e in ENGINES}
        self.sems = {}
        self.cnt = {}
        for name in ("t", "v", "s", "g"):
            self._mk(name)

    def _mk(self, name):
        self.sems[name] = self.es.enter_context(self.nc.semaphore(name))
        self.cnt[name] = 0

    ENG_SEM = {"vector": "v", "scalar": "s", "gpsimd": "g"}

    gate = {}

    def op(self, eng, fn, waits=(), inc=None):
        gw = self.gate.pop(eng, None) if self.gate else None
        if gw:
            waits = list(waits) + gw
        esem = self.ENG_SEM.get(eng)
        pre_self = self.cnt.get(esem, 0) if esem is not None else 0
        incs = []
        is_dma = False
        if inc is not None:
            for snm in ([inc] if isinstance(inc, str) else list(inc)):
                if snm not in self.sems:
                    self._mk(snm)
                n = 16 if snm.startswith("d") else 1
                if snm.startswith("d"):
                    is_dma = True
                self.cnt[snm] += n
                incs.append((snm, n))
        flat = []
        for w in waits:
            if w is None:
                continue
            if isinstance(w, list):
                flat.extend(x for x in w if x is not None)
            else:
                flat.append(w)
        if esem is not None:
            flat.append((esem, pre_self))
            if not is_dma and not any(s == esem for (s, _n) in incs):
                self.cnt[esem] += 1
                incs.append((esem, 1))
        flat = tuple((s, v) for (s, v) in flat if v > 0)
        self.ops[eng].append((fn, flat, incs))
        return (incs[0][0], self.cnt[incs[0][0]]) if incs else None

    def now(self, sem):
        if sem not in self.sems:
            self._mk(sem)
        return (sem, self.cnt[sem])

    def barrier(self):
        gate = [self.now(x) for x in list(self.sems.keys())]
        self.gate = {e: list(gate) for e in ENGINES}

    def build(self, blk):
        sems = self.sems
        for ename in ENGINES:
            eng_ops = self.ops[ename]
            if not eng_ops:
                continue

            def make(ops_):
                def body(eng):
                    lw = {}
                    for fn, waits, incs in ops_:
                        for (snm, v) in waits:
                            if lw.get(snm, 0) < v:
                                eng.wait_ge(sems[snm], v)
                                lw[snm] = v
                        ins = fn(eng)
                        for (snm, n) in incs:
                            ins = ins.then_inc(sems[snm], n)
                return body

            getattr(blk, ename)(make(eng_ops))


def build_nc(phase_limit=99):
    nc = bass.Bass("TRN2", target_bir_lowering=False)

    x_d = nc.dram_tensor("x", [S, D], F32, kind="ExternalInput").ap()
    wqT_d = nc.dram_tensor("wqT", [D, D], F32, kind="ExternalInput").ap()
    wkT_d = nc.dram_tensor("wkT", [D, D], F32, kind="ExternalInput").ap()
    wvT_d = nc.dram_tensor("wvT", [D, D], F32, kind="ExternalInput").ap()
    owT_d = nc.dram_tensor("owT", [D, D], F32, kind="ExternalInput").ap()
    ob_d = nc.dram_tensor("ob", [1, D], F32, kind="ExternalInput").ap()
    blkA_d = nc.dram_tensor("blkA", [4, 8, 128, 128], F32, kind="ExternalInput").ap()
    blkB_d = nc.dram_tensor("blkB", [4, 8, 128, 512], F32, kind="ExternalInput").ap()
    wcT_d = nc.dram_tensor("wcT", [MF, D], F32, kind="ExternalInput").ap()
    out_d = nc.dram_tensor("out", [S, D], F32, kind="ExternalOutput").ap()
    xres_d = nc.dram_tensor("xres", [S, D], F32, kind="Internal").ap()
    btscr_d = nc.dram_tensor("btscr", [NSL, 128, 8, 128], BF16, kind="Internal").ap()
    wcb_d = nc.dram_tensor("wcb", [MF, D], BF16, kind="Internal").ap()
    w2scr_d = nc.dram_tensor("w2scr", [4, 8, 128, 512], BF16, kind="Internal").ap()

    es = ExitStack()
    P = Prog(nc, es)

    cpool = es.enter_context(nc.sbuf_tensor("cpool", [128, 200], F32))
    ident = es.enter_context(nc.sbuf_tensor("ident", [128, 128], BF16))
    onesF = es.enter_context(nc.sbuf_tensor("onesF", [128, 128], F32))
    onesB = es.enter_context(nc.sbuf_tensor("onesB", [128, 128], BF16))
    xfar = es.enter_context(nc.sbuf_tensor("xfar", [128, 512], BF16))
    xdg = [es.enter_context(nc.sbuf_tensor(f"xdg{i}", [128, 512], BF16)) for i in range(4)]
    rrow = es.enter_context(nc.sbuf_tensor("rrow", [128, 1024], F32))
    sqd = es.enter_context(nc.sbuf_tensor("sqd", [128, 1024], F32))
    lnst = es.enter_context(nc.sbuf_tensor("lnst", [128, 16], F32))
    browb = es.enter_context(nc.sbuf_tensor("browb", [1, D], BF16))

    CP_FAR0 = 0
    CP_M12 = 192
    CP_EPS = 193
    CP_FRQ0 = 194
    cpool_vals = {}
    for h in range(NH):
        for d in range(1, 13):
            cpool_vals[CP_FAR0 + h * 12 + (d - 1)] = -MSUB + slope(h) * (127.0 - 128.0 * d)
    cpool_vals[CP_M12] = -MSUB
    cpool_vals[CP_EPS] = EPS
    for c in range(4):
        cpool_vals[CP_FRQ0 + c] = -128.0 * c * CLN

    items = sorted(cpool_vals.items())
    for j, (idx, val) in enumerate(items):
        P.op("vector", (lambda i=idx, vv=val: (lambda e: e.memset(cpool[:, i:i + 1], vv)))(),
             inc="v" if j == len(items) - 1 else None)
    ev_cpool = P.now("v")
    P.op("vector", lambda e: e.memset(onesF[:], 1.0))
    P.op("vector", lambda e: e.memset(onesB[:], 1.0))
    P.op("vector", lambda e: e.memset(sqd[:], 1.0))
    P.op("vector", lambda e: e.memset(rrow[:], 1.0), inc="v")
    ev_vconst = P.now("v")

    P.op("gpsimd", lambda e: e.affine_select(ident[:], onesB[:], [[-1, 128]],
                                             OP.is_equal, 0.0, base=0,
                                             channel_multiplier=1),
         waits=[ev_vconst], inc="g")
    P.op("gpsimd", lambda e: e.iota(xfar[:], [[-1, 512]], base=-127,
                                    channel_multiplier=1,
                                    allow_small_or_imprecise_dtypes=True), inc="g")
    for sf in range(4):
        P.op("gpsimd", (lambda s_=sf: (lambda e: e.iota(
            xdg[s_][:], [[-1, 512]], base=128 * s_, channel_multiplier=1,
            allow_small_or_imprecise_dtypes=True)))(), inc="g")
    ev_giota = P.now("g")
    for sf in range(4):
        P.op("gpsimd", (lambda s_=sf: (lambda e: e.affine_select(
            xdg[s_][:], xdg[s_][:], [[1, 512]], OP.is_ge, -1.0e5,
            base=-128 * s_, channel_multiplier=-1)))(),
             waits=[ev_giota], inc="g")
    ev_gconst = P.now("g")

    P.op("gpsimd", lambda e: e.dma_start(out=browb[:], in_=ob_d), inc="dbr")
    ev_browb = P.now("dbr")

    dbg_d = None
    if phase_limit < 99:
        dbg_d = nc.dram_tensor("dbg", [128, 16384], F32, kind="ExternalOutput").ap()

    def dump_dbg(src_ap, cols, prior_ev, rows=128):
        if src_ap.dtype == F32:
            P.op("scalar", lambda e: e.dma_start(out=dbg_d[0:rows, 0:cols], in_=src_ap),
                 waits=[prior_ev], inc="ddb")
        else:
            n = 0
            last_st = None
            while n < cols:
                wdt = min(1024, cols - n)
                P.op("vector", (lambda nn=n, ww=wdt, sp=src_ap, r_=rows: (lambda e: e.tensor_copy(
                    sqd[0:r_, 0:ww], sp[:, nn:nn + ww])))(), waits=[prior_ev, last_st], inc="v")
                evc = P.now("v")
                P.op("scalar", (lambda nn=n, ww=wdt, r_=rows: (lambda e: e.dma_start(
                    out=dbg_d[0:r_, nn:nn + ww], in_=sqd[0:r_, 0:ww])))(),
                     waits=[evc], inc="ddb")
                last_st = P.now("ddb")
                n += wdt

    phase_stacks = []

    def finish():
        for snm in list(P.sems.keys()):
            if snm.startswith("d"):
                tot = P.cnt[snm]
                if tot > 0:
                    P.op("sync", (lambda s_=snm, t_=tot: (lambda e: e.wait_ge(
                        P.sems[s_], t_)))())
        for st in reversed(phase_stacks):
            st.close()
        blk = es.enter_context(nc.Block())
        P.build(blk)
        es.close()
        return nc

    def emit_ln(xtile, otile, waits0, col0):
        c = col0
        P.op("vector", lambda e: e.reduce_sum(lnst[:, c + 0:c + 1], xtile[:], axis=AX.X),
             waits=list(waits0), inc="v")
        ev1 = P.now("v")
        P.op("scalar", lambda e: e.activation(lnst[:, c + 1:c + 2], lnst[:, c + 0:c + 1],
                                              AF.Copy, bias=0.0, scale=-1.0 / D),
             waits=[ev1], inc="s")
        P.op("scalar", lambda e: e.activation(sqd[:], xtile[:], AF.Square,
                                              bias=lnst[:, c + 1:c + 2], scale=1.0,
                                              accum_out=lnst[:, c + 2:c + 3]), inc="s")
        P.op("scalar", lambda e: e.activation(lnst[:, c + 3:c + 4], lnst[:, c + 2:c + 3],
                                              AF.Ln, bias=cpool[:, CP_EPS:CP_EPS + 1],
                                              scale=1.0 / D),
             waits=[ev_cpool], inc="s")
        P.op("scalar", lambda e: e.activation(lnst[:, c + 4:c + 5], lnst[:, c + 3:c + 4],
                                              AF.Exp, bias=0.0, scale=-0.5), inc="s")
        ev5 = P.now("s")
        P.op("vector", lambda e: e.tensor_mul(lnst[:, c + 5:c + 6], lnst[:, c + 1:c + 2],
                                              lnst[:, c + 4:c + 5]),
             waits=[ev5], inc="v")
        ev6 = P.now("v")
        P.op("scalar", lambda e: e.activation(otile[:], xtile[:], AF.Identity,
                                              bias=lnst[:, c + 5:c + 6],
                                              scale=lnst[:, c + 4:c + 5]),
             waits=[ev6], inc="s")
        return P.now("s")

    def emit_transposes(src_tile, n_slices, dst_fn, waits0, pair, state):
        evs = state.setdefault("evs", [])
        for i in range(n_slices):
            k = state.get("k", 0)
            bank = pair[k % 2]
            slot = (k // 2) % 8
            wts = list(waits0) + [ev_gconst]
            if len(evs) >= 2:
                wts.append(evs[-2])
            if len(evs) >= 16:
                wts.append(evs[-16])
            P.op("tensor", (lambda b_=bank, sl_=slot, i_=i, sp=src_tile: (lambda e: e.transpose(
                b_[:, 128 * sl_:128 * (sl_ + 1)],
                sp[:, 128 * i_:128 * (i_ + 1)], ident[:])))(),
                 waits=wts, inc="t")
            ev_tr = P.now("t")
            dst = dst_fn(i)
            P.op("vector", (lambda b_=bank, sl_=slot, d_=dst: (lambda e: e.tensor_copy(
                d_, b_[:, 128 * sl_:128 * (sl_ + 1)])))(),
                 waits=[ev_tr], inc="v")
            evs.append(P.now("v"))
            state["k"] = k + 1
        return P.now("t"), P.now("v")

    es_at = ExitStack()
    phase_stacks.append(es_at)
    At = es_at.enter_context(nc.sbuf_tensor("At", [128, S * NCH], BF16, side="right"))
    p1_es = ExitStack()
    phase_stacks.append(p1_es)
    xin = [p1_es.enter_context(nc.sbuf_tensor(f"xin{i}", [128, D], F32, side="right"))
           for i in range(2)]
    atm = [p1_es.enter_context(nc.sbuf_tensor(f"atm{i}", [128, D], BF16, side="right"))
           for i in range(2)]
    pbtA = p1_es.enter_context(nc.psum_tensor("pbtA", [128, 1024], BF16))
    pbtB = p1_es.enter_context(nc.psum_tensor("pbtB", [128, 1024], BF16))

    xin_free = [None, None]
    atm_free = [None, None]
    tst = {}
    for sl in range(NSL):
        bi = sl % 2
        P.op("sync", (lambda s_=sl, b=bi: (lambda e: e.dma_start(
            out=xin[b][:], in_=x_d[128 * s_:128 * (s_ + 1), :])))(),
             waits=[xin_free[bi]], inc=f"dx{bi}")
        ev_x = P.now(f"dx{bi}")
        ev_a = emit_ln(xin[bi], atm[bi], [ev_x, atm_free[bi]], 8 * bi)
        xin_free[bi] = ev_a
        ev_t, ev_v = emit_transposes(
            atm[bi], NCH,
            (lambda s_=sl: (lambda i: At[:, S * i + 128 * s_: S * i + 128 * (s_ + 1)]))(),
            [ev_a], (pbtA, pbtB), tst)
        atm_free[bi] = ev_t
    ev_At = P.now("v")

    if phase_limit <= 1:
        dump_dbg(At[:, 0:16384], 16384, ev_At)
        return finish()

    p1_es.close()
    phase_stacks.remove(p1_es)
    P.barrier()
    es_qkv = ExitStack()
    phase_stacks.append(es_qkv)
    QT = es_qkv.enter_context(nc.sbuf_tensor("QT", [128, S * NCH], BF16))
    KT = es_qkv.enter_context(nc.sbuf_tensor("KT", [128, S * NCH], BF16))
    Vx = es_qkv.enter_context(nc.sbuf_tensor("Vx", [128, NSL * 1040], BF16))

    p2_es = ExitStack()
    phase_stacks.append(p2_es)
    iotaS = p2_es.enter_context(nc.sbuf_tensor("iotaS", [128, S], F32, side="right"))
    scrW = p2_es.enter_context(nc.sbuf_tensor("scrW", [128, S], F32, side="right"))
    scrI = p2_es.enter_context(nc.sbuf_tensor("scrI", [128, S // 4], I32, side="right"))
    scrB = p2_es.enter_context(nc.sbuf_tensor("scrB", [128, S // 4], F32, side="right"))
    sintab = p2_es.enter_context(nc.sbuf_tensor("sintab", [128, S], BF16, side="right"))
    costab = p2_es.enter_context(nc.sbuf_tensor("costab", [128, S], BF16, side="right"))
    rtmp = [p2_es.enter_context(nc.sbuf_tensor(f"rtmp{i}", [128, S], BF16, side="right"))
            for i in range(2)]
    freqs = p2_es.enter_context(nc.sbuf_tensor("freqs", [128, 8], F32, side="right"))
    wqb = [p2_es.enter_context(nc.sbuf_tensor(f"wqb{i}", [128, 2048], BF16, side="right"))
           for i in range(2)]
    vts = [p2_es.enter_context(nc.sbuf_tensor(f"vts{i}", [128, S], BF16, side="right"))
           for i in range(2)]
    pqk = [p2_es.enter_context(nc.psum_tensor(f"pqk{i}", [128, 512], F32)) for i in range(4)]
    pbt2A = p2_es.enter_context(nc.psum_tensor("pbt2A", [128, 1024], BF16))
    pbt2B = p2_es.enter_context(nc.psum_tensor("pbt2B", [128, 1024], BF16))

    qk_evt = {}
    wqb_free = [None, None]
    bank_free = [None] * 4
    for (pname, wd, dst, scl) in (("q", wqT_d, QT, 0.125), ("k", wkT_d, KT, 1.0)):
        for qtr in range(4):
            wi = qtr % 2
            for e8 in range(8):
                P.op("gpsimd", (lambda wd_=wd, q_=qtr, e_=e8, i_=wi: (lambda e: e.dma_start(
                    out=wqb[i_][:, 256 * e_:256 * (e_ + 1)],
                    in_=wd_[128 * e_:128 * (e_ + 1), 256 * q_:256 * (q_ + 1)])))(),
                     waits=[wqb_free[wi]] if e8 == 0 else (), inc=f"dw{wi}")
            ev_w = P.now(f"dw{wi}")
            last_mm = None
            for d2 in range(2):
                ch = 2 * qtr + d2
                for e8 in range(8):
                    for sc in range(4):
                        wts = [ev_w, ev_At]
                        if e8 == 0:
                            wts.append(bank_free[sc])
                        P.op("tensor", (lambda i_=wi, e_=e8, d_=d2, s_=sc: (lambda e: e.matmul(
                            pqk[s_][:],
                            wqb[i_][:, 256 * e_ + 128 * d_:256 * e_ + 128 * (d_ + 1)],
                            At[:, S * e_ + 512 * s_:S * e_ + 512 * (s_ + 1)],
                            start=(e_ == 0), stop=(e_ == 7))))(),
                             waits=wts if e8 == 0 else (), inc="t")
                ev_mm = P.now("t")
                last_mm = ev_mm
                for sc in range(4):
                    P.op("scalar", (lambda s_=sc, ch_=ch, dst_=dst, sc_=scl: (lambda e: e.activation(
                        dst_[:, S * ch_ + 512 * s_:S * ch_ + 512 * (s_ + 1)],
                        pqk[s_][:], AF.Copy, bias=0.0, scale=sc_)))(),
                         waits=[ev_mm], inc="s")
                    bank_free[sc] = P.now("s")
                qk_evt[(pname, ch)] = P.now("s")
            wqb_free[wi] = last_mm

    P.op("gpsimd", lambda e: e.iota(iotaS[:], [[1, S]], base=0, channel_multiplier=0,
                                    allow_small_or_imprecise_dtypes=True), inc="g")
    ev_iotaS = P.now("g")
    P.op("gpsimd", lambda e: e.iota(freqs[:, 0:1], [[0, 1]], base=0, channel_multiplier=1,
                                    allow_small_or_imprecise_dtypes=True), inc="g")
    ev_iotaP = P.now("g")
    for c in range(4):
        P.op("scalar", (lambda c_=c: (lambda e: e.activation(
            freqs[:, 4 + c_:5 + c_], freqs[:, 0:1], AF.Exp,
            bias=cpool[:, CP_FRQ0 + c_:CP_FRQ0 + c_ + 1], scale=-CLN)))(),
             waits=[ev_iotaP, ev_cpool], inc="s")
    ev_freq = P.now("s")
    P.op("vector", lambda e: e.tensor_scalar_mul(freqs[:, 0:4], freqs[:, 4:8],
                                                 1.0 / TWO_PI),
         waits=[ev_freq], inc="v")
    ev_freq2 = P.now("v")

    def emit_tab(tab, c, add_quarter, waits0):
        if add_quarter:
            P.op("vector", (lambda c_=c: (lambda e: e.tensor_scalar(
                scrW[:], iotaS[:], freqs[:, c_:c_ + 1], 0.25, OP.mult, OP.add)))(),
                 waits=list(waits0), inc="v")
        else:
            P.op("vector", (lambda c_=c: (lambda e: e.tensor_scalar_mul(
                scrW[:], iotaS[:], freqs[:, c_:c_ + 1])))(),
                 waits=list(waits0), inc="v")
        for hf in range(4):
            P.op("vector", (lambda h_=hf: (lambda e: e.tensor_copy(
                scrI[:], scrW[:, 512 * h_:512 * (h_ + 1)])))(), inc="v")
            P.op("vector", lambda e: e.tensor_copy(scrB[:], scrI[:]), inc="v")
            P.op("vector", (lambda h_=hf: (lambda e: e.tensor_tensor(
                scrW[:, 512 * h_:512 * (h_ + 1)],
                scrW[:, 512 * h_:512 * (h_ + 1)], scrB[:], OP.subtract)))(), inc="v")
        P.op("vector", lambda e: e.scalar_tensor_tensor(
            scrW[:], scrW[:], 0.5, scrW[:], OP.is_ge, OP.subtract), inc="v")
        ev_w = P.now("v")
        P.op("scalar", (lambda t_=tab: (lambda e: e.activation(
            t_[:], scrW[:], AF.Sin, bias=0.0, scale=-TWO_PI)))(),
             waits=[ev_w, prev_tab_reader[0]], inc="s")
        return P.now("s")

    rope_evt = {}
    prev_tab_reader = [None]
    prev_act = [None]
    for c in range(4):
        prev_act[0] = emit_tab(sintab, c, False, [ev_iotaS, ev_freq2, prev_act[0]])
        ev_cos = emit_tab(costab, c, True, [ev_iotaS, ev_freq2, prev_act[0]])
        prev_act[0] = ev_cos
        for (pname, T) in (("q", QT), ("k", KT)):
            lo = lambda T_=T, o_=S * c: T_[:, o_:o_ + S]
            hi = lambda T_=T, o_=S * (c + 4): T_[:, o_:o_ + S]
            P.op("vector", (lambda hi_=hi(): (lambda e: e.tensor_mul(
                rtmp[0][:], hi_, sintab[:])))(),
                 waits=[ev_cos, qk_evt[(pname, c)], qk_evt[(pname, c + 4)]], inc="v")
            P.op("vector", (lambda hi_=hi(): (lambda e: e.tensor_mul(
                rtmp[1][:], hi_, costab[:])))(), inc="v")
            P.op("vector", (lambda lo_=lo(), hi_=hi(): (lambda e: e.tensor_mul(
                hi_, lo_, sintab[:])))(), inc="v")
            P.op("vector", (lambda hi_=hi(): (lambda e: e.tensor_add(
                hi_, hi_, rtmp[1][:])))(), inc="v")
            P.op("vector", (lambda lo_=lo(): (lambda e: e.tensor_mul(
                lo_, lo_, costab[:])))(), inc="v")
            P.op("vector", (lambda lo_=lo(): (lambda e: e.tensor_sub(
                lo_, lo_, rtmp[0][:])))(), inc="v")
            rope_evt[(pname, c)] = rope_evt[(pname, c + 4)] = P.now("v")
        prev_tab_reader[0] = P.now("v")

    if phase_limit <= 2:
        dump_dbg(QT[:, 0:16384], 16384, P.now("v"))
        return finish()

    for sl in range(NSL):
        P.op("vector", (lambda s_=sl: (lambda e: e.memset(
            Vx[:, 1040 * s_:1040 * (s_ + 1)].rearrange(
                "p (a b) -> p a b", a=16)[:, :, 64:65], 1.0)))(),
             inc="v" if sl == NSL - 1 else None)
    ones_evt = P.now("v")

    vst = {}
    vts_free = [None, None]
    for vq in range(4):
        wi = vq % 2
        for e8 in range(8):
            P.op("gpsimd", (lambda q_=vq, e_=e8, i_=wi: (lambda e: e.dma_start(
                out=wqb[i_][:, 256 * e_:256 * (e_ + 1)],
                in_=wvT_d[128 * e_:128 * (e_ + 1), 256 * q_:256 * (q_ + 1)])))(),
                 waits=[wqb_free[wi]] if e8 == 0 else (), inc=f"dw{wi}")
        ev_w = P.now(f"dw{wi}")
        last_mm = None
        for d2 in range(2):
            dvch = 2 * vq + d2
            vb = dvch % 2
            for e8 in range(8):
                for sc in range(4):
                    wts = [ev_w, ev_At]
                    if e8 == 0:
                        wts.append(bank_free[sc])
                    P.op("tensor", (lambda i_=wi, e_=e8, d_=d2, s_=sc: (lambda e: e.matmul(
                        pqk[s_][:],
                        wqb[i_][:, 256 * e_ + 128 * d_:256 * e_ + 128 * (d_ + 1)],
                        At[:, S * e_ + 512 * s_:S * e_ + 512 * (s_ + 1)],
                        start=(e_ == 0), stop=(e_ == 7))))(),
                         waits=wts if e8 == 0 else (), inc="t")
            ev_mm = P.now("t")
            last_mm = ev_mm
            for sc in range(4):
                P.op("scalar", (lambda s_=sc, v_=vb: (lambda e: e.activation(
                    vts[v_][:, 512 * s_:512 * (s_ + 1)], pqk[s_][:], AF.Copy,
                    bias=0.0, scale=1.0)))(),
                     waits=[ev_mm, vts_free[vb]], inc="s")
                bank_free[sc] = P.now("s")
            ev_vts = P.now("s")
            h0 = 2 * dvch
            emit_transposes(
                vts[vb], NSL,
                (lambda h_=h0: (lambda i: Vx[:, 1040 * i + 65 * h_: 1040 * i + 65 * h_ + 130].rearrange(
                    "p (a b) -> p a b", a=2)[:, :, 0:64]))(),
                [ev_vts, ones_evt], (pbt2A, pbt2B), vst)
            vts_free[vb] = P.now("t")
        wqb_free[wi] = last_mm
    ev_Vx = P.now("v")

    if phase_limit <= 3:
        dump_dbg(Vx[:, 0:16384], 16384, ev_Vx)
        return finish()

    p2_es.close()
    phase_stacks.remove(p2_es)
    es_at.close()
    phase_stacks.remove(es_at)
    P.barrier()
    p3_es = ExitStack()
    phase_stacks.append(p3_es)
    p3p_es = ExitStack()
    phase_stacks.append(p3p_es)
    Ost = p3_es.enter_context(nc.sbuf_tensor("Ost", [64, NH * S], BF16, side="right"))
    dfar = [p3_es.enter_context(nc.sbuf_tensor(f"dfar{j}", [128, 512], BF16, side="right"))
            for j in range(2)]
    ddg = [[p3_es.enter_context(nc.sbuf_tensor(f"ddg{j}_{i}", [128, 512], BF16, side="right"))
            for i in range(4)] for j in range(2)]
    ebuf = [p3_es.enter_context(nc.sbuf_tensor(f"ebuf{i}", [128, 1024], BF16, side="right"))
            for i in range(2)]
    qstg = [p3_es.enter_context(nc.sbuf_tensor(f"qstg{i}", [128, 512], BF16, side="right"))
            for i in range(2)]
    pbuf = [p3_es.enter_context(nc.sbuf_tensor(f"pbuf{i}", [128, 1024], BF16, side="right"))
            for i in range(4)]
    rbt = [p3_es.enter_context(nc.sbuf_tensor(f"rbt{i}", [128, 512], BF16, side="right"))
           for i in range(1)]
    ps = [p3p_es.enter_context(nc.psum_tensor(f"ps{i}", [128, 1024], F32)) for i in range(2)]
    po = [p3p_es.enter_context(nc.psum_tensor(f"po{i}", [128, 512], F32)) for i in range(2)]
    pc = p3p_es.enter_context(nc.psum_tensor("pc", [128, 512], F32))

    ba = [p3_es.enter_context(nc.sbuf_tensor(f"ba{i}", [128, 128], BF16, side="right"))
          for i in range(2)]
    bbuf = [p3_es.enter_context(nc.sbuf_tensor(f"bbuf{i}", [128, 512], BF16, side="right"))
            for i in range(2)]
    w2asm = [p3_es.enter_context(nc.sbuf_tensor(f"w2asm{i}", [128, 512], BF16, side="right"))
             for i in range(2)]
    wcv = [p3_es.enter_context(nc.sbuf_tensor(f"wcv{i}", [128, D // 4], BF16, side="right"))
           for i in range(2)]
    pw2 = p3p_es.enter_context(nc.psum_tensor("pw2", [128, 512], F32))

    ba_free = [None, None]
    bb_free = [None, None]
    w2_free = [None, None]
    pw2_free = None
    for h4 in range(4):
        for oc in range(8):
            i = (h4 * 8 + oc) % 2
            P.op("gpsimd", (lambda h_=h4, o_=oc, i_=i: (lambda e: e.dma_start(
                out=ba[i_][:], in_=blkA_d[h_, o_])))(),
                 waits=[ba_free[i]], inc=f"dba{i}")
            ev_lda = P.now(f"dba{i}")
            P.op("gpsimd", (lambda h_=h4, o_=oc, i_=i: (lambda e: e.dma_start(
                out=bbuf[i_][:], in_=blkB_d[h_, o_])))(),
                 waits=[bb_free[i]], inc=f"dbb{i}")
            ev_ldb = P.now(f"dbb{i}")
            P.op("tensor", (lambda i_=i: (lambda e: e.matmul(
                pw2[:], ba[i_][:], bbuf[i_][:], start=True, stop=True)))(),
                 waits=[ev_lda, ev_ldb, pw2_free], inc="t")
            ev_mm = P.now("t")
            ba_free[i] = ev_mm
            bb_free[i] = ev_mm
            P.op("vector", (lambda i_=i: (lambda e: e.tensor_copy(w2asm[i_][:], pw2[:])))(),
                 waits=[ev_mm, w2_free[i]], inc="v")
            pw2_free = P.now("v")
            P.op("scalar", (lambda h_=h4, o_=oc, i_=i: (lambda e: e.dma_start(
                out=w2scr_d[h_, o_], in_=w2asm[i_][:])))(),
                 waits=[pw2_free], inc=f"dws{i}")
            w2_free[i] = P.now(f"dws{i}")
    ev_w2s = [P.now("dws0"), P.now("dws1")]

    wcv_ld = [None, None]
    wcv_st = [None, None]
    for mh in range(512):
        mc, hh = divmod(mh, 4)
        i = mh % 2
        P.op("gpsimd", (lambda m_=mc, h_=hh, i_=i: (lambda e: e.dma_start(
            out=wcv[i_][:], in_=wcT_d[128 * m_:128 * (m_ + 1), 256 * h_:256 * (h_ + 1)])))(),
             waits=[wcv_st[i]], inc=f"dcv{i}")
        wcv_ld[i] = P.now(f"dcv{i}")
        P.op("sync", (lambda m_=mc, h_=hh, i_=i: (lambda e: e.dma_start(
            out=wcb_d[128 * m_:128 * (m_ + 1), 256 * h_:256 * (h_ + 1)], in_=wcv[i_][:])))(),
             waits=[wcv_ld[i]], inc=f"dcb{i}")
        wcv_st[i] = P.now(f"dcb{i}")
    ev_wcb_done = [P.now("dcb0"), P.now("dcb1")]

    ps_free = [None] * 2
    eb_free = [None] * 2
    pb_free = [None] * 4
    qs_free = [None, None]
    qs_evt = {}
    pend_s = []
    pair_no = [0]
    po_free = [None, None]
    pc_free = None
    rb_free = None
    drow_free = None
    rrow_free = None
    ost_evt = {}

    LOOK = 6
    jobs = []
    for h in range(NH):
        for qc in range(4):
            kmin = kc_min_for(h, qc)
            diag = [4 * qc + i for i in range(4)]
            far = [k for k in range(kmin, 4 * qc)]
            jobs.append((h, qc, diag + far))
    chunks = []
    for ji, (h, qc, kcs) in enumerate(jobs):
        for i, kc in enumerate(kcs):
            chunks.append((ji, h, qc, kc, i == 0, i == len(kcs) - 1))
    NCK = len(chunks)

    dtile_evt = {}
    dtile_lastuse = {}
    s1evt = [None] * NCK
    tails1 = []
    ev_avlast = {}

    def emit_dtiles(h):
        if h >= NH or h in dtile_evt:
            return
        hp = h % 2
        sh = slope(h)
        prev = dtile_lastuse.get(h - 2)
        P.op("scalar", (lambda sh_=sh, j_=hp: (lambda e: e.activation(
            dfar[j_][:], xfar[:], AF.Exp, bias=0.0, scale=sh_)))(),
             waits=[ev_gconst, prev], inc="s")
        for sf in range(4):
            P.op("scalar", (lambda sh_=sh, s_=sf, j_=hp: (lambda e: e.activation(
                ddg[j_][s_][:], xdg[s_][:], AF.Exp, bias=0.0, scale=sh_)))(), inc="s")
        dtile_evt[h] = P.now("s")

    cur_pair = [0]
    chunk_slot = {}

    def stage1(cj):
        ji, h, qc, kc, first, last = chunks[cj]
        if h not in dtile_evt:
            emit_dtiles(h)
        if qc == 2 and first:
            emit_dtiles(h + 1)
        hb = 64 * (h % 2)
        hc = h // 2
        if first:
            qp = qc % 2
            P.op("vector", (lambda p_=qp, hb_=hb: (lambda e: e.memset(
                qstg[p_][64 - hb_:128 - hb_, :], 0.0)))(),
                 waits=[qs_free[qp]], inc="v")
            P.op("vector", (lambda p_=qp, hb_=hb, hc_=hc, qc_=qc: (lambda e: e.tensor_copy(
                qstg[p_][hb_:hb_ + 64, :],
                QT[hb_:hb_ + 64, S * hc_ + 512 * qc_:S * hc_ + 512 * (qc_ + 1)])))(),
                 waits=[rope_evt[("q", hc)]], inc="v")
            qs_evt[ji] = P.now("v")
        pr = cur_pair[0]
        hf = len(pend_s)
        chunk_slot[cj] = (pr, hf, pair_no[0] % 4)
        P.op("tensor", (lambda p_=pr, f_=hf, kc_=kc, hc_=hc, qp_=qc % 2: (lambda e: e.matmul(
            ps[p_][:, 512 * f_:512 * (f_ + 1)],
            KT[0:128, S * hc_ + 128 * kc_:S * hc_ + 128 * (kc_ + 1)],
            qstg[qp_][:], start=True, stop=True)))(),
             waits=[rope_evt[("k", hc)], qs_evt[ji],
                    ps_free[pr] if hf == 0 else None], inc="t")
        pend_s.append((cj, P.now("t")))
        if last:
            qs_free[qc % 2] = P.now("t")
        if hf == 1 or last:
            flush_pair()

    def cidx_of(cjx):
        jx, hx, qx, kx, _f, _l = chunks[cjx]
        d = 4 * qx - kx
        return CP_M12 if kx >= 4 * qx else CP_FAR0 + hx * 12 + (d - 1)

    def dsel_of(cjx):
        jx, hx, qx, kx, _f, _l = chunks[cjx]
        return ddg[hx % 2][kx - 4 * qx] if kx >= 4 * qx else dfar[hx % 2]

    def flush_pair():
        if not pend_s:
            return
        items = list(pend_s)
        del pend_s[:]
        cur_pair[0] ^= 1
        pair_no[0] += 1
        cj0 = items[0][0]
        pr = chunk_slot[cj0][0]
        n = len(items)
        h = chunks[cj0][1]
        idxs = [cidx_of(c[0]) for c in items]
        evs = [c[1] for c in items]
        if n == 2 and idxs[0] == idxs[1]:
            P.op("scalar", (lambda p_=pr, ci=idxs[0]: (lambda e: e.activation(
                ebuf[p_][:], ps[p_][:], AF.Exp,
                bias=cpool[:, ci:ci + 1], scale=1.0)))(),
                 waits=evs + [ev_cpool, eb_free[pr], dtile_evt[h]], inc="s")
        else:
            for (cjx, evx), ci in zip(items, idxs):
                f = chunk_slot[cjx][1]
                P.op("scalar", (lambda p_=pr, f_=f, ci_=ci: (lambda e: e.activation(
                    ebuf[p_][:, 512 * f_:512 * (f_ + 1)],
                    ps[p_][:, 512 * f_:512 * (f_ + 1)], AF.Exp,
                    bias=cpool[:, ci_:ci_ + 1], scale=1.0)))(),
                     waits=[evx, ev_cpool,
                            eb_free[pr] if cjx == cj0 else None,
                            dtile_evt[h]], inc="s")
        ps_free[pr] = P.now("s")
        ev_e = P.now("s")
        pbi = chunk_slot[cj0][2]
        for (cjx, _e) in items:
            f = chunk_slot[cjx][1]
            dd = dsel_of(cjx)
            P.op("vector", (lambda p_=pr, b_=pbi, f_=f, d_=dd: (lambda e: e.tensor_mul(
                pbuf[b_][:, 512 * f_:512 * (f_ + 1)],
                ebuf[p_][:, 512 * f_:512 * (f_ + 1)], d_[:])))(),
                 waits=[ev_e] + ([pb_free[pbi]] if cjx == cj0 else []), inc="v")
        eb_free[pr] = P.now("v")
        for (cjx, _e) in items:
            s1evt[cjx] = P.now("v")
        dtile_lastuse[h] = P.now("v")

    P.op("vector", lambda e: e.memset(sqd[64:66, :], 1.0), inc="v")
    ev_sq65 = P.now("v")

    s2count = [0]
    t1q = []
    t2q = []

    def stage2(cj):
        ji, h, qc, kc, first, last = chunks[cj]
        pr, hf, pbi = chunk_slot[cj]
        pob = ji % 2
        if first:
            drain_parity(pob)
        P.op("tensor", (lambda b_=pbi, f2=hf, kc_=kc, h_=h, pb_=pob, f_=first, l_=last: (lambda e: e.matmul(
            po[pb_][0:65, :],
            Vx[:, 1040 * kc_ + 65 * h_:1040 * kc_ + 65 * (h_ + 1)],
            pbuf[b_][:, 512 * f2:512 * (f2 + 1)], start=f_, stop=l_)))(),
             waits=[s1evt[cj], ev_Vx] + ([po_free[pob]] if first else []), inc="t")
        if chunks[cj][5] or hf == 1:
            pb_free[pbi] = P.now("t")
        s2count[0] += 1
        if last:
            ev_av = P.now("t")
            P.op("vector", (lambda p_=pob: (lambda e: e.tensor_copy(
                sqd[64:65, 512 * p_:512 * (p_ + 1)], po[p_][64:65, :])))(),
                 waits=[ev_av, ev_vconst, drow_evt[pob]], inc="v")
            t1q.append((s2count[0] + 2, ji, P.now("v")))

    def drain_parity(pob):
        for q, fn in ((t1q, _t1_emit), (t2q, _t2_emit)):
            keep = []
            for item in q:
                if item[1] % 2 == pob:
                    fn(item)
                else:
                    keep.append(item)
            q[:] = keep

    def _t1_emit(item):
        _, ji, ev_dc = item
        _emit_t1(ji, ev_dc)

    def _t2_emit(item):
        _, ji, ev_rr = item
        _emit_t2(ji, ev_rr)

    def _emit_t1(ji, ev_dc):
        pob = ji % 2
        P.op("scalar", (lambda p_=pob: (lambda e: e.activation(
            rrow[64:66, 512 * p_:512 * (p_ + 1)],
            sqd[64:66, 512 * p_:512 * (p_ + 1)],
            AF.Ln, bias=0.0, scale=1.0)))(),
             waits=[ev_dc, rrow_evt[pob]], inc="s")
        drow_evt[pob] = P.now("s")
        P.op("scalar", (lambda p_=pob: (lambda e: e.activation(
            rrow[64:66, 512 * p_:512 * (p_ + 1)],
            rrow[64:66, 512 * p_:512 * (p_ + 1)],
            AF.Exp, bias=0.0, scale=-1.0)))(), inc="s")
        t2q.append((s2count[0] + 4, ji, P.now("s")))

    def _emit_t2(ji, ev_rr):
        h, qc, kcs = jobs[ji]
        pob = ji % 2
        P.op("tensor", (lambda p_=pob: (lambda e: e.matmul(
            pc[:], onesF[64:65, 0:128],
            rrow[64:65, 512 * p_:512 * (p_ + 1)], start=True, stop=True)))(),
             waits=[ev_rr, ev_vconst, pc_evt[0]], inc="t")
        ev_bc = P.now("t")
        rrow_evt[pob] = ev_bc
        P.op("vector", (lambda: (lambda e: e.tensor_copy(rbt[0][:], pc[:])))(),
             waits=[ev_bc, rb_evt[0]], inc="v")
        pc_evt[0] = P.now("v")
        P.op("vector", (lambda h_=h, q_=qc, p_=pob: (lambda e: e.tensor_mul(
            Ost[0:64, S * h_ + 512 * q_:S * h_ + 512 * (q_ + 1)],
            po[p_][0:64, :], rbt[0][0:64, :])))(), inc="v")
        ev = P.now("v")
        po_free[pob] = ev
        rb_evt[0] = ev
        ost_evt[(h, qc)] = ev

    def drain_tails(now, flush=False):
        while t1q and (flush or t1q[0][0] <= now):
            _, ji, ev_dc = t1q.pop(0)
            _emit_t1(ji, ev_dc)
        while t2q and (flush or t2q[0][0] <= now):
            _, ji, ev_rr = t2q.pop(0)
            _emit_t2(ji, ev_rr)

    drow_evt = [None, None]
    rrow_evt = [None, None]
    pc_evt = [None]
    rb_evt = [None]

    for j in range(NCK + LOOK):
        if j >= LOOK:
            stage2(j - LOOK)
            drain_tails(s2count[0])
        if j < NCK:
            stage1(j)
    drain_tails(0, flush=True)

    ev_attn = P.now("v")
    if phase_limit <= 4:
        dump_dbg(Ost[0:64, 0:16384], 16384, ev_attn, rows=64)
        return finish()

    p3p_es.close()
    phase_stacks.remove(p3p_es)
    es_qkv.close()
    phase_stacks.remove(es_qkv)
    P.barrier()
    p3b_es = ExitStack()
    phase_stacks.append(p3b_es)
    owt = p3b_es.enter_context(nc.sbuf_tensor("owt", [64, NH * 1024], BF16, side="right"))
    xrb = [p3b_es.enter_context(nc.sbuf_tensor(f"xrb{i}", [128, D], F32, side="right"))
           for i in range(2)]
    xnw = [p3b_es.enter_context(nc.sbuf_tensor(f"xnw{i}", [128, D], F32, side="right"))
           for i in range(2)]
    btm = [p3b_es.enter_context(nc.sbuf_tensor(f"btm{i}", [128, D], BF16, side="right"))
           for i in range(2)]
    btasm = [p3b_es.enter_context(nc.sbuf_tensor(f"btasm{i}", [128, D], BF16, side="right"))
             for i in range(2)]
    pat = [p3b_es.enter_context(nc.psum_tensor(f"pat{i}", [128, 512], F32)) for i in range(4)]
    pbt3A = p3b_es.enter_context(nc.psum_tensor("pbt3A", [128, 1024], BF16))
    pbt3B = p3b_es.enter_context(nc.psum_tensor("pbt3B", [128, 1024], BF16))

    for h in range(NH):
        P.op("gpsimd", (lambda h_=h: (lambda e: e.dma_start(
            out=owt[0:64, 1024 * h_:1024 * (h_ + 1)],
            in_=owT_d[64 * h_:64 * (h_ + 1), :])))(), inc="dow")
    ev_owt = P.now("dow")

    xrb_free = [None, None]
    xnw_free = [[], []]
    pat_free = [None] * 4
    btm_free = [None, None]
    btasm_free = [None, None]
    bst = {}
    bt_pend = [None]

    def emit_bt(sl, bi, ev_b):
        wts0 = [ev_b] + ([btasm_free[bi]] if btasm_free[bi] else [])
        ev_t, ev_bt = emit_transposes(
            btm[bi], NCH, (lambda b_=bi: (lambda i: btasm[b_][:, 128 * i:128 * (i + 1)]))(),
            wts0, (pbt3A, pbt3B), bst)
        btm_free[bi] = ev_t
        P.op("scalar", (lambda s_=sl, b_=bi: (lambda e: e.dma_start(
            out=btscr_d[s_], in_=btasm[b_][:].rearrange("p (a b) -> p a b", a=8))))(),
             waits=[ev_bt], inc=f"dbt{bi}")
        btasm_free[bi] = P.now(f"dbt{bi}")

    for sl in range(NSL):
        bi = sl % 2
        P.op("sync", (lambda s_=sl, b=bi: (lambda e: e.dma_start(
            out=xrb[b][:], in_=x_d[128 * s_:128 * (s_ + 1), :])))(),
             waits=[xrb_free[bi]], inc=f"dxr{bi}")
        ev_xr = P.now(f"dxr{bi}")
        qcsl = sl // 4
        for e2 in range(2):
            pi = (sl % 2) * 2 + e2
            wts = [ev_owt, ev_browb, pat_free[pi]] + [ost_evt[(h, qcsl)] for h in range(NH)]
            for h in range(NH):
                P.op("tensor", (lambda h_=h, e_=e2, s_=sl, p_=pi: (lambda e: e.matmul(
                    pat[p_][:],
                    Ost[0:64, S * h_ + 128 * s_:S * h_ + 128 * (s_ + 1)],
                    owt[0:64, 1024 * h_ + 512 * e_:1024 * h_ + 512 * (e_ + 1)],
                    start=(h_ == 0), stop=False)))(),
                     waits=wts if h == 0 else (), inc="t")
            P.op("tensor", (lambda e_=e2, p_=pi: (lambda e: e.matmul(
                pat[p_][:], onesB[0:1, 0:128], browb[0:1, 512 * e_:512 * (e_ + 1)],
                start=False, stop=True)))(), inc="t")
        if bt_pend[0] is not None:
            emit_bt(*bt_pend[0])
            bt_pend[0] = None
        ev_op = P.now("t")
        for e2 in range(2):
            pi = (sl % 2) * 2 + e2
            P.op("vector", (lambda b=bi, e_=e2, p_=pi: (lambda e: e.tensor_add(
                xnw[b][:, 512 * e_:512 * (e_ + 1)], pat[p_][:],
                xrb[b][:, 512 * e_:512 * (e_ + 1)])))(),
                 waits=[ev_op, ev_xr] + (xnw_free[bi] if e2 == 0 else []), inc="v")
            pat_free[pi] = P.now("v")
        ev_xn = P.now("v")
        xrb_free[bi] = ev_xn
        P.op("scalar", (lambda s_=sl, b=bi: (lambda e: e.dma_start(
            out=xres_d[128 * s_:128 * (s_ + 1), :], in_=xnw[b][:])))(),
             waits=[ev_xn], inc=f"dsp{bi}")
        ev_spill = P.now(f"dsp{bi}")
        ev_b = emit_ln(xnw[bi], btm[bi], [ev_xn, btm_free[bi]], 8 * bi)
        xnw_free[bi] = [ev_spill, ev_b]
        bt_pend[0] = (sl, bi, ev_b)
    if bt_pend[0] is not None:
        emit_bt(*bt_pend[0])
        bt_pend[0] = None

    ev_spills = [P.now("dsp0"), P.now("dsp1")]
    ev_btspill = [P.now("dbt0"), P.now("dbt1")]

    if phase_limit <= 5:
        return finish()

    p3b_es.close()
    phase_stacks.remove(p3b_es)
    p3_es.close()
    phase_stacks.remove(p3_es)
    P.barrier()
    p4_es = ExitStack()
    phase_stacks.append(p4_es)
    SB2 = 256
    NB2 = S // SB2
    Hb2 = [p4_es.enter_context(nc.sbuf_tensor(f"Hb2{i}", [128, 128 * SB2], BF16))
           for i in range(2)]
    btb = [p4_es.enter_context(nc.sbuf_tensor(f"btb{i}", [128, 8 * SB2], BF16))
           for i in range(2)]
    w2b = [p4_es.enter_context(nc.sbuf_tensor(f"w2b{i}", [128, 512], BF16)) for i in range(2)]
    wcs = [p4_es.enter_context(nc.sbuf_tensor(f"wcs{i}", [128, D], BF16)) for i in range(10)]
    xr2 = [p4_es.enter_context(nc.sbuf_tensor(f"xr2{i}", [128, D], F32)) for i in range(2)]
    ob2 = [p4_es.enter_context(nc.sbuf_tensor(f"ob2{i}", [128, D], F32)) for i in range(2)]
    pf = [p4_es.enter_context(nc.psum_tensor(f"pf{i}", [128, 512], F32)) for i in range(8)]

    wcs_free = [None] * 10
    w2b_free = [None, None]
    xr2_free = [None, None]
    ob2_free = [None, None]
    pf_free = [None] * 8
    btb_free = [None, None]
    hb2_free = [None, None]
    rails_done = {}
    cp_done = {-1: None}

    def load_btb(bk):
        if bk >= NB2:
            return
        p = bk % 2
        for oc in range(8):
            P.op("sync", (lambda o_=oc, b_=bk, p_=p: (lambda e: e.dma_start(
                out=btb[p_][:, SB2 * o_:SB2 * (o_ + 1)].rearrange("p (a b) -> p a b", a=2),
                in_=btscr_d[2 * b_:2 * (b_ + 1), :, o_, :].rearrange("a p b -> p a b"))))(),
                 waits=(ev_btspill + [btb_free[p]]) if oc == 0 else (), inc=f"dtb{p}")
        btb_free[p] = None
        return P.now(f"dtb{p}")

    btb_evt = {}
    btb_evt[0] = load_btb(0)

    def rails_unit(bk, u, standalone=False):
        p = bk % 2
        h4, rem = divmod(u, 32)
        oc, j = divmod(rem, 4)
        i = (u // 4) % 2
        if j == 0:
            P.op("sync", (lambda h_=h4, o_=oc, i_=i: (lambda e: e.dma_start(
                out=w2b[i_][:], in_=w2scr_d[h_, o_])))(),
                 waits=[ev_w2s[0], ev_w2s[1], w2b_free[i]], inc=f"dw2{i}")
        ev_wl = P.now(f"dw2{i}")
        zb = 4 + (u % 4)
        mwts = [ev_wl, btb_evt[bk], pf_free[zb]]
        P.op("tensor", (lambda i_=i, j_=j, o_=oc, z_=zb, p_=p: (lambda e: e.matmul(
            pf[z_][:, 0:SB2], w2b[i_][:, 128 * j_:128 * (j_ + 1)],
            btb[p_][:, SB2 * o_:SB2 * (o_ + 1)], start=True, stop=True)))(),
             waits=mwts, inc="t")
        ev_z = P.now("t")
        if j == 3:
            w2b_free[i] = ev_z
        if u == 127:
            btb_free[p] = ev_z
        mc = u
        gw = [ev_z]
        if u == 0:
            gw.append(hb2_free[p])
        P.op("scalar", (lambda m_=mc, z_=zb, p_=p: (lambda e: e.activation(
            Hb2[p_][:, SB2 * m_:SB2 * (m_ + 1)], pf[z_][:, 0:SB2], AF.Gelu,
            bias=0.0, scale=1.0)))(),
             waits=gw, inc="s")
        pf_free[zb] = P.now("s")
        if u == 127:
            rails_done[bk] = P.now("s")

    btb_evt[1] = load_btb(1)
    for u in range(128):
        rails_unit(0, u)

    for bk in range(NB2):
        p = bk % 2
        if bk + 2 < NB2:
            btb_evt[bk + 2] = load_btb(bk + 2)
        xr_evts = {}
        for sl2 in (1, 0):
            q4 = sl2
            sl = 2 * bk + sl2
            P.op("sync", (lambda s_=sl, q_=q4: (lambda e: e.dma_start(
                out=xr2[q_][:], in_=xres_d[128 * s_:128 * (s_ + 1), :])))(),
                 waits=[xr2_free[q4], ev_spills[0], ev_spills[1]], inc=f"dr2{q4}")
            xr_evts[sl2] = P.now(f"dr2{q4}")
        for mc in range(128):
            i = mc % 10
            P.op("sync", (lambda m_=mc, i_=i: (lambda e: e.dma_start(
                out=wcs[i_][:], in_=wcb_d[128 * m_:128 * (m_ + 1), :])))(),
                 waits=[wcs_free[i]] + (ev_wcb_done if (bk == 0 and mc < 10) else []),
                 inc=f"dwc{i}")
            ev_wc = P.now(f"dwc{i}")
            for sl2 in range(2):
                for e2 in range(2):
                    pb_i = sl2 * 2 + e2
                    mwts = [ev_wc, rails_done[bk]]
                    if mc == 0:
                        mwts.append(pf_free[pb_i])
                    P.op("tensor", (lambda m_=mc, s_=sl2, e_=e2, pi=pb_i, i_=i, p_=p: (lambda e: e.matmul(
                        pf[pi][:], Hb2[p_][:, SB2 * m_ + 128 * s_:SB2 * m_ + 128 * (s_ + 1)],
                        wcs[i_][:, 512 * e_:512 * (e_ + 1)],
                        start=(m_ == 0), stop=(m_ == 127))))(),
                         waits=mwts, inc="t")
            wcs_free[i] = P.now("t")
            if bk + 1 < NB2:
                rails_unit(bk + 1, mc)
        ev_cp = P.now("t")
        cp_done[bk] = ev_cp
        hb2_free[(bk + 2) % 2] = ev_cp
        for sl2 in (1, 0):
            sl = 2 * bk + sl2
            q4 = sl2
            ev_xr = xr_evts[sl2]
            for e2 in range(2):
                pb_i = sl2 * 2 + e2
                P.op("vector", (lambda q_=q4, e_=e2, pi=pb_i: (lambda e: e.tensor_add(
                    ob2[q_][:, 512 * e_:512 * (e_ + 1)], pf[pi][:],
                    xr2[q_][:, 512 * e_:512 * (e_ + 1)])))(),
                     waits=[ev_cp, ev_xr] + ([ob2_free[q4]] if e2 == 0 else []), inc="v")
                pf_free[pb_i] = P.now("v")
            ev_ot = P.now("v")
            xr2_free[q4] = ev_ot
            P.op("scalar", (lambda s_=sl, q_=q4: (lambda e: e.dma_start(
                out=out_d[128 * s_:128 * (s_ + 1), :], in_=ob2[q_][:])))(),
                 waits=[ev_ot], inc=f"dot{q4}")
            ob2_free[q4] = P.now(f"dot{q4}")

    return finish()



_NC_CACHE = {}


def _get_nc(phase_limit=99):
    if phase_limit not in _NC_CACHE:
        _NC_CACHE[phase_limit] = build_nc(phase_limit)
    return _NC_CACHE[phase_limit]


def kernel(L0_attn_x, L0_attn_wq, L0_attn_wk, L0_attn_wv, L0_attn_out_w,
           L0_attn_out_b, L0_ffn_rail_maker, L0_ffn_rail_up_proj, L0_ffn_w_cproj,
           _phase_limit=99, _cores=8):
    x = np.ascontiguousarray(np.asarray(L0_attn_x, dtype=np.float32))
    wq = np.asarray(L0_attn_wq, dtype=np.float32)
    wk = np.asarray(L0_attn_wk, dtype=np.float32)
    wv = np.asarray(L0_attn_wv, dtype=np.float32)
    ow = np.asarray(L0_attn_out_w, dtype=np.float32)
    ob = np.ascontiguousarray(np.asarray(L0_attn_out_b, dtype=np.float32).reshape(1, D))
    rm = np.asarray(L0_ffn_rail_maker, dtype=np.float32)
    ru = np.asarray(L0_ffn_rail_up_proj, dtype=np.float32)
    wc = np.asarray(L0_ffn_w_cproj, dtype=np.float32)

    wqT = np.ascontiguousarray(wq.T)
    wkT = np.ascontiguousarray(wk.T)
    wvT = np.ascontiguousarray(wv.T)
    owT = np.ascontiguousarray(ow.T)
    wcT = np.ascontiguousarray(wc.T)

    blkA = np.zeros((4, 8, 128, 128), dtype=np.float32)
    blkB = np.zeros((4, 8, 128, 512), dtype=np.float32)
    for h in range(4):
        for oc in range(8):
            for r in range(8):
                Mb = rm[h, 8 * oc + r]
                Ub = ru[h, 8 * oc + r]
                blkA[h, oc, 16 * r:16 * (r + 1), 16 * r:16 * (r + 1)] = Mb.T
                blkB[h, oc, 16 * r:16 * (r + 1), 64 * r:64 * (r + 1)] = Ub.T

    nc = _get_nc(_phase_limit)
    shared = dict(wqT=wqT, wkT=wkT, wvT=wvT, owT=owT, ob=ob,
                  blkA=blkA, blkB=blkB, wcT=wcT)
    in_maps = [dict(shared, x=np.ascontiguousarray(x[i])) for i in range(_cores)]
    res = run_bass_kernel_spmd(nc, in_maps, core_ids=list(range(_cores)))
    if _phase_limit < 99:
        return res
    return np.stack([np.asarray(r["out"]) for r in res.results], axis=0)


# version 18
# speedup vs baseline: 1.3539x; 1.2987x over previous
import math
from contextlib import ExitStack

import numpy as np

import concourse.bass as bass
import concourse.mybir as mybir
from concourse.bass_utils import run_bass_kernel_spmd

F32 = mybir.dt.float32
BF16 = mybir.dt.bfloat16
I32 = mybir.dt.int32
AF = mybir.ActivationFunctionType
OP = mybir.AluOpType
AX = mybir.AxisListType

S = 2048
D = 1024
NH = 16
NSL = S // 128
NCH = D // 128
MF = 16384
SBK = 512
NBK = S // SBK
EPS = 1e-5
MSUB = 12.0
TWO_PI = 6.283185307179586
HALF_PI = 1.5707963267948966
CLN = math.log(10000.0) / 512.0

ENGINES = ("sync", "gpsimd", "tensor", "vector", "scalar")


def slope(h):
    return 2.0 ** (-8.0 * (h + 1) / 16.0)


def kc_min_for(h, qc):
    lo = 0
    for d in range(1, 13):
        if -MSUB + slope(h) * (127.0 - 128.0 * d) <= -124.0:
            lo = max(lo, 4 * qc - d + 1)
    return max(0, lo)


class Prog:
    def __init__(self, nc, es):
        self.nc = nc
        self.es = es
        self.ops = {e: [] for e in ENGINES}
        self.sems = {}
        self.cnt = {}
        for name in ("t", "v", "s", "g"):
            self._mk(name)

    def _mk(self, name):
        self.sems[name] = self.es.enter_context(self.nc.semaphore(name))
        self.cnt[name] = 0

    ENG_SEM = {"vector": "v", "scalar": "s", "gpsimd": "g"}

    gate = {}

    def op(self, eng, fn, waits=(), inc=None):
        gw = self.gate.pop(eng, None) if self.gate else None
        if gw:
            waits = list(waits) + gw
        esem = self.ENG_SEM.get(eng)
        pre_self = self.cnt.get(esem, 0) if esem is not None else 0
        incs = []
        is_dma = False
        if inc is not None:
            for snm in ([inc] if isinstance(inc, str) else list(inc)):
                if snm not in self.sems:
                    self._mk(snm)
                n = 16 if snm.startswith("d") else 1
                if snm.startswith("d"):
                    is_dma = True
                self.cnt[snm] += n
                incs.append((snm, n))
        flat = []
        for w in waits:
            if w is None:
                continue
            if isinstance(w, list):
                flat.extend(x for x in w if x is not None)
            else:
                flat.append(w)
        if esem is not None:
            flat.append((esem, pre_self))
            if not is_dma and not any(s == esem for (s, _n) in incs):
                self.cnt[esem] += 1
                incs.append((esem, 1))
        flat = tuple((s, v) for (s, v) in flat if v > 0)
        self.ops[eng].append((fn, flat, incs))
        return (incs[0][0], self.cnt[incs[0][0]]) if incs else None

    def now(self, sem):
        if sem not in self.sems:
            self._mk(sem)
        return (sem, self.cnt[sem])

    def barrier(self):
        gate = [self.now(x) for x in list(self.sems.keys())]
        self.gate = {e: list(gate) for e in ENGINES}

    def build(self, blk):
        sems = self.sems
        for ename in ENGINES:
            eng_ops = self.ops[ename]
            if not eng_ops:
                continue

            def make(ops_):
                def body(eng):
                    lw = {}
                    for fn, waits, incs in ops_:
                        for (snm, v) in waits:
                            if lw.get(snm, 0) < v:
                                eng.wait_ge(sems[snm], v)
                                lw[snm] = v
                        ins = fn(eng)
                        for (snm, n) in incs:
                            ins = ins.then_inc(sems[snm], n)
                return body

            getattr(blk, ename)(make(eng_ops))


def build_nc(phase_limit=99):
    nc = bass.Bass("TRN2", target_bir_lowering=False)

    x_d = nc.dram_tensor("x", [S, D], F32, kind="ExternalInput").ap()
    wqT_d = nc.dram_tensor("wqT", [D, D], F32, kind="ExternalInput").ap()
    wkT_d = nc.dram_tensor("wkT", [D, D], F32, kind="ExternalInput").ap()
    wvT_d = nc.dram_tensor("wvT", [D, D], F32, kind="ExternalInput").ap()
    owT_d = nc.dram_tensor("owT", [D, D], F32, kind="ExternalInput").ap()
    ob_d = nc.dram_tensor("ob", [1, D], F32, kind="ExternalInput").ap()
    blkA_d = nc.dram_tensor("blkA", [4, 8, 128, 128], F32, kind="ExternalInput").ap()
    blkB_d = nc.dram_tensor("blkB", [4, 8, 128, 512], F32, kind="ExternalInput").ap()
    wcT_d = nc.dram_tensor("wcT", [MF, D], F32, kind="ExternalInput").ap()
    out_d = nc.dram_tensor("out", [S, D], F32, kind="ExternalOutput").ap()
    xres_d = nc.dram_tensor("xres", [S, D], F32, kind="Internal").ap()
    btscr_d = nc.dram_tensor("btscr", [NSL, 128, 8, 128], BF16, kind="Internal").ap()
    wcb_d = nc.dram_tensor("wcb", [MF, D], BF16, kind="Internal").ap()
    w2scr_d = nc.dram_tensor("w2scr", [4, 8, 128, 512], BF16, kind="Internal").ap()

    es = ExitStack()
    P = Prog(nc, es)

    cpool = es.enter_context(nc.sbuf_tensor("cpool", [128, 200], F32))
    ident = es.enter_context(nc.sbuf_tensor("ident", [128, 128], BF16))
    onesF = es.enter_context(nc.sbuf_tensor("onesF", [128, 128], F32))
    onesB = es.enter_context(nc.sbuf_tensor("onesB", [128, 128], BF16))
    xfar = es.enter_context(nc.sbuf_tensor("xfar", [128, 512], BF16))
    xdg = [es.enter_context(nc.sbuf_tensor(f"xdg{i}", [128, 512], BF16)) for i in range(4)]
    rrow = es.enter_context(nc.sbuf_tensor("rrow", [128, 1024], F32))
    sqd = es.enter_context(nc.sbuf_tensor("sqd", [128, 1024], BF16))
    lnst = es.enter_context(nc.sbuf_tensor("lnst", [128, 16], F32))
    browb = es.enter_context(nc.sbuf_tensor("browb", [1, D], BF16))

    CP_FAR0 = 0
    CP_M12 = 192
    CP_EPS = 193
    CP_FRQ0 = 194
    cpool_vals = {}
    for h in range(NH):
        for d in range(1, 13):
            cpool_vals[CP_FAR0 + h * 12 + (d - 1)] = -MSUB + slope(h) * (127.0 - 128.0 * d)
    cpool_vals[CP_M12] = -MSUB
    cpool_vals[CP_EPS] = EPS
    for c in range(4):
        cpool_vals[CP_FRQ0 + c] = -128.0 * c * CLN

    items = sorted(cpool_vals.items())
    for j, (idx, val) in enumerate(items):
        P.op("vector", (lambda i=idx, vv=val: (lambda e: e.memset(cpool[:, i:i + 1], vv)))(),
             inc="v" if j == len(items) - 1 else None)
    ev_cpool = P.now("v")
    P.op("vector", lambda e: e.memset(onesF[:], 1.0))
    P.op("vector", lambda e: e.memset(onesB[:], 1.0))
    P.op("vector", lambda e: e.memset(sqd[:], 1.0))
    P.op("vector", lambda e: e.memset(rrow[:], 1.0), inc="v")
    ev_vconst = P.now("v")

    P.op("gpsimd", lambda e: e.affine_select(ident[:], onesB[:], [[-1, 128]],
                                             OP.is_equal, 0.0, base=0,
                                             channel_multiplier=1),
         waits=[ev_vconst], inc="g")
    P.op("gpsimd", lambda e: e.iota(xfar[:], [[-1, 512]], base=-127,
                                    channel_multiplier=1,
                                    allow_small_or_imprecise_dtypes=True), inc="g")
    for sf in range(4):
        P.op("gpsimd", (lambda s_=sf: (lambda e: e.iota(
            xdg[s_][:], [[-1, 512]], base=128 * s_, channel_multiplier=1,
            allow_small_or_imprecise_dtypes=True)))(), inc="g")
    ev_giota = P.now("g")
    for sf in range(4):
        P.op("gpsimd", (lambda s_=sf: (lambda e: e.affine_select(
            xdg[s_][:], xdg[s_][:], [[1, 512]], OP.is_ge, -1.0e5,
            base=-128 * s_, channel_multiplier=-1)))(),
             waits=[ev_giota], inc="g")
    ev_gconst = P.now("g")

    P.op("gpsimd", lambda e: e.dma_start(out=browb[:], in_=ob_d), inc="dbr")
    ev_browb = P.now("dbr")

    dbg_d = None
    if phase_limit < 99:
        dbg_d = nc.dram_tensor("dbg", [128, 16384], F32, kind="ExternalOutput").ap()

    def dump_dbg(src_ap, cols, prior_ev, rows=128):
        if src_ap.dtype == F32:
            P.op("scalar", lambda e: e.dma_start(out=dbg_d[0:rows, 0:cols], in_=src_ap),
                 waits=[prior_ev], inc="ddb")
        else:
            n = 0
            last_st = None
            while n < cols:
                wdt = min(1024, cols - n)
                P.op("vector", (lambda nn=n, ww=wdt, sp=src_ap, r_=rows: (lambda e: e.tensor_copy(
                    sqd[0:r_, 0:ww], sp[:, nn:nn + ww])))(), waits=[prior_ev, last_st], inc="v")
                evc = P.now("v")
                P.op("scalar", (lambda nn=n, ww=wdt, r_=rows: (lambda e: e.dma_start(
                    out=dbg_d[0:r_, nn:nn + ww], in_=sqd[0:r_, 0:ww])))(),
                     waits=[evc], inc="ddb")
                last_st = P.now("ddb")
                n += wdt

    phase_stacks = []

    def finish():
        for snm in list(P.sems.keys()):
            if snm.startswith("d"):
                tot = P.cnt[snm]
                if tot > 0:
                    P.op("sync", (lambda s_=snm, t_=tot: (lambda e: e.wait_ge(
                        P.sems[s_], t_)))())
        for st in reversed(phase_stacks):
            st.close()
        blk = es.enter_context(nc.Block())
        P.build(blk)
        es.close()
        return nc

    def emit_ln(xtile, otile, waits0, col0):
        c = col0
        P.op("vector", lambda e: e.reduce_sum(lnst[:, c + 0:c + 1], xtile[:], axis=AX.X),
             waits=list(waits0), inc="v")
        ev1 = P.now("v")
        P.op("scalar", lambda e: e.activation(lnst[:, c + 1:c + 2], lnst[:, c + 0:c + 1],
                                              AF.Copy, bias=0.0, scale=-1.0 / D),
             waits=[ev1], inc="s")
        P.op("scalar", lambda e: e.activation(sqd[:], xtile[:], AF.Square,
                                              bias=lnst[:, c + 1:c + 2], scale=1.0,
                                              accum_out=lnst[:, c + 2:c + 3]), inc="s")
        P.op("scalar", lambda e: e.activation(lnst[:, c + 3:c + 4], lnst[:, c + 2:c + 3],
                                              AF.Ln, bias=cpool[:, CP_EPS:CP_EPS + 1],
                                              scale=1.0 / D),
             waits=[ev_cpool], inc="s")
        P.op("scalar", lambda e: e.activation(lnst[:, c + 4:c + 5], lnst[:, c + 3:c + 4],
                                              AF.Exp, bias=0.0, scale=-0.5), inc="s")
        ev5 = P.now("s")
        P.op("vector", lambda e: e.tensor_mul(lnst[:, c + 5:c + 6], lnst[:, c + 1:c + 2],
                                              lnst[:, c + 4:c + 5]),
             waits=[ev5], inc="v")
        ev6 = P.now("v")
        P.op("scalar", lambda e: e.activation(otile[:], xtile[:], AF.Identity,
                                              bias=lnst[:, c + 5:c + 6],
                                              scale=lnst[:, c + 4:c + 5]),
             waits=[ev6], inc="s")
        return P.now("s")

    def emit_transposes(src_tile, n_slices, dst_fn, waits0, pair, state):
        evs = state.setdefault("evs", [])
        for i in range(n_slices):
            k = state.get("k", 0)
            bank = pair[k % 2]
            slot = (k // 2) % 8
            wts = list(waits0) + [ev_gconst]
            if len(evs) >= 2:
                wts.append(evs[-2])
            if len(evs) >= 16:
                wts.append(evs[-16])
            P.op("tensor", (lambda b_=bank, sl_=slot, i_=i, sp=src_tile: (lambda e: e.transpose(
                b_[:, 128 * sl_:128 * (sl_ + 1)],
                sp[:, 128 * i_:128 * (i_ + 1)], ident[:])))(),
                 waits=wts, inc="t")
            ev_tr = P.now("t")
            dst = dst_fn(i)
            P.op("vector", (lambda b_=bank, sl_=slot, d_=dst: (lambda e: e.tensor_copy(
                d_, b_[:, 128 * sl_:128 * (sl_ + 1)])))(),
                 waits=[ev_tr], inc="v")
            evs.append(P.now("v"))
            state["k"] = k + 1
        return P.now("t"), P.now("v")

    es_at = ExitStack()
    phase_stacks.append(es_at)
    At = es_at.enter_context(nc.sbuf_tensor("At", [128, S * NCH], BF16, side="right"))
    p1_es = ExitStack()
    phase_stacks.append(p1_es)
    xin = [p1_es.enter_context(nc.sbuf_tensor(f"xin{i}", [128, D], F32, side="right"))
           for i in range(2)]
    atm = [p1_es.enter_context(nc.sbuf_tensor(f"atm{i}", [128, D], BF16, side="right"))
           for i in range(2)]
    pbtA = p1_es.enter_context(nc.psum_tensor("pbtA", [128, 1024], BF16))
    pbtB = p1_es.enter_context(nc.psum_tensor("pbtB", [128, 1024], BF16))

    xin_free = [None, None]
    atm_free = [None, None]
    tst = {}
    for sl in range(NSL):
        bi = sl % 2
        P.op("sync", (lambda s_=sl, b=bi: (lambda e: e.dma_start(
            out=xin[b][:], in_=x_d[128 * s_:128 * (s_ + 1), :])))(),
             waits=[xin_free[bi]], inc=f"dx{bi}")
        ev_x = P.now(f"dx{bi}")
        ev_a = emit_ln(xin[bi], atm[bi], [ev_x, atm_free[bi]], 8 * bi)
        xin_free[bi] = ev_a
        ev_t, ev_v = emit_transposes(
            atm[bi], NCH,
            (lambda s_=sl: (lambda i: At[:, S * i + 128 * s_: S * i + 128 * (s_ + 1)]))(),
            [ev_a], (pbtA, pbtB), tst)
        atm_free[bi] = ev_t
    ev_At = P.now("v")

    if phase_limit <= 1:
        dump_dbg(At[:, 0:16384], 16384, ev_At)
        return finish()

    p1_es.close()
    phase_stacks.remove(p1_es)
    P.barrier()
    es_qkv = ExitStack()
    phase_stacks.append(es_qkv)
    QT = es_qkv.enter_context(nc.sbuf_tensor("QT", [128, S * NCH], BF16))
    KT = es_qkv.enter_context(nc.sbuf_tensor("KT", [128, S * NCH], BF16))
    Vx = es_qkv.enter_context(nc.sbuf_tensor("Vx", [128, NSL * 1040], BF16))

    p2_es = ExitStack()
    phase_stacks.append(p2_es)
    iotaS = p2_es.enter_context(nc.sbuf_tensor("iotaS", [128, S], F32, side="right"))
    scrW = p2_es.enter_context(nc.sbuf_tensor("scrW", [128, S], F32, side="right"))
    scrI = p2_es.enter_context(nc.sbuf_tensor("scrI", [128, S // 4], I32, side="right"))
    scrB = p2_es.enter_context(nc.sbuf_tensor("scrB", [128, S // 4], F32, side="right"))
    sintab = p2_es.enter_context(nc.sbuf_tensor("sintab", [128, S], BF16, side="right"))
    costab = p2_es.enter_context(nc.sbuf_tensor("costab", [128, S], BF16, side="right"))
    rtmp = [p2_es.enter_context(nc.sbuf_tensor(f"rtmp{i}", [128, S], BF16, side="right"))
            for i in range(2)]
    freqs = p2_es.enter_context(nc.sbuf_tensor("freqs", [128, 8], F32, side="right"))
    wqb = [p2_es.enter_context(nc.sbuf_tensor(f"wqb{i}", [128, 2048], BF16, side="right"))
           for i in range(2)]
    vts = [p2_es.enter_context(nc.sbuf_tensor(f"vts{i}", [128, S], BF16, side="right"))
           for i in range(2)]
    pqk = [p2_es.enter_context(nc.psum_tensor(f"pqk{i}", [128, 512], F32)) for i in range(4)]
    pbt2A = p2_es.enter_context(nc.psum_tensor("pbt2A", [128, 1024], BF16))
    pbt2B = p2_es.enter_context(nc.psum_tensor("pbt2B", [128, 1024], BF16))

    qk_evt = {}
    wqb_free = [None, None]
    bank_free = [None] * 4
    for (pname, wd, dst, scl) in (("q", wqT_d, QT, 0.125), ("k", wkT_d, KT, 1.0)):
        for qtr in range(4):
            wi = qtr % 2
            for e8 in range(8):
                P.op("gpsimd", (lambda wd_=wd, q_=qtr, e_=e8, i_=wi: (lambda e: e.dma_start(
                    out=wqb[i_][:, 256 * e_:256 * (e_ + 1)],
                    in_=wd_[128 * e_:128 * (e_ + 1), 256 * q_:256 * (q_ + 1)])))(),
                     waits=[wqb_free[wi]] if e8 == 0 else (), inc=f"dw{wi}")
            ev_w = P.now(f"dw{wi}")
            last_mm = None
            for d2 in range(2):
                ch = 2 * qtr + d2
                for e8 in range(8):
                    for sc in range(4):
                        wts = [ev_w, ev_At]
                        if e8 == 0:
                            wts.append(bank_free[sc])
                        P.op("tensor", (lambda i_=wi, e_=e8, d_=d2, s_=sc: (lambda e: e.matmul(
                            pqk[s_][:],
                            wqb[i_][:, 256 * e_ + 128 * d_:256 * e_ + 128 * (d_ + 1)],
                            At[:, S * e_ + 512 * s_:S * e_ + 512 * (s_ + 1)],
                            start=(e_ == 0), stop=(e_ == 7))))(),
                             waits=wts if e8 == 0 else (), inc="t")
                ev_mm = P.now("t")
                last_mm = ev_mm
                for sc in range(4):
                    P.op("scalar", (lambda s_=sc, ch_=ch, dst_=dst, sc_=scl: (lambda e: e.activation(
                        dst_[:, S * ch_ + 512 * s_:S * ch_ + 512 * (s_ + 1)],
                        pqk[s_][:], AF.Copy, bias=0.0, scale=sc_)))(),
                         waits=[ev_mm], inc="s")
                    bank_free[sc] = P.now("s")
                qk_evt[(pname, ch)] = P.now("s")
            wqb_free[wi] = last_mm

    P.op("gpsimd", lambda e: e.iota(iotaS[:], [[1, S]], base=0, channel_multiplier=0,
                                    allow_small_or_imprecise_dtypes=True), inc="g")
    ev_iotaS = P.now("g")
    P.op("gpsimd", lambda e: e.iota(freqs[:, 0:1], [[0, 1]], base=0, channel_multiplier=1,
                                    allow_small_or_imprecise_dtypes=True), inc="g")
    ev_iotaP = P.now("g")
    for c in range(4):
        P.op("scalar", (lambda c_=c: (lambda e: e.activation(
            freqs[:, 4 + c_:5 + c_], freqs[:, 0:1], AF.Exp,
            bias=cpool[:, CP_FRQ0 + c_:CP_FRQ0 + c_ + 1], scale=-CLN)))(),
             waits=[ev_iotaP, ev_cpool], inc="s")
    ev_freq = P.now("s")
    P.op("vector", lambda e: e.tensor_scalar_mul(freqs[:, 0:4], freqs[:, 4:8],
                                                 1.0 / TWO_PI),
         waits=[ev_freq], inc="v")
    ev_freq2 = P.now("v")

    def emit_tab(tab, c, add_quarter, waits0):
        if add_quarter:
            P.op("vector", (lambda c_=c: (lambda e: e.tensor_scalar(
                scrW[:], iotaS[:], freqs[:, c_:c_ + 1], 0.25, OP.mult, OP.add)))(),
                 waits=list(waits0), inc="v")
        else:
            P.op("vector", (lambda c_=c: (lambda e: e.tensor_scalar_mul(
                scrW[:], iotaS[:], freqs[:, c_:c_ + 1])))(),
                 waits=list(waits0), inc="v")
        for hf in range(4):
            P.op("vector", (lambda h_=hf: (lambda e: e.tensor_copy(
                scrI[:], scrW[:, 512 * h_:512 * (h_ + 1)])))(), inc="v")
            P.op("vector", lambda e: e.tensor_copy(scrB[:], scrI[:]), inc="v")
            P.op("vector", (lambda h_=hf: (lambda e: e.tensor_tensor(
                scrW[:, 512 * h_:512 * (h_ + 1)],
                scrW[:, 512 * h_:512 * (h_ + 1)], scrB[:], OP.subtract)))(), inc="v")
        P.op("vector", lambda e: e.scalar_tensor_tensor(
            scrW[:], scrW[:], 0.5, scrW[:], OP.is_ge, OP.subtract), inc="v")
        ev_w = P.now("v")
        P.op("scalar", (lambda t_=tab: (lambda e: e.activation(
            t_[:], scrW[:], AF.Sin, bias=0.0, scale=-TWO_PI)))(),
             waits=[ev_w, prev_tab_reader[0]], inc="s")
        return P.now("s")

    rope_evt = {}
    prev_tab_reader = [None]
    prev_act = [None]
    for c in range(4):
        prev_act[0] = emit_tab(sintab, c, False, [ev_iotaS, ev_freq2, prev_act[0]])
        ev_cos = emit_tab(costab, c, True, [ev_iotaS, ev_freq2, prev_act[0]])
        prev_act[0] = ev_cos
        for (pname, T) in (("q", QT), ("k", KT)):
            lo = lambda T_=T, o_=S * c: T_[:, o_:o_ + S]
            hi = lambda T_=T, o_=S * (c + 4): T_[:, o_:o_ + S]
            P.op("vector", (lambda hi_=hi(): (lambda e: e.tensor_mul(
                rtmp[0][:], hi_, sintab[:])))(),
                 waits=[ev_cos, qk_evt[(pname, c)], qk_evt[(pname, c + 4)]], inc="v")
            P.op("vector", (lambda hi_=hi(): (lambda e: e.tensor_mul(
                rtmp[1][:], hi_, costab[:])))(), inc="v")
            P.op("vector", (lambda lo_=lo(), hi_=hi(): (lambda e: e.tensor_mul(
                hi_, lo_, sintab[:])))(), inc="v")
            P.op("vector", (lambda hi_=hi(): (lambda e: e.tensor_add(
                hi_, hi_, rtmp[1][:])))(), inc="v")
            P.op("vector", (lambda lo_=lo(): (lambda e: e.tensor_mul(
                lo_, lo_, costab[:])))(), inc="v")
            P.op("vector", (lambda lo_=lo(): (lambda e: e.tensor_sub(
                lo_, lo_, rtmp[0][:])))(), inc="v")
            rope_evt[(pname, c)] = rope_evt[(pname, c + 4)] = P.now("v")
        prev_tab_reader[0] = P.now("v")

    if phase_limit <= 2:
        dump_dbg(QT[:, 0:16384], 16384, P.now("v"))
        return finish()

    for sl in range(NSL):
        P.op("vector", (lambda s_=sl: (lambda e: e.memset(
            Vx[:, 1040 * s_:1040 * (s_ + 1)].rearrange(
                "p (a b) -> p a b", a=16)[:, :, 64:65], 1.0)))(),
             inc="v" if sl == NSL - 1 else None)
    ones_evt = P.now("v")

    vst = {}
    vts_free = [None, None]
    for vq in range(4):
        wi = vq % 2
        for e8 in range(8):
            P.op("gpsimd", (lambda q_=vq, e_=e8, i_=wi: (lambda e: e.dma_start(
                out=wqb[i_][:, 256 * e_:256 * (e_ + 1)],
                in_=wvT_d[128 * e_:128 * (e_ + 1), 256 * q_:256 * (q_ + 1)])))(),
                 waits=[wqb_free[wi]] if e8 == 0 else (), inc=f"dw{wi}")
        ev_w = P.now(f"dw{wi}")
        last_mm = None
        for d2 in range(2):
            dvch = 2 * vq + d2
            vb = dvch % 2
            for e8 in range(8):
                for sc in range(4):
                    wts = [ev_w, ev_At]
                    if e8 == 0:
                        wts.append(bank_free[sc])
                    P.op("tensor", (lambda i_=wi, e_=e8, d_=d2, s_=sc: (lambda e: e.matmul(
                        pqk[s_][:],
                        wqb[i_][:, 256 * e_ + 128 * d_:256 * e_ + 128 * (d_ + 1)],
                        At[:, S * e_ + 512 * s_:S * e_ + 512 * (s_ + 1)],
                        start=(e_ == 0), stop=(e_ == 7))))(),
                         waits=wts if e8 == 0 else (), inc="t")
            ev_mm = P.now("t")
            last_mm = ev_mm
            for sc in range(4):
                P.op("scalar", (lambda s_=sc, v_=vb: (lambda e: e.activation(
                    vts[v_][:, 512 * s_:512 * (s_ + 1)], pqk[s_][:], AF.Copy,
                    bias=0.0, scale=1.0)))(),
                     waits=[ev_mm, vts_free[vb]], inc="s")
                bank_free[sc] = P.now("s")
            ev_vts = P.now("s")
            h0 = 2 * dvch
            emit_transposes(
                vts[vb], NSL,
                (lambda h_=h0: (lambda i: Vx[:, 1040 * i + 65 * h_: 1040 * i + 65 * h_ + 130].rearrange(
                    "p (a b) -> p a b", a=2)[:, :, 0:64]))(),
                [ev_vts, ones_evt], (pbt2A, pbt2B), vst)
            vts_free[vb] = P.now("t")
        wqb_free[wi] = last_mm
    ev_Vx = P.now("v")

    if phase_limit <= 3:
        dump_dbg(Vx[:, 0:16384], 16384, ev_Vx)
        return finish()

    p2_es.close()
    phase_stacks.remove(p2_es)
    es_at.close()
    phase_stacks.remove(es_at)
    P.barrier()
    p3_es = ExitStack()
    phase_stacks.append(p3_es)
    p3p_es = ExitStack()
    phase_stacks.append(p3p_es)
    Ost = p3_es.enter_context(nc.sbuf_tensor("Ost", [64, NH * S], BF16, side="right"))
    dfar = [p3_es.enter_context(nc.sbuf_tensor(f"dfar{j}", [128, 512], BF16, side="right"))
            for j in range(2)]
    ddg = [[p3_es.enter_context(nc.sbuf_tensor(f"ddg{j}_{i}", [128, 512], BF16, side="right"))
            for i in range(4)] for j in range(2)]
    ebuf = [p3_es.enter_context(nc.sbuf_tensor(f"ebuf{i}", [128, 1024], BF16, side="right"))
            for i in range(2)]
    qstg = [p3_es.enter_context(nc.sbuf_tensor(f"qstg{i}", [128, 512], BF16, side="right"))
            for i in range(2)]
    pbuf = [p3_es.enter_context(nc.sbuf_tensor(f"pbuf{i}", [128, 1024], BF16, side="right"))
            for i in range(4)]
    rbt = [p3_es.enter_context(nc.sbuf_tensor(f"rbt{i}", [128, 512], BF16, side="right"))
           for i in range(1)]
    ps = [p3p_es.enter_context(nc.psum_tensor(f"ps{i}", [128, 1024], F32)) for i in range(2)]
    po = [p3p_es.enter_context(nc.psum_tensor(f"po{i}", [128, 512], F32)) for i in range(2)]
    pc = p3p_es.enter_context(nc.psum_tensor("pc", [128, 512], F32))

    ba = [p3_es.enter_context(nc.sbuf_tensor(f"ba{i}", [128, 128], BF16, side="right"))
          for i in range(2)]
    bbuf = [p3_es.enter_context(nc.sbuf_tensor(f"bbuf{i}", [128, 512], BF16, side="right"))
            for i in range(2)]
    w2asm = [p3_es.enter_context(nc.sbuf_tensor(f"w2asm{i}", [128, 512], BF16, side="right"))
             for i in range(2)]
    wcv = [p3_es.enter_context(nc.sbuf_tensor(f"wcv{i}", [128, D // 2], BF16, side="right"))
           for i in range(2)]
    pw2 = p3p_es.enter_context(nc.psum_tensor("pw2", [128, 512], F32))

    ba_free = [None, None]
    bb_free = [None, None]
    w2_free = [None, None]
    pw2_free = None
    for h4 in range(4):
        for oc in range(8):
            i = (h4 * 8 + oc) % 2
            P.op("gpsimd", (lambda h_=h4, o_=oc, i_=i: (lambda e: e.dma_start(
                out=ba[i_][:], in_=blkA_d[h_, o_])))(),
                 waits=[ba_free[i]], inc=f"dba{i}")
            ev_lda = P.now(f"dba{i}")
            P.op("gpsimd", (lambda h_=h4, o_=oc, i_=i: (lambda e: e.dma_start(
                out=bbuf[i_][:], in_=blkB_d[h_, o_])))(),
                 waits=[bb_free[i]], inc=f"dbb{i}")
            ev_ldb = P.now(f"dbb{i}")
            P.op("tensor", (lambda i_=i: (lambda e: e.matmul(
                pw2[:], ba[i_][:], bbuf[i_][:], start=True, stop=True)))(),
                 waits=[ev_lda, ev_ldb, pw2_free], inc="t")
            ev_mm = P.now("t")
            ba_free[i] = ev_mm
            bb_free[i] = ev_mm
            P.op("vector", (lambda i_=i: (lambda e: e.tensor_copy(w2asm[i_][:], pw2[:])))(),
                 waits=[ev_mm, w2_free[i]], inc="v")
            pw2_free = P.now("v")
            P.op("scalar", (lambda h_=h4, o_=oc, i_=i: (lambda e: e.dma_start(
                out=w2scr_d[h_, o_], in_=w2asm[i_][:])))(),
                 waits=[pw2_free], inc=f"dws{i}")
            w2_free[i] = P.now(f"dws{i}")
    ev_w2s = [P.now("dws0"), P.now("dws1")]

    wcv_ld = [None, None]
    wcv_st = [None, None]
    for mh in range(256):
        mc, hh = divmod(mh, 2)
        i = mh % 2
        P.op("gpsimd", (lambda m_=mc, h_=hh, i_=i: (lambda e: e.dma_start(
            out=wcv[i_][:], in_=wcT_d[128 * m_:128 * (m_ + 1), 512 * h_:512 * (h_ + 1)])))(),
             waits=[wcv_st[i]], inc=f"dcv{i}")
        wcv_ld[i] = P.now(f"dcv{i}")
        P.op("sync", (lambda m_=mc, h_=hh, i_=i: (lambda e: e.dma_start(
            out=wcb_d[128 * m_:128 * (m_ + 1), 512 * h_:512 * (h_ + 1)], in_=wcv[i_][:])))(),
             waits=[wcv_ld[i]], inc=f"dcb{i}")
        wcv_st[i] = P.now(f"dcb{i}")
    ev_wcb_done = [P.now("dcb0"), P.now("dcb1")]

    ps_free = [None] * 2
    eb_free = [None] * 2
    pb_free = [None] * 4
    qs_free = [None, None]
    qs_evt = {}
    pend_s = []
    pair_no = [0]
    po_free = [None, None]
    pc_free = None
    rb_free = None
    drow_free = None
    rrow_free = None
    ost_evt = {}

    LOOK = 6
    jobs = []
    for h in range(NH):
        for qc in range(4):
            kmin = kc_min_for(h, qc)
            diag = [4 * qc + i for i in range(4)]
            far = [k for k in range(kmin, 4 * qc)]
            jobs.append((h, qc, diag + far))
    chunks = []
    for ji, (h, qc, kcs) in enumerate(jobs):
        for i, kc in enumerate(kcs):
            chunks.append((ji, h, qc, kc, i == 0, i == len(kcs) - 1))
    NCK = len(chunks)

    dtile_evt = {}
    dtile_lastuse = {}
    s1evt = [None] * NCK
    tails1 = []
    ev_avlast = {}

    def emit_dtiles(h):
        if h >= NH or h in dtile_evt:
            return
        hp = h % 2
        sh = slope(h)
        prev = dtile_lastuse.get(h - 2)
        P.op("scalar", (lambda sh_=sh, j_=hp: (lambda e: e.activation(
            dfar[j_][:], xfar[:], AF.Exp, bias=0.0, scale=sh_)))(),
             waits=[ev_gconst, prev], inc="s")
        for sf in range(4):
            P.op("scalar", (lambda sh_=sh, s_=sf, j_=hp: (lambda e: e.activation(
                ddg[j_][s_][:], xdg[s_][:], AF.Exp, bias=0.0, scale=sh_)))(), inc="s")
        dtile_evt[h] = P.now("s")

    cur_pair = [0]
    chunk_slot = {}

    def stage1(cj):
        ji, h, qc, kc, first, last = chunks[cj]
        if h not in dtile_evt:
            emit_dtiles(h)
        if qc == 2 and first:
            emit_dtiles(h + 1)
        hb = 64 * (h % 2)
        hc = h // 2
        if first:
            qp = qc % 2
            P.op("vector", (lambda p_=qp, hb_=hb: (lambda e: e.memset(
                qstg[p_][64 - hb_:128 - hb_, :], 0.0)))(),
                 waits=[qs_free[qp]], inc="v")
            P.op("vector", (lambda p_=qp, hb_=hb, hc_=hc, qc_=qc: (lambda e: e.tensor_copy(
                qstg[p_][hb_:hb_ + 64, :],
                QT[hb_:hb_ + 64, S * hc_ + 512 * qc_:S * hc_ + 512 * (qc_ + 1)])))(),
                 waits=[rope_evt[("q", hc)]], inc="v")
            qs_evt[ji] = P.now("v")
        pr = cur_pair[0]
        hf = len(pend_s)
        chunk_slot[cj] = (pr, hf, pair_no[0] % 4)
        P.op("tensor", (lambda p_=pr, f_=hf, kc_=kc, hc_=hc, qp_=qc % 2: (lambda e: e.matmul(
            ps[p_][:, 512 * f_:512 * (f_ + 1)],
            KT[0:128, S * hc_ + 128 * kc_:S * hc_ + 128 * (kc_ + 1)],
            qstg[qp_][:], start=True, stop=True)))(),
             waits=[rope_evt[("k", hc)], qs_evt[ji],
                    ps_free[pr] if hf == 0 else None], inc="t")
        pend_s.append((cj, P.now("t")))
        if last:
            qs_free[qc % 2] = P.now("t")
        if hf == 1 or last:
            flush_pair()

    def cidx_of(cjx):
        jx, hx, qx, kx, _f, _l = chunks[cjx]
        d = 4 * qx - kx
        return CP_M12 if kx >= 4 * qx else CP_FAR0 + hx * 12 + (d - 1)

    def dsel_of(cjx):
        jx, hx, qx, kx, _f, _l = chunks[cjx]
        return ddg[hx % 2][kx - 4 * qx] if kx >= 4 * qx else dfar[hx % 2]

    def flush_pair():
        if not pend_s:
            return
        items = list(pend_s)
        del pend_s[:]
        cur_pair[0] ^= 1
        pair_no[0] += 1
        cj0 = items[0][0]
        pr = chunk_slot[cj0][0]
        n = len(items)
        h = chunks[cj0][1]
        idxs = [cidx_of(c[0]) for c in items]
        evs = [c[1] for c in items]
        if n == 2 and idxs[0] == idxs[1]:
            P.op("scalar", (lambda p_=pr, ci=idxs[0]: (lambda e: e.activation(
                ebuf[p_][:], ps[p_][:], AF.Exp,
                bias=cpool[:, ci:ci + 1], scale=1.0)))(),
                 waits=evs + [ev_cpool, eb_free[pr], dtile_evt[h]], inc="s")
        else:
            for (cjx, evx), ci in zip(items, idxs):
                f = chunk_slot[cjx][1]
                P.op("scalar", (lambda p_=pr, f_=f, ci_=ci: (lambda e: e.activation(
                    ebuf[p_][:, 512 * f_:512 * (f_ + 1)],
                    ps[p_][:, 512 * f_:512 * (f_ + 1)], AF.Exp,
                    bias=cpool[:, ci_:ci_ + 1], scale=1.0)))(),
                     waits=[evx, ev_cpool,
                            eb_free[pr] if cjx == cj0 else None,
                            dtile_evt[h]], inc="s")
        ps_free[pr] = P.now("s")
        ev_e = P.now("s")
        pbi = chunk_slot[cj0][2]
        for (cjx, _e) in items:
            f = chunk_slot[cjx][1]
            dd = dsel_of(cjx)
            P.op("vector", (lambda p_=pr, b_=pbi, f_=f, d_=dd: (lambda e: e.tensor_mul(
                pbuf[b_][:, 512 * f_:512 * (f_ + 1)],
                ebuf[p_][:, 512 * f_:512 * (f_ + 1)], d_[:])))(),
                 waits=[ev_e] + ([pb_free[pbi]] if cjx == cj0 else []), inc="v")
        eb_free[pr] = P.now("v")
        for (cjx, _e) in items:
            s1evt[cjx] = P.now("v")
        dtile_lastuse[h] = P.now("v")

    P.op("vector", lambda e: e.memset(rrow[64:66, :], 1.0), inc="v")
    ev_sq65 = P.now("v")

    s2count = [0]
    t1q = []
    t2q = []

    def stage2(cj):
        ji, h, qc, kc, first, last = chunks[cj]
        pr, hf, pbi = chunk_slot[cj]
        pob = ji % 2
        if first:
            drain_parity(pob)
        P.op("tensor", (lambda b_=pbi, f2=hf, kc_=kc, h_=h, pb_=pob, f_=first, l_=last: (lambda e: e.matmul(
            po[pb_][0:65, :],
            Vx[:, 1040 * kc_ + 65 * h_:1040 * kc_ + 65 * (h_ + 1)],
            pbuf[b_][:, 512 * f2:512 * (f2 + 1)], start=f_, stop=l_)))(),
             waits=[s1evt[cj], ev_Vx] + ([po_free[pob]] if first else []), inc="t")
        if chunks[cj][5] or hf == 1:
            pb_free[pbi] = P.now("t")
        s2count[0] += 1
        if last:
            ev_av = P.now("t")
            P.op("vector", (lambda p_=pob: (lambda e: e.tensor_copy(
                rrow[64:65, 512 * p_:512 * (p_ + 1)], po[p_][64:65, :])))(),
                 waits=[ev_av, ev_vconst, drow_evt[pob], rrow_evt[pob]], inc="v")
            t1q.append((s2count[0] + 2, ji, P.now("v")))

    def drain_parity(pob):
        for q, fn in ((t1q, _t1_emit), (t2q, _t2_emit)):
            keep = []
            for item in q:
                if item[1] % 2 == pob:
                    fn(item)
                else:
                    keep.append(item)
            q[:] = keep

    def _t1_emit(item):
        _, ji, ev_dc = item
        _emit_t1(ji, ev_dc)

    def _t2_emit(item):
        _, ji, ev_rr = item
        _emit_t2(ji, ev_rr)

    def _emit_t1(ji, ev_dc):
        pob = ji % 2
        P.op("scalar", (lambda p_=pob: (lambda e: e.activation(
            rrow[64:66, 512 * p_:512 * (p_ + 1)],
            rrow[64:66, 512 * p_:512 * (p_ + 1)],
            AF.Ln, bias=0.0, scale=1.0)))(),
             waits=[ev_dc, rrow_evt[pob]], inc="s")
        drow_evt[pob] = P.now("s")
        P.op("scalar", (lambda p_=pob: (lambda e: e.activation(
            rrow[64:66, 512 * p_:512 * (p_ + 1)],
            rrow[64:66, 512 * p_:512 * (p_ + 1)],
            AF.Exp, bias=0.0, scale=-1.0)))(), inc="s")
        t2q.append((s2count[0] + 4, ji, P.now("s")))

    def _emit_t2(ji, ev_rr):
        h, qc, kcs = jobs[ji]
        pob = ji % 2
        P.op("tensor", (lambda p_=pob: (lambda e: e.matmul(
            pc[:], onesF[64:65, 0:128],
            rrow[64:65, 512 * p_:512 * (p_ + 1)], start=True, stop=True)))(),
             waits=[ev_rr, ev_vconst, pc_evt[0]], inc="t")
        ev_bc = P.now("t")
        rrow_evt[pob] = ev_bc
        P.op("vector", (lambda: (lambda e: e.tensor_copy(rbt[0][:], pc[:])))(),
             waits=[ev_bc, rb_evt[0]], inc="v")
        pc_evt[0] = P.now("v")
        P.op("vector", (lambda h_=h, q_=qc, p_=pob: (lambda e: e.tensor_mul(
            Ost[0:64, S * h_ + 512 * q_:S * h_ + 512 * (q_ + 1)],
            po[p_][0:64, :], rbt[0][0:64, :])))(), inc="v")
        ev = P.now("v")
        po_free[pob] = ev
        rb_evt[0] = ev
        ost_evt[(h, qc)] = ev

    def drain_tails(now, flush=False):
        while t1q and (flush or t1q[0][0] <= now):
            _, ji, ev_dc = t1q.pop(0)
            _emit_t1(ji, ev_dc)
        while t2q and (flush or t2q[0][0] <= now):
            _, ji, ev_rr = t2q.pop(0)
            _emit_t2(ji, ev_rr)

    drow_evt = [None, None]
    rrow_evt = [None, None]
    pc_evt = [None]
    rb_evt = [None]

    for j in range(NCK + LOOK):
        if j >= LOOK:
            stage2(j - LOOK)
            drain_tails(s2count[0])
        if j < NCK:
            stage1(j)
    drain_tails(0, flush=True)

    ev_attn = P.now("v")
    if phase_limit <= 4:
        dump_dbg(Ost[0:64, 0:16384], 16384, ev_attn, rows=64)
        return finish()

    p3p_es.close()
    phase_stacks.remove(p3p_es)
    es_qkv.close()
    phase_stacks.remove(es_qkv)
    P.barrier()
    p3b_es = ExitStack()
    phase_stacks.append(p3b_es)
    owt = p3b_es.enter_context(nc.sbuf_tensor("owt", [64, NH * 1024], BF16, side="right"))
    xrb = [p3b_es.enter_context(nc.sbuf_tensor(f"xrb{i}", [128, D], F32, side="right"))
           for i in range(2)]
    xnw = [p3b_es.enter_context(nc.sbuf_tensor(f"xnw{i}", [128, D], F32, side="right"))
           for i in range(2)]
    btm = [p3b_es.enter_context(nc.sbuf_tensor(f"btm{i}", [128, D], BF16, side="right"))
           for i in range(2)]
    btasm = [p3b_es.enter_context(nc.sbuf_tensor(f"btasm{i}", [128, D], BF16, side="right"))
             for i in range(2)]
    pat = [p3b_es.enter_context(nc.psum_tensor(f"pat{i}", [128, 512], F32)) for i in range(4)]
    pbt3A = p3b_es.enter_context(nc.psum_tensor("pbt3A", [128, 1024], BF16))
    pbt3B = p3b_es.enter_context(nc.psum_tensor("pbt3B", [128, 1024], BF16))

    for h in range(NH):
        P.op("gpsimd", (lambda h_=h: (lambda e: e.dma_start(
            out=owt[0:64, 1024 * h_:1024 * (h_ + 1)],
            in_=owT_d[64 * h_:64 * (h_ + 1), :])))(), inc="dow")
    ev_owt = P.now("dow")

    xrb_free = [None, None]
    xnw_free = [[], []]
    pat_free = [None] * 4
    btm_free = [None, None]
    btasm_free = [None, None]
    bst = {}
    bt_pend = [None]

    def emit_bt(sl, bi, ev_b):
        wts0 = [ev_b] + ([btasm_free[bi]] if btasm_free[bi] else [])
        ev_t, ev_bt = emit_transposes(
            btm[bi], NCH, (lambda b_=bi: (lambda i: btasm[b_][:, 128 * i:128 * (i + 1)]))(),
            wts0, (pbt3A, pbt3B), bst)
        btm_free[bi] = ev_t
        P.op("scalar", (lambda s_=sl, b_=bi: (lambda e: e.dma_start(
            out=btscr_d[s_], in_=btasm[b_][:].rearrange("p (a b) -> p a b", a=8))))(),
             waits=[ev_bt], inc=f"dbt{bi}")
        btasm_free[bi] = P.now(f"dbt{bi}")

    for sl in range(NSL):
        bi = sl % 2
        P.op("sync", (lambda s_=sl, b=bi: (lambda e: e.dma_start(
            out=xrb[b][:], in_=x_d[128 * s_:128 * (s_ + 1), :])))(),
             waits=[xrb_free[bi]], inc=f"dxr{bi}")
        ev_xr = P.now(f"dxr{bi}")
        qcsl = sl // 4
        for e2 in range(2):
            pi = (sl % 2) * 2 + e2
            wts = [ev_owt, ev_browb, pat_free[pi]] + [ost_evt[(h, qcsl)] for h in range(NH)]
            for h in range(NH):
                P.op("tensor", (lambda h_=h, e_=e2, s_=sl, p_=pi: (lambda e: e.matmul(
                    pat[p_][:],
                    Ost[0:64, S * h_ + 128 * s_:S * h_ + 128 * (s_ + 1)],
                    owt[0:64, 1024 * h_ + 512 * e_:1024 * h_ + 512 * (e_ + 1)],
                    start=(h_ == 0), stop=False)))(),
                     waits=wts if h == 0 else (), inc="t")
            P.op("tensor", (lambda e_=e2, p_=pi: (lambda e: e.matmul(
                pat[p_][:], onesB[0:1, 0:128], browb[0:1, 512 * e_:512 * (e_ + 1)],
                start=False, stop=True)))(), inc="t")
        if bt_pend[0] is not None:
            emit_bt(*bt_pend[0])
            bt_pend[0] = None
        ev_op = P.now("t")
        for e2 in range(2):
            pi = (sl % 2) * 2 + e2
            P.op("vector", (lambda b=bi, e_=e2, p_=pi: (lambda e: e.tensor_add(
                xnw[b][:, 512 * e_:512 * (e_ + 1)], pat[p_][:],
                xrb[b][:, 512 * e_:512 * (e_ + 1)])))(),
                 waits=[ev_op, ev_xr] + (xnw_free[bi] if e2 == 0 else []), inc="v")
            pat_free[pi] = P.now("v")
        ev_xn = P.now("v")
        xrb_free[bi] = ev_xn
        P.op("scalar", (lambda s_=sl, b=bi: (lambda e: e.dma_start(
            out=xres_d[128 * s_:128 * (s_ + 1), :], in_=xnw[b][:])))(),
             waits=[ev_xn], inc=f"dsp{bi}")
        ev_spill = P.now(f"dsp{bi}")
        ev_b = emit_ln(xnw[bi], btm[bi], [ev_xn, btm_free[bi]], 8 * bi)
        xnw_free[bi] = [ev_spill, ev_b]
        bt_pend[0] = (sl, bi, ev_b)
    if bt_pend[0] is not None:
        emit_bt(*bt_pend[0])
        bt_pend[0] = None

    ev_spills = [P.now("dsp0"), P.now("dsp1")]
    ev_btspill = [P.now("dbt0"), P.now("dbt1")]

    if phase_limit <= 5:
        return finish()

    p3b_es.close()
    phase_stacks.remove(p3b_es)
    p3_es.close()
    phase_stacks.remove(p3_es)
    P.barrier()
    p4_es = ExitStack()
    phase_stacks.append(p4_es)
    SB2 = 256
    NB2 = S // SB2
    Hb2 = [p4_es.enter_context(nc.sbuf_tensor(f"Hb2{i}", [128, 128 * SB2], BF16))
           for i in range(2)]
    btb = [p4_es.enter_context(nc.sbuf_tensor(f"btb{i}", [128, 8 * SB2], BF16))
           for i in range(2)]
    w2b = [p4_es.enter_context(nc.sbuf_tensor(f"w2b{i}", [128, 512], BF16)) for i in range(2)]
    wcs = [p4_es.enter_context(nc.sbuf_tensor(f"wcs{i}", [128, D], BF16)) for i in range(10)]
    xr2 = [p4_es.enter_context(nc.sbuf_tensor(f"xr2{i}", [128, D], F32)) for i in range(2)]
    ob2 = [p4_es.enter_context(nc.sbuf_tensor(f"ob2{i}", [128, D], F32)) for i in range(2)]
    pf = [p4_es.enter_context(nc.psum_tensor(f"pf{i}", [128, 512], F32)) for i in range(8)]

    wcs_free = [None] * 10
    w2b_free = [None, None]
    xr2_free = [None, None]
    ob2_free = [None, None]
    pf_free = [None] * 8
    btb_free = [None, None]
    hb2_free = [None, None]
    rails_done = {}
    cp_done = {-1: None}

    def load_btb(bk):
        if bk >= NB2:
            return
        p = bk % 2
        for oc in range(8):
            P.op("sync", (lambda o_=oc, b_=bk, p_=p: (lambda e: e.dma_start(
                out=btb[p_][:, SB2 * o_:SB2 * (o_ + 1)].rearrange("p (a b) -> p a b", a=2),
                in_=btscr_d[2 * b_:2 * (b_ + 1), :, o_, :].rearrange("a p b -> p a b"))))(),
                 waits=(ev_btspill + [btb_free[p]]) if oc == 0 else (), inc=f"dtb{p}")
        btb_free[p] = None
        return P.now(f"dtb{p}")

    btb_evt = {}
    btb_evt[0] = load_btb(0)

    def rails_unit(bk, u, standalone=False):
        p = bk % 2
        h4, rem = divmod(u, 32)
        oc, j = divmod(rem, 4)
        i = (u // 4) % 2
        if j == 0:
            P.op("sync", (lambda h_=h4, o_=oc, i_=i: (lambda e: e.dma_start(
                out=w2b[i_][:], in_=w2scr_d[h_, o_])))(),
                 waits=[ev_w2s[0], ev_w2s[1], w2b_free[i]], inc=f"dw2{i}")
        ev_wl = P.now(f"dw2{i}")
        zb = 4 + (u % 4)
        mwts = [ev_wl, btb_evt[bk], pf_free[zb]]
        P.op("tensor", (lambda i_=i, j_=j, o_=oc, z_=zb, p_=p: (lambda e: e.matmul(
            pf[z_][:, 0:SB2], w2b[i_][:, 128 * j_:128 * (j_ + 1)],
            btb[p_][:, SB2 * o_:SB2 * (o_ + 1)], start=True, stop=True)))(),
             waits=mwts, inc="t")
        ev_z = P.now("t")
        if j == 3:
            w2b_free[i] = ev_z
        if u == 127:
            btb_free[p] = ev_z
        mc = u
        gw = [ev_z]
        if u == 0:
            gw.append(hb2_free[p])
        P.op("scalar", (lambda m_=mc, z_=zb, p_=p: (lambda e: e.activation(
            Hb2[p_][:, SB2 * m_:SB2 * (m_ + 1)], pf[z_][:, 0:SB2], AF.Gelu,
            bias=0.0, scale=1.0)))(),
             waits=gw, inc="s")
        pf_free[zb] = P.now("s")
        if u == 127:
            rails_done[bk] = P.now("s")

    btb_evt[1] = load_btb(1)
    for u in range(128):
        rails_unit(0, u)

    for bk in range(NB2):
        p = bk % 2
        if bk + 2 < NB2:
            btb_evt[bk + 2] = load_btb(bk + 2)
        xr_evts = {}
        for sl2 in (1, 0):
            q4 = sl2
            sl = 2 * bk + sl2
            P.op("sync", (lambda s_=sl, q_=q4: (lambda e: e.dma_start(
                out=xr2[q_][:], in_=xres_d[128 * s_:128 * (s_ + 1), :])))(),
                 waits=[xr2_free[q4], ev_spills[0], ev_spills[1]], inc=f"dr2{q4}")
            xr_evts[sl2] = P.now(f"dr2{q4}")
        for mc in range(128):
            i = mc % 10
            P.op("sync", (lambda m_=mc, i_=i: (lambda e: e.dma_start(
                out=wcs[i_][:], in_=wcb_d[128 * m_:128 * (m_ + 1), :])))(),
                 waits=[wcs_free[i]] + (ev_wcb_done if (bk == 0 and mc < 10) else []),
                 inc=f"dwc{i}")
            ev_wc = P.now(f"dwc{i}")
            for sl2 in range(2):
                for e2 in range(2):
                    pb_i = sl2 * 2 + e2
                    mwts = [ev_wc, rails_done[bk]]
                    if mc == 0:
                        mwts.append(pf_free[pb_i])
                    P.op("tensor", (lambda m_=mc, s_=sl2, e_=e2, pi=pb_i, i_=i, p_=p: (lambda e: e.matmul(
                        pf[pi][:], Hb2[p_][:, SB2 * m_ + 128 * s_:SB2 * m_ + 128 * (s_ + 1)],
                        wcs[i_][:, 512 * e_:512 * (e_ + 1)],
                        start=(m_ == 0), stop=(m_ == 127))))(),
                         waits=mwts, inc="t")
            wcs_free[i] = P.now("t")
            if bk + 1 < NB2:
                rails_unit(bk + 1, mc)
        ev_cp = P.now("t")
        cp_done[bk] = ev_cp
        hb2_free[(bk + 2) % 2] = ev_cp
        for sl2 in (1, 0):
            sl = 2 * bk + sl2
            q4 = sl2
            ev_xr = xr_evts[sl2]
            for e2 in range(2):
                pb_i = sl2 * 2 + e2
                P.op("vector", (lambda q_=q4, e_=e2, pi=pb_i: (lambda e: e.tensor_add(
                    ob2[q_][:, 512 * e_:512 * (e_ + 1)], pf[pi][:],
                    xr2[q_][:, 512 * e_:512 * (e_ + 1)])))(),
                     waits=[ev_cp, ev_xr] + ([ob2_free[q4]] if e2 == 0 else []), inc="v")
                pf_free[pb_i] = P.now("v")
            ev_ot = P.now("v")
            xr2_free[q4] = ev_ot
            P.op("scalar", (lambda s_=sl, q_=q4: (lambda e: e.dma_start(
                out=out_d[128 * s_:128 * (s_ + 1), :], in_=ob2[q_][:])))(),
                 waits=[ev_ot], inc=f"dot{q4}")
            ob2_free[q4] = P.now(f"dot{q4}")

    return finish()



_NC_CACHE = {}


def _get_nc(phase_limit=99):
    if phase_limit not in _NC_CACHE:
        _NC_CACHE[phase_limit] = build_nc(phase_limit)
    return _NC_CACHE[phase_limit]


def kernel(L0_attn_x, L0_attn_wq, L0_attn_wk, L0_attn_wv, L0_attn_out_w,
           L0_attn_out_b, L0_ffn_rail_maker, L0_ffn_rail_up_proj, L0_ffn_w_cproj,
           _phase_limit=99, _cores=8):
    x = np.ascontiguousarray(np.asarray(L0_attn_x, dtype=np.float32))
    wq = np.asarray(L0_attn_wq, dtype=np.float32)
    wk = np.asarray(L0_attn_wk, dtype=np.float32)
    wv = np.asarray(L0_attn_wv, dtype=np.float32)
    ow = np.asarray(L0_attn_out_w, dtype=np.float32)
    ob = np.ascontiguousarray(np.asarray(L0_attn_out_b, dtype=np.float32).reshape(1, D))
    rm = np.asarray(L0_ffn_rail_maker, dtype=np.float32)
    ru = np.asarray(L0_ffn_rail_up_proj, dtype=np.float32)
    wc = np.asarray(L0_ffn_w_cproj, dtype=np.float32)

    wqT = np.ascontiguousarray(wq.T)
    wkT = np.ascontiguousarray(wk.T)
    wvT = np.ascontiguousarray(wv.T)
    owT = np.ascontiguousarray(ow.T)
    wcT = np.ascontiguousarray(wc.T)

    blkA = np.zeros((4, 8, 128, 128), dtype=np.float32)
    blkB = np.zeros((4, 8, 128, 512), dtype=np.float32)
    for h in range(4):
        for oc in range(8):
            for r in range(8):
                Mb = rm[h, 8 * oc + r]
                Ub = ru[h, 8 * oc + r]
                blkA[h, oc, 16 * r:16 * (r + 1), 16 * r:16 * (r + 1)] = Mb.T
                blkB[h, oc, 16 * r:16 * (r + 1), 64 * r:64 * (r + 1)] = Ub.T

    nc = _get_nc(_phase_limit)
    shared = dict(wqT=wqT, wkT=wkT, wvT=wvT, owT=owT, ob=ob,
                  blkA=blkA, blkB=blkB, wcT=wcT)
    in_maps = [dict(shared, x=np.ascontiguousarray(x[i])) for i in range(_cores)]
    res = run_bass_kernel_spmd(nc, in_maps, core_ids=list(range(_cores)))
    if _phase_limit < 99:
        return res
    return np.stack([np.asarray(r["out"]) for r in res.results], axis=0)


# version 19
# speedup vs baseline: 1.4462x; 1.0681x over previous
import math
from contextlib import ExitStack

import numpy as np

import concourse.bass as bass
import concourse.mybir as mybir
from concourse.bass_utils import run_bass_kernel_spmd

F32 = mybir.dt.float32
BF16 = mybir.dt.bfloat16
I32 = mybir.dt.int32
AF = mybir.ActivationFunctionType
OP = mybir.AluOpType
AX = mybir.AxisListType

S = 2048
D = 1024
NH = 16
NSL = S // 128
NCH = D // 128
MF = 16384
SBK = 512
NBK = S // SBK
EPS = 1e-5
MSUB = 12.0
TWO_PI = 6.283185307179586
HALF_PI = 1.5707963267948966
CLN = math.log(10000.0) / 512.0

ENGINES = ("sync", "gpsimd", "tensor", "vector", "scalar")


def slope(h):
    return 2.0 ** (-8.0 * (h + 1) / 16.0)


def kc_min_for(h, qc):
    lo = 0
    for d in range(1, 13):
        if -MSUB + slope(h) * (127.0 - 128.0 * d) <= -124.0:
            lo = max(lo, 4 * qc - d + 1)
    return max(0, lo)


class Prog:
    def __init__(self, nc, es):
        self.nc = nc
        self.es = es
        self.ops = {e: [] for e in ENGINES}
        self.sems = {}
        self.cnt = {}
        for name in ("t", "v", "s", "g"):
            self._mk(name)

    def _mk(self, name):
        self.sems[name] = self.es.enter_context(self.nc.semaphore(name))
        self.cnt[name] = 0

    ENG_SEM = {"vector": "v", "scalar": "s", "gpsimd": "g"}

    gate = {}

    def op(self, eng, fn, waits=(), inc=None):
        gw = self.gate.pop(eng, None) if self.gate else None
        if gw:
            waits = list(waits) + gw
        esem = self.ENG_SEM.get(eng)
        pre_self = self.cnt.get(esem, 0) if esem is not None else 0
        incs = []
        is_dma = False
        if inc is not None:
            for snm in ([inc] if isinstance(inc, str) else list(inc)):
                if snm not in self.sems:
                    self._mk(snm)
                n = 16 if snm.startswith("d") else 1
                if snm.startswith("d"):
                    is_dma = True
                self.cnt[snm] += n
                incs.append((snm, n))
        flat = []
        for w in waits:
            if w is None:
                continue
            if isinstance(w, list):
                flat.extend(x for x in w if x is not None)
            else:
                flat.append(w)
        if esem is not None:
            flat.append((esem, pre_self))
            if not is_dma and not any(s == esem for (s, _n) in incs):
                self.cnt[esem] += 1
                incs.append((esem, 1))
        flat = tuple((s, v) for (s, v) in flat if v > 0)
        self.ops[eng].append((fn, flat, incs))
        return (incs[0][0], self.cnt[incs[0][0]]) if incs else None

    def now(self, sem):
        if sem not in self.sems:
            self._mk(sem)
        return (sem, self.cnt[sem])

    def barrier(self):
        gate = [self.now(x) for x in list(self.sems.keys())]
        self.gate = {e: list(gate) for e in ENGINES}

    def build(self, blk):
        sems = self.sems
        for ename in ENGINES:
            eng_ops = self.ops[ename]
            if not eng_ops:
                continue

            def make(ops_):
                def body(eng):
                    lw = {}
                    for fn, waits, incs in ops_:
                        for (snm, v) in waits:
                            if lw.get(snm, 0) < v:
                                eng.wait_ge(sems[snm], v)
                                lw[snm] = v
                        ins = fn(eng)
                        for (snm, n) in incs:
                            ins = ins.then_inc(sems[snm], n)
                return body

            getattr(blk, ename)(make(eng_ops))


def build_nc(phase_limit=99):
    nc = bass.Bass("TRN2", target_bir_lowering=False)

    x_d = nc.dram_tensor("x", [S, D], F32, kind="ExternalInput").ap()
    wqT_d = nc.dram_tensor("wqT", [D, D], F32, kind="ExternalInput").ap()
    wkT_d = nc.dram_tensor("wkT", [D, D], F32, kind="ExternalInput").ap()
    wvT_d = nc.dram_tensor("wvT", [D, D], F32, kind="ExternalInput").ap()
    owT_d = nc.dram_tensor("owT", [D, D], F32, kind="ExternalInput").ap()
    ob_d = nc.dram_tensor("ob", [1, D], F32, kind="ExternalInput").ap()
    blkA_d = nc.dram_tensor("blkA", [4, 8, 128, 128], F32, kind="ExternalInput").ap()
    blkB_d = nc.dram_tensor("blkB", [4, 8, 128, 512], F32, kind="ExternalInput").ap()
    wcT_d = nc.dram_tensor("wcT", [MF, D], F32, kind="ExternalInput").ap()
    out_d = nc.dram_tensor("out", [S, D], F32, kind="ExternalOutput").ap()
    xres_d = nc.dram_tensor("xres", [S, D], F32, kind="Internal").ap()
    btscr_d = nc.dram_tensor("btscr", [NSL, 128, 8, 128], BF16, kind="Internal").ap()
    wcb_d = nc.dram_tensor("wcb", [MF, D], BF16, kind="Internal").ap()
    w2scr_d = nc.dram_tensor("w2scr", [4, 8, 128, 512], BF16, kind="Internal").ap()

    es = ExitStack()
    P = Prog(nc, es)

    cpool = es.enter_context(nc.sbuf_tensor("cpool", [128, 200], F32))
    ident = es.enter_context(nc.sbuf_tensor("ident", [128, 128], BF16))
    onesF = es.enter_context(nc.sbuf_tensor("onesF", [128, 128], F32))
    onesB = es.enter_context(nc.sbuf_tensor("onesB", [128, 128], BF16))
    xfar = es.enter_context(nc.sbuf_tensor("xfar", [128, 512], BF16))
    xdg = [es.enter_context(nc.sbuf_tensor(f"xdg{i}", [128, 512], BF16)) for i in range(4)]
    rrow = es.enter_context(nc.sbuf_tensor("rrow", [128, 1024], F32))
    sqd = es.enter_context(nc.sbuf_tensor("sqd", [128, 1024], F32))
    lnst = es.enter_context(nc.sbuf_tensor("lnst", [128, 16], F32))
    browb = es.enter_context(nc.sbuf_tensor("browb", [1, D], BF16))

    CP_FAR0 = 0
    CP_M12 = 192
    CP_EPS = 193
    CP_FRQ0 = 194
    cpool_vals = {}
    for h in range(NH):
        for d in range(1, 13):
            cpool_vals[CP_FAR0 + h * 12 + (d - 1)] = -MSUB + slope(h) * (127.0 - 128.0 * d)
    cpool_vals[CP_M12] = -MSUB
    cpool_vals[CP_EPS] = EPS
    for c in range(4):
        cpool_vals[CP_FRQ0 + c] = -128.0 * c * CLN

    items = sorted(cpool_vals.items())
    for j, (idx, val) in enumerate(items):
        P.op("vector", (lambda i=idx, vv=val: (lambda e: e.memset(cpool[:, i:i + 1], vv)))(),
             inc="v" if j == len(items) - 1 else None)
    ev_cpool = P.now("v")
    P.op("vector", lambda e: e.memset(onesF[:], 1.0))
    P.op("vector", lambda e: e.memset(onesB[:], 1.0))
    P.op("vector", lambda e: e.memset(sqd[:], 1.0))
    P.op("vector", lambda e: e.memset(rrow[:], 1.0), inc="v")
    ev_vconst = P.now("v")

    P.op("gpsimd", lambda e: e.affine_select(ident[:], onesB[:], [[-1, 128]],
                                             OP.is_equal, 0.0, base=0,
                                             channel_multiplier=1),
         waits=[ev_vconst], inc="g")
    P.op("gpsimd", lambda e: e.iota(xfar[:], [[-1, 512]], base=-127,
                                    channel_multiplier=1,
                                    allow_small_or_imprecise_dtypes=True), inc="g")
    for sf in range(4):
        P.op("gpsimd", (lambda s_=sf: (lambda e: e.iota(
            xdg[s_][:], [[-1, 512]], base=128 * s_, channel_multiplier=1,
            allow_small_or_imprecise_dtypes=True)))(), inc="g")
    ev_giota = P.now("g")
    for sf in range(4):
        P.op("gpsimd", (lambda s_=sf: (lambda e: e.affine_select(
            xdg[s_][:], xdg[s_][:], [[1, 512]], OP.is_ge, -1.0e5,
            base=-128 * s_, channel_multiplier=-1)))(),
             waits=[ev_giota], inc="g")
    ev_gconst = P.now("g")

    P.op("gpsimd", lambda e: e.dma_start(out=browb[:], in_=ob_d), inc="dbr")
    ev_browb = P.now("dbr")

    dbg_d = None
    if phase_limit < 99:
        dbg_d = nc.dram_tensor("dbg", [128, 16384], F32, kind="ExternalOutput").ap()

    def dump_dbg(src_ap, cols, prior_ev, rows=128):
        if src_ap.dtype == F32:
            P.op("scalar", lambda e: e.dma_start(out=dbg_d[0:rows, 0:cols], in_=src_ap),
                 waits=[prior_ev], inc="ddb")
        else:
            n = 0
            last_st = None
            while n < cols:
                wdt = min(1024, cols - n)
                P.op("vector", (lambda nn=n, ww=wdt, sp=src_ap, r_=rows: (lambda e: e.tensor_copy(
                    sqd[0:r_, 0:ww], sp[:, nn:nn + ww])))(), waits=[prior_ev, last_st], inc="v")
                evc = P.now("v")
                P.op("scalar", (lambda nn=n, ww=wdt, r_=rows: (lambda e: e.dma_start(
                    out=dbg_d[0:r_, nn:nn + ww], in_=sqd[0:r_, 0:ww])))(),
                     waits=[evc], inc="ddb")
                last_st = P.now("ddb")
                n += wdt

    phase_stacks = []

    def finish():
        for snm in list(P.sems.keys()):
            if snm.startswith("d"):
                tot = P.cnt[snm]
                if tot > 0:
                    P.op("sync", (lambda s_=snm, t_=tot: (lambda e: e.wait_ge(
                        P.sems[s_], t_)))())
        for st in reversed(phase_stacks):
            st.close()
        blk = es.enter_context(nc.Block())
        P.build(blk)
        es.close()
        return nc

    def emit_ln(xtile, otile, waits0, col0):
        c = col0
        P.op("vector", lambda e: e.reduce_sum(lnst[:, c + 0:c + 1], xtile[:], axis=AX.X),
             waits=list(waits0), inc="v")
        ev1 = P.now("v")
        P.op("scalar", lambda e: e.activation(lnst[:, c + 1:c + 2], lnst[:, c + 0:c + 1],
                                              AF.Copy, bias=0.0, scale=-1.0 / D),
             waits=[ev1], inc="s")
        P.op("scalar", lambda e: e.activation(sqd[:], xtile[:], AF.Square,
                                              bias=lnst[:, c + 1:c + 2], scale=1.0,
                                              accum_out=lnst[:, c + 2:c + 3]), inc="s")
        P.op("scalar", lambda e: e.activation(lnst[:, c + 3:c + 4], lnst[:, c + 2:c + 3],
                                              AF.Ln, bias=cpool[:, CP_EPS:CP_EPS + 1],
                                              scale=1.0 / D),
             waits=[ev_cpool], inc="s")
        P.op("scalar", lambda e: e.activation(lnst[:, c + 4:c + 5], lnst[:, c + 3:c + 4],
                                              AF.Exp, bias=0.0, scale=-0.5), inc="s")
        ev5 = P.now("s")
        P.op("vector", lambda e: e.tensor_mul(lnst[:, c + 5:c + 6], lnst[:, c + 1:c + 2],
                                              lnst[:, c + 4:c + 5]),
             waits=[ev5], inc="v")
        ev6 = P.now("v")
        P.op("scalar", lambda e: e.activation(otile[:], xtile[:], AF.Identity,
                                              bias=lnst[:, c + 5:c + 6],
                                              scale=lnst[:, c + 4:c + 5]),
             waits=[ev6], inc="s")
        return P.now("s")

    def emit_transposes(src_tile, n_slices, dst_fn, waits0, pair, state):
        evs = state.setdefault("evs", [])
        for i in range(n_slices):
            k = state.get("k", 0)
            bank = pair[k % 2]
            slot = (k // 2) % 8
            wts = list(waits0) + [ev_gconst]
            if len(evs) >= 2:
                wts.append(evs[-2])
            if len(evs) >= 16:
                wts.append(evs[-16])
            P.op("tensor", (lambda b_=bank, sl_=slot, i_=i, sp=src_tile: (lambda e: e.transpose(
                b_[:, 128 * sl_:128 * (sl_ + 1)],
                sp[:, 128 * i_:128 * (i_ + 1)], ident[:])))(),
                 waits=wts, inc="t")
            ev_tr = P.now("t")
            dst = dst_fn(i)
            P.op("vector", (lambda b_=bank, sl_=slot, d_=dst: (lambda e: e.tensor_copy(
                d_, b_[:, 128 * sl_:128 * (sl_ + 1)])))(),
                 waits=[ev_tr], inc="v")
            evs.append(P.now("v"))
            state["k"] = k + 1
        return P.now("t"), P.now("v")

    es_at = ExitStack()
    phase_stacks.append(es_at)
    At = es_at.enter_context(nc.sbuf_tensor("At", [128, S * NCH], BF16, side="right"))
    p1_es = ExitStack()
    phase_stacks.append(p1_es)
    xin = [p1_es.enter_context(nc.sbuf_tensor(f"xin{i}", [128, D], F32, side="right"))
           for i in range(2)]
    atm = [p1_es.enter_context(nc.sbuf_tensor(f"atm{i}", [128, D], BF16, side="right"))
           for i in range(2)]
    pbtA = p1_es.enter_context(nc.psum_tensor("pbtA", [128, 1024], BF16))
    pbtB = p1_es.enter_context(nc.psum_tensor("pbtB", [128, 1024], BF16))

    xin_free = [None, None]
    atm_free = [None, None]
    tst = {}
    for sl in range(NSL):
        bi = sl % 2
        P.op("sync", (lambda s_=sl, b=bi: (lambda e: e.dma_start(
            out=xin[b][:], in_=x_d[128 * s_:128 * (s_ + 1), :])))(),
             waits=[xin_free[bi]], inc=f"dx{bi}")
        ev_x = P.now(f"dx{bi}")
        ev_a = emit_ln(xin[bi], atm[bi], [ev_x, atm_free[bi]], 8 * bi)
        xin_free[bi] = ev_a
        ev_t, ev_v = emit_transposes(
            atm[bi], NCH,
            (lambda s_=sl: (lambda i: At[:, S * i + 128 * s_: S * i + 128 * (s_ + 1)]))(),
            [ev_a], (pbtA, pbtB), tst)
        atm_free[bi] = ev_t
    ev_At = P.now("v")

    if phase_limit <= 1:
        dump_dbg(At[:, 0:16384], 16384, ev_At)
        return finish()

    p1_es.close()
    phase_stacks.remove(p1_es)
    P.barrier()
    es_qkv = ExitStack()
    phase_stacks.append(es_qkv)
    QT = es_qkv.enter_context(nc.sbuf_tensor("QT", [128, S * NCH], BF16))
    KT = es_qkv.enter_context(nc.sbuf_tensor("KT", [128, S * NCH], BF16))
    Vx = es_qkv.enter_context(nc.sbuf_tensor("Vx", [128, NSL * 1040], BF16))

    p2_es = ExitStack()
    phase_stacks.append(p2_es)
    iotaS = p2_es.enter_context(nc.sbuf_tensor("iotaS", [128, S], F32, side="right"))
    scrW = p2_es.enter_context(nc.sbuf_tensor("scrW", [128, S], F32, side="right"))
    scrI = p2_es.enter_context(nc.sbuf_tensor("scrI", [128, S // 4], I32, side="right"))
    scrB = p2_es.enter_context(nc.sbuf_tensor("scrB", [128, S // 4], F32, side="right"))
    sintab = p2_es.enter_context(nc.sbuf_tensor("sintab", [128, S], BF16, side="right"))
    costab = p2_es.enter_context(nc.sbuf_tensor("costab", [128, S], BF16, side="right"))
    rtmp = [p2_es.enter_context(nc.sbuf_tensor(f"rtmp{i}", [128, S], BF16, side="right"))
            for i in range(2)]
    freqs = p2_es.enter_context(nc.sbuf_tensor("freqs", [128, 8], F32, side="right"))
    wqb = [p2_es.enter_context(nc.sbuf_tensor(f"wqb{i}", [128, 2048], BF16, side="right"))
           for i in range(2)]
    vts = [p2_es.enter_context(nc.sbuf_tensor(f"vts{i}", [128, S], BF16, side="right"))
           for i in range(2)]
    pqk = [p2_es.enter_context(nc.psum_tensor(f"pqk{i}", [128, 512], F32)) for i in range(4)]
    pbt2A = p2_es.enter_context(nc.psum_tensor("pbt2A", [128, 1024], BF16))
    pbt2B = p2_es.enter_context(nc.psum_tensor("pbt2B", [128, 1024], BF16))

    qk_evt = {}
    wqb_free = [None, None]
    bank_free = [None] * 4
    for (pname, wd, dst, scl) in (("q", wqT_d, QT, 0.125), ("k", wkT_d, KT, 1.0)):
        for qtr in range(4):
            wi = qtr % 2
            for e8 in range(8):
                P.op("gpsimd", (lambda wd_=wd, q_=qtr, e_=e8, i_=wi: (lambda e: e.dma_start(
                    out=wqb[i_][:, 256 * e_:256 * (e_ + 1)],
                    in_=wd_[128 * e_:128 * (e_ + 1), 256 * q_:256 * (q_ + 1)])))(),
                     waits=[wqb_free[wi]] if e8 == 0 else (), inc=f"dw{wi}")
            ev_w = P.now(f"dw{wi}")
            last_mm = None
            for d2 in range(2):
                ch = 2 * qtr + d2
                for e8 in range(8):
                    for sc in range(4):
                        wts = [ev_w, ev_At]
                        if e8 == 0:
                            wts.append(bank_free[sc])
                        P.op("tensor", (lambda i_=wi, e_=e8, d_=d2, s_=sc: (lambda e: e.matmul(
                            pqk[s_][:],
                            wqb[i_][:, 256 * e_ + 128 * d_:256 * e_ + 128 * (d_ + 1)],
                            At[:, S * e_ + 512 * s_:S * e_ + 512 * (s_ + 1)],
                            start=(e_ == 0), stop=(e_ == 7))))(),
                             waits=wts if e8 == 0 else (), inc="t")
                ev_mm = P.now("t")
                last_mm = ev_mm
                for sc in range(4):
                    P.op("scalar", (lambda s_=sc, ch_=ch, dst_=dst, sc_=scl: (lambda e: e.activation(
                        dst_[:, S * ch_ + 512 * s_:S * ch_ + 512 * (s_ + 1)],
                        pqk[s_][:], AF.Copy, bias=0.0, scale=sc_)))(),
                         waits=[ev_mm], inc="s")
                    bank_free[sc] = P.now("s")
                qk_evt[(pname, ch)] = P.now("s")
            wqb_free[wi] = last_mm

    P.op("gpsimd", lambda e: e.iota(iotaS[:], [[1, S]], base=0, channel_multiplier=0,
                                    allow_small_or_imprecise_dtypes=True), inc="g")
    ev_iotaS = P.now("g")
    P.op("gpsimd", lambda e: e.iota(freqs[:, 0:1], [[0, 1]], base=0, channel_multiplier=1,
                                    allow_small_or_imprecise_dtypes=True), inc="g")
    ev_iotaP = P.now("g")
    for c in range(4):
        P.op("scalar", (lambda c_=c: (lambda e: e.activation(
            freqs[:, 4 + c_:5 + c_], freqs[:, 0:1], AF.Exp,
            bias=cpool[:, CP_FRQ0 + c_:CP_FRQ0 + c_ + 1], scale=-CLN)))(),
             waits=[ev_iotaP, ev_cpool], inc="s")
    ev_freq = P.now("s")
    P.op("vector", lambda e: e.tensor_scalar_mul(freqs[:, 0:4], freqs[:, 4:8],
                                                 1.0 / TWO_PI),
         waits=[ev_freq], inc="v")
    ev_freq2 = P.now("v")

    def emit_tab(tab, c, add_quarter, waits0):
        if add_quarter:
            P.op("vector", (lambda c_=c: (lambda e: e.tensor_scalar(
                scrW[:], iotaS[:], freqs[:, c_:c_ + 1], 0.25, OP.mult, OP.add)))(),
                 waits=list(waits0), inc="v")
        else:
            P.op("vector", (lambda c_=c: (lambda e: e.tensor_scalar_mul(
                scrW[:], iotaS[:], freqs[:, c_:c_ + 1])))(),
                 waits=list(waits0), inc="v")
        for hf in range(4):
            P.op("vector", (lambda h_=hf: (lambda e: e.tensor_copy(
                scrI[:], scrW[:, 512 * h_:512 * (h_ + 1)])))(), inc="v")
            P.op("vector", lambda e: e.tensor_copy(scrB[:], scrI[:]), inc="v")
            P.op("vector", (lambda h_=hf: (lambda e: e.tensor_tensor(
                scrW[:, 512 * h_:512 * (h_ + 1)],
                scrW[:, 512 * h_:512 * (h_ + 1)], scrB[:], OP.subtract)))(), inc="v")
        P.op("vector", lambda e: e.scalar_tensor_tensor(
            scrW[:], scrW[:], 0.5, scrW[:], OP.is_ge, OP.subtract), inc="v")
        ev_w = P.now("v")
        P.op("scalar", (lambda t_=tab: (lambda e: e.activation(
            t_[:], scrW[:], AF.Sin, bias=0.0, scale=-TWO_PI)))(),
             waits=[ev_w, prev_tab_reader[0]], inc="s")
        return P.now("s")

    rope_evt = {}
    prev_tab_reader = [None]
    prev_act = [None]
    for c in range(4):
        prev_act[0] = emit_tab(sintab, c, False, [ev_iotaS, ev_freq2, prev_act[0]])
        ev_cos = emit_tab(costab, c, True, [ev_iotaS, ev_freq2, prev_act[0]])
        prev_act[0] = ev_cos
        for (pname, T) in (("q", QT), ("k", KT)):
            lo = lambda T_=T, o_=S * c: T_[:, o_:o_ + S]
            hi = lambda T_=T, o_=S * (c + 4): T_[:, o_:o_ + S]
            P.op("vector", (lambda hi_=hi(): (lambda e: e.tensor_mul(
                rtmp[0][:], hi_, sintab[:])))(),
                 waits=[ev_cos, qk_evt[(pname, c)], qk_evt[(pname, c + 4)]], inc="v")
            P.op("vector", (lambda hi_=hi(): (lambda e: e.tensor_mul(
                rtmp[1][:], hi_, costab[:])))(), inc="v")
            P.op("vector", (lambda lo_=lo(), hi_=hi(): (lambda e: e.tensor_mul(
                hi_, lo_, sintab[:])))(), inc="v")
            P.op("vector", (lambda hi_=hi(): (lambda e: e.tensor_add(
                hi_, hi_, rtmp[1][:])))(), inc="v")
            P.op("vector", (lambda lo_=lo(): (lambda e: e.tensor_mul(
                lo_, lo_, costab[:])))(), inc="v")
            P.op("vector", (lambda lo_=lo(): (lambda e: e.tensor_sub(
                lo_, lo_, rtmp[0][:])))(), inc="v")
            rope_evt[(pname, c)] = rope_evt[(pname, c + 4)] = P.now("v")
        prev_tab_reader[0] = P.now("v")

    if phase_limit <= 2:
        dump_dbg(QT[:, 0:16384], 16384, P.now("v"))
        return finish()

    for sl in range(NSL):
        P.op("vector", (lambda s_=sl: (lambda e: e.memset(
            Vx[:, 1040 * s_:1040 * (s_ + 1)].rearrange(
                "p (a b) -> p a b", a=16)[:, :, 64:65], 1.0)))(),
             inc="v" if sl == NSL - 1 else None)
    ones_evt = P.now("v")

    vst = {}
    vts_free = [None, None]
    for vq in range(4):
        wi = vq % 2
        for e8 in range(8):
            P.op("gpsimd", (lambda q_=vq, e_=e8, i_=wi: (lambda e: e.dma_start(
                out=wqb[i_][:, 256 * e_:256 * (e_ + 1)],
                in_=wvT_d[128 * e_:128 * (e_ + 1), 256 * q_:256 * (q_ + 1)])))(),
                 waits=[wqb_free[wi]] if e8 == 0 else (), inc=f"dw{wi}")
        ev_w = P.now(f"dw{wi}")
        last_mm = None
        for d2 in range(2):
            dvch = 2 * vq + d2
            vb = dvch % 2
            for e8 in range(8):
                for sc in range(4):
                    wts = [ev_w, ev_At]
                    if e8 == 0:
                        wts.append(bank_free[sc])
                    P.op("tensor", (lambda i_=wi, e_=e8, d_=d2, s_=sc: (lambda e: e.matmul(
                        pqk[s_][:],
                        wqb[i_][:, 256 * e_ + 128 * d_:256 * e_ + 128 * (d_ + 1)],
                        At[:, S * e_ + 512 * s_:S * e_ + 512 * (s_ + 1)],
                        start=(e_ == 0), stop=(e_ == 7))))(),
                         waits=wts if e8 == 0 else (), inc="t")
            ev_mm = P.now("t")
            last_mm = ev_mm
            for sc in range(4):
                P.op("scalar", (lambda s_=sc, v_=vb: (lambda e: e.activation(
                    vts[v_][:, 512 * s_:512 * (s_ + 1)], pqk[s_][:], AF.Copy,
                    bias=0.0, scale=1.0)))(),
                     waits=[ev_mm, vts_free[vb]], inc="s")
                bank_free[sc] = P.now("s")
            ev_vts = P.now("s")
            h0 = 2 * dvch
            emit_transposes(
                vts[vb], NSL,
                (lambda h_=h0: (lambda i: Vx[:, 1040 * i + 65 * h_: 1040 * i + 65 * h_ + 130].rearrange(
                    "p (a b) -> p a b", a=2)[:, :, 0:64]))(),
                [ev_vts, ones_evt], (pbt2A, pbt2B), vst)
            vts_free[vb] = P.now("t")
        wqb_free[wi] = last_mm
    ev_Vx = P.now("v")

    if phase_limit <= 3:
        dump_dbg(Vx[:, 0:16384], 16384, ev_Vx)
        return finish()

    p2_es.close()
    phase_stacks.remove(p2_es)
    es_at.close()
    phase_stacks.remove(es_at)
    P.barrier()
    p3_es = ExitStack()
    phase_stacks.append(p3_es)
    p3p_es = ExitStack()
    phase_stacks.append(p3p_es)
    Ost = p3_es.enter_context(nc.sbuf_tensor("Ost", [128, NH * S], BF16, side="right"))
    dfar = [p3_es.enter_context(nc.sbuf_tensor(f"dfar{j}", [128, 512], BF16, side="right"))
            for j in range(2)]
    ddg = [[p3_es.enter_context(nc.sbuf_tensor(f"ddg{j}_{i}", [128, 512], BF16, side="right"))
            for i in range(4)] for j in range(2)]
    ebuf = [p3_es.enter_context(nc.sbuf_tensor(f"ebuf{i}", [128, 1024], BF16, side="right"))
            for i in range(2)]
    pbuf = [p3_es.enter_context(nc.sbuf_tensor(f"pbuf{i}", [128, 1024], BF16, side="right"))
            for i in range(4)]
    rbt = [p3_es.enter_context(nc.sbuf_tensor(f"rbt{i}", [128, 512], F32, side="right"))
           for i in range(1)]
    ps = [p3p_es.enter_context(nc.psum_tensor(f"ps{i}", [128, 1024], F32)) for i in range(2)]
    po = [p3p_es.enter_context(nc.psum_tensor(f"po{i}", [128, 512], F32)) for i in range(2)]
    pc = p3p_es.enter_context(nc.psum_tensor("pc", [128, 512], F32))

    ba = [p3_es.enter_context(nc.sbuf_tensor(f"ba{i}", [128, 128], BF16, side="right"))
          for i in range(2)]
    bbuf = [p3_es.enter_context(nc.sbuf_tensor(f"bbuf{i}", [128, 512], BF16, side="right"))
            for i in range(2)]
    w2asm = [p3_es.enter_context(nc.sbuf_tensor(f"w2asm{i}", [128, 512], BF16, side="right"))
             for i in range(2)]
    wcv = [p3_es.enter_context(nc.sbuf_tensor(f"wcv{i}", [128, D // 2], BF16, side="right"))
           for i in range(2)]
    pw2 = p3p_es.enter_context(nc.psum_tensor("pw2", [128, 512], F32))

    ba_free = [None, None]
    bb_free = [None, None]
    w2_free = [None, None]
    pw2_free = None
    for h4 in range(4):
        for oc in range(8):
            i = (h4 * 8 + oc) % 2
            P.op("gpsimd", (lambda h_=h4, o_=oc, i_=i: (lambda e: e.dma_start(
                out=ba[i_][:], in_=blkA_d[h_, o_])))(),
                 waits=[ba_free[i]], inc=f"dba{i}")
            ev_lda = P.now(f"dba{i}")
            P.op("gpsimd", (lambda h_=h4, o_=oc, i_=i: (lambda e: e.dma_start(
                out=bbuf[i_][:], in_=blkB_d[h_, o_])))(),
                 waits=[bb_free[i]], inc=f"dbb{i}")
            ev_ldb = P.now(f"dbb{i}")
            P.op("tensor", (lambda i_=i: (lambda e: e.matmul(
                pw2[:], ba[i_][:], bbuf[i_][:], start=True, stop=True)))(),
                 waits=[ev_lda, ev_ldb, pw2_free], inc="t")
            ev_mm = P.now("t")
            ba_free[i] = ev_mm
            bb_free[i] = ev_mm
            P.op("vector", (lambda i_=i: (lambda e: e.tensor_copy(w2asm[i_][:], pw2[:])))(),
                 waits=[ev_mm, w2_free[i]], inc="v")
            pw2_free = P.now("v")
            P.op("scalar", (lambda h_=h4, o_=oc, i_=i: (lambda e: e.dma_start(
                out=w2scr_d[h_, o_], in_=w2asm[i_][:])))(),
                 waits=[pw2_free], inc=f"dws{i}")
            w2_free[i] = P.now(f"dws{i}")
    ev_w2s = [P.now("dws0"), P.now("dws1")]

    wcv_ld = [None, None]
    wcv_st = [None, None]
    for mh in range(256):
        mc, hh = divmod(mh, 2)
        i = mh % 2
        P.op("gpsimd", (lambda m_=mc, h_=hh, i_=i: (lambda e: e.dma_start(
            out=wcv[i_][:], in_=wcT_d[128 * m_:128 * (m_ + 1), 512 * h_:512 * (h_ + 1)])))(),
             waits=[wcv_st[i]], inc=f"dcv{i}")
        wcv_ld[i] = P.now(f"dcv{i}")
        P.op("sync", (lambda m_=mc, h_=hh, i_=i: (lambda e: e.dma_start(
            out=wcb_d[128 * m_:128 * (m_ + 1), 512 * h_:512 * (h_ + 1)], in_=wcv[i_][:])))(),
             waits=[wcv_ld[i]], inc=f"dcb{i}")
        wcv_st[i] = P.now(f"dcb{i}")
    ev_wcb_done = [P.now("dcb0"), P.now("dcb1")]

    ps_free = [None] * 2
    eb_free = [None] * 2
    pb_free = [None] * 4
    pend_s = []
    pair_no = [0]
    po_free = [None, None]
    pc_free = None
    rb_free = None
    drow_free = None
    rrow_free = None
    ost_evt = {}

    LOOK = 6
    jobs = []
    for h in range(NH):
        for qc in range(4):
            kmin = kc_min_for(h, qc)
            diag = [4 * qc + i for i in range(4)]
            far = [k for k in range(kmin, 4 * qc)]
            jobs.append((h, qc, diag + far))
    chunks = []
    for ji, (h, qc, kcs) in enumerate(jobs):
        for i, kc in enumerate(kcs):
            chunks.append((ji, h, qc, kc, i == 0, i == len(kcs) - 1))
    NCK = len(chunks)

    dtile_evt = {}
    dtile_lastuse = {}
    s1evt = [None] * NCK
    tails1 = []
    ev_avlast = {}

    def emit_dtiles(h):
        if h >= NH or h in dtile_evt:
            return
        hp = h % 2
        sh = slope(h)
        prev = dtile_lastuse.get(h - 2)
        P.op("scalar", (lambda sh_=sh, j_=hp: (lambda e: e.activation(
            dfar[j_][:], xfar[:], AF.Exp, bias=0.0, scale=sh_)))(),
             waits=[ev_gconst, prev], inc="s")
        for sf in range(4):
            P.op("scalar", (lambda sh_=sh, s_=sf, j_=hp: (lambda e: e.activation(
                ddg[j_][s_][:], xdg[s_][:], AF.Exp, bias=0.0, scale=sh_)))(), inc="s")
        dtile_evt[h] = P.now("s")

    cur_pair = [0]
    chunk_slot = {}

    def stage1(cj):
        ji, h, qc, kc, first, last = chunks[cj]
        if h not in dtile_evt:
            emit_dtiles(h)
        if qc == 2 and first:
            emit_dtiles(h + 1)
        hb = 64 * (h % 2)
        hc = h // 2
        pr = cur_pair[0]
        hf = len(pend_s)
        chunk_slot[cj] = (pr, hf, pair_no[0] % 4)
        P.op("tensor", (lambda p_=pr, f_=hf, kc_=kc, hb_=hb, hc_=hc, qc_=qc: (lambda e: e.matmul(
            ps[p_][:, 512 * f_:512 * (f_ + 1)],
            KT[hb_:hb_ + 64, S * hc_ + 128 * kc_:S * hc_ + 128 * (kc_ + 1)],
            QT[hb_:hb_ + 64, S * hc_ + 512 * qc_:S * hc_ + 512 * (qc_ + 1)],
            start=True, stop=True)))(),
             waits=[rope_evt[("q", hc)], rope_evt[("k", hc)],
                    ps_free[pr] if hf == 0 else None], inc="t")
        pend_s.append((cj, P.now("t")))
        if hf == 1 or last:
            flush_pair()

    def cidx_of(cjx):
        jx, hx, qx, kx, _f, _l = chunks[cjx]
        d = 4 * qx - kx
        return CP_M12 if kx >= 4 * qx else CP_FAR0 + hx * 12 + (d - 1)

    def dsel_of(cjx):
        jx, hx, qx, kx, _f, _l = chunks[cjx]
        return ddg[hx % 2][kx - 4 * qx] if kx >= 4 * qx else dfar[hx % 2]

    def flush_pair():
        if not pend_s:
            return
        items = list(pend_s)
        del pend_s[:]
        cur_pair[0] ^= 1
        pair_no[0] += 1
        cj0 = items[0][0]
        pr = chunk_slot[cj0][0]
        n = len(items)
        h = chunks[cj0][1]
        idxs = [cidx_of(c[0]) for c in items]
        evs = [c[1] for c in items]
        if n == 2 and idxs[0] == idxs[1]:
            P.op("scalar", (lambda p_=pr, ci=idxs[0]: (lambda e: e.activation(
                ebuf[p_][:], ps[p_][:], AF.Exp,
                bias=cpool[:, ci:ci + 1], scale=1.0)))(),
                 waits=evs + [ev_cpool, eb_free[pr], dtile_evt[h]], inc="s")
        else:
            for (cjx, evx), ci in zip(items, idxs):
                f = chunk_slot[cjx][1]
                P.op("scalar", (lambda p_=pr, f_=f, ci_=ci: (lambda e: e.activation(
                    ebuf[p_][:, 512 * f_:512 * (f_ + 1)],
                    ps[p_][:, 512 * f_:512 * (f_ + 1)], AF.Exp,
                    bias=cpool[:, ci_:ci_ + 1], scale=1.0)))(),
                     waits=[evx, ev_cpool,
                            eb_free[pr] if cjx == cj0 else None,
                            dtile_evt[h]], inc="s")
        ps_free[pr] = P.now("s")
        ev_e = P.now("s")
        pbi = chunk_slot[cj0][2]
        for (cjx, _e) in items:
            f = chunk_slot[cjx][1]
            dd = dsel_of(cjx)
            P.op("vector", (lambda p_=pr, b_=pbi, f_=f, d_=dd: (lambda e: e.tensor_mul(
                pbuf[b_][:, 512 * f_:512 * (f_ + 1)],
                ebuf[p_][:, 512 * f_:512 * (f_ + 1)], d_[:])))(),
                 waits=[ev_e] + ([pb_free[pbi]] if cjx == cj0 else []), inc="v")
        eb_free[pr] = P.now("v")
        for (cjx, _e) in items:
            s1evt[cjx] = P.now("v")
        dtile_lastuse[h] = P.now("v")

    P.op("vector", lambda e: e.memset(sqd[64:66, :], 1.0), inc="v")
    ev_sq65 = P.now("v")
    P.op("vector", lambda e: e.memset(Ost[64:128, :], 0.0), inc="v")
    ev_ost0 = P.now("v")

    s2count = [0]
    t1q = []
    t2q = []

    def stage2(cj):
        ji, h, qc, kc, first, last = chunks[cj]
        pr, hf, pbi = chunk_slot[cj]
        pob = ji % 2
        if first:
            drain_parity(pob)
        P.op("tensor", (lambda b_=pbi, f2=hf, kc_=kc, h_=h, pb_=pob, f_=first, l_=last: (lambda e: e.matmul(
            po[pb_][0:65, :],
            Vx[:, 1040 * kc_ + 65 * h_:1040 * kc_ + 65 * (h_ + 1)],
            pbuf[b_][:, 512 * f2:512 * (f2 + 1)], start=f_, stop=l_)))(),
             waits=[s1evt[cj], ev_Vx] + ([po_free[pob]] if first else []), inc="t")
        if chunks[cj][5] or hf == 1:
            pb_free[pbi] = P.now("t")
        s2count[0] += 1
        if last:
            ev_av = P.now("t")
            P.op("vector", (lambda p_=pob: (lambda e: e.tensor_copy(
                sqd[64:65, 512 * p_:512 * (p_ + 1)], po[p_][64:65, :])))(),
                 waits=[ev_av, ev_vconst, drow_evt[pob]], inc="v")
            t1q.append((s2count[0] + 2, ji, P.now("v")))

    def drain_parity(pob):
        for q, fn in ((t1q, _t1_emit), (t2q, _t2_emit)):
            keep = []
            for item in q:
                if item[1] % 2 == pob:
                    fn(item)
                else:
                    keep.append(item)
            q[:] = keep

    def _t1_emit(item):
        _, ji, ev_dc = item
        _emit_t1(ji, ev_dc)

    def _t2_emit(item):
        _, ji, ev_rr = item
        _emit_t2(ji, ev_rr)

    def _emit_t1(ji, ev_dc):
        pob = ji % 2
        P.op("scalar", (lambda p_=pob: (lambda e: e.activation(
            rrow[64:66, 512 * p_:512 * (p_ + 1)],
            sqd[64:66, 512 * p_:512 * (p_ + 1)],
            AF.Ln, bias=0.0, scale=1.0)))(),
             waits=[ev_dc, rrow_evt[pob]], inc="s")
        drow_evt[pob] = P.now("s")
        P.op("scalar", (lambda p_=pob: (lambda e: e.activation(
            rrow[64:66, 512 * p_:512 * (p_ + 1)],
            rrow[64:66, 512 * p_:512 * (p_ + 1)],
            AF.Exp, bias=0.0, scale=-1.0)))(), inc="s")
        t2q.append((s2count[0] + 4, ji, P.now("s")))

    def _emit_t2(ji, ev_rr):
        h, qc, kcs = jobs[ji]
        pob = ji % 2
        P.op("tensor", (lambda p_=pob: (lambda e: e.matmul(
            pc[:], onesF[64:65, 0:128],
            rrow[64:65, 512 * p_:512 * (p_ + 1)], start=True, stop=True)))(),
             waits=[ev_rr, ev_vconst, pc_evt[0]], inc="t")
        ev_bc = P.now("t")
        rrow_evt[pob] = ev_bc
        P.op("vector", (lambda: (lambda e: e.tensor_copy(rbt[0][:], pc[:])))(),
             waits=[ev_bc, rb_evt[0]], inc="v")
        pc_evt[0] = P.now("v")
        P.op("vector", (lambda h_=h, q_=qc, p_=pob: (lambda e: e.tensor_mul(
            Ost[0:64, S * h_ + 512 * q_:S * h_ + 512 * (q_ + 1)],
            po[p_][0:64, :], rbt[0][0:64, :])))(), inc="v")
        ev = P.now("v")
        po_free[pob] = ev
        rb_evt[0] = ev
        ost_evt[(h, qc)] = ev

    def drain_tails(now, flush=False):
        while t1q and (flush or t1q[0][0] <= now):
            _, ji, ev_dc = t1q.pop(0)
            _emit_t1(ji, ev_dc)
        while t2q and (flush or t2q[0][0] <= now):
            _, ji, ev_rr = t2q.pop(0)
            _emit_t2(ji, ev_rr)

    drow_evt = [None, None]
    rrow_evt = [None, None]
    pc_evt = [None]
    rb_evt = [None]

    for j in range(NCK + LOOK):
        if j >= LOOK:
            stage2(j - LOOK)
            drain_tails(s2count[0])
        if j < NCK:
            stage1(j)
    drain_tails(0, flush=True)

    ev_attn = P.now("v")
    if phase_limit <= 4:
        dump_dbg(Ost[0:64, 0:16384], 16384, ev_attn, rows=64)
        return finish()

    p3p_es.close()
    phase_stacks.remove(p3p_es)
    es_qkv.close()
    phase_stacks.remove(es_qkv)
    P.barrier()
    p3b_es = ExitStack()
    phase_stacks.append(p3b_es)
    owt = p3b_es.enter_context(nc.sbuf_tensor("owt", [128, NH * 1024], BF16, side="right"))
    xrb = [p3b_es.enter_context(nc.sbuf_tensor(f"xrb{i}", [128, D], F32, side="right"))
           for i in range(2)]
    xnw = [p3b_es.enter_context(nc.sbuf_tensor(f"xnw{i}", [128, D], F32, side="right"))
           for i in range(2)]
    btm = [p3b_es.enter_context(nc.sbuf_tensor(f"btm{i}", [128, D], BF16, side="right"))
           for i in range(2)]
    btasm = [p3b_es.enter_context(nc.sbuf_tensor(f"btasm{i}", [128, D], BF16, side="right"))
             for i in range(2)]
    pat = [p3b_es.enter_context(nc.psum_tensor(f"pat{i}", [128, 512], F32)) for i in range(4)]
    pbt3A = p3b_es.enter_context(nc.psum_tensor("pbt3A", [128, 1024], BF16))
    pbt3B = p3b_es.enter_context(nc.psum_tensor("pbt3B", [128, 1024], BF16))

    P.op("vector", lambda e: e.memset(owt[64:128, :], 0.0), inc="v")
    ev_owt0 = P.now("v")
    for h in range(NH):
        P.op("gpsimd", (lambda h_=h: (lambda e: e.dma_start(
            out=owt[0:64, 1024 * h_:1024 * (h_ + 1)],
            in_=owT_d[64 * h_:64 * (h_ + 1), :])))(), inc="dow")
    ev_owt = P.now("dow")

    xrb_free = [None, None]
    xnw_free = [[], []]
    pat_free = [None] * 4
    btm_free = [None, None]
    btasm_free = [None, None]
    bst = {}
    bt_pend = [None]

    def emit_bt(sl, bi, ev_b):
        wts0 = [ev_b] + ([btasm_free[bi]] if btasm_free[bi] else [])
        ev_t, ev_bt = emit_transposes(
            btm[bi], NCH, (lambda b_=bi: (lambda i: btasm[b_][:, 128 * i:128 * (i + 1)]))(),
            wts0, (pbt3A, pbt3B), bst)
        btm_free[bi] = ev_t
        P.op("scalar", (lambda s_=sl, b_=bi: (lambda e: e.dma_start(
            out=btscr_d[s_], in_=btasm[b_][:].rearrange("p (a b) -> p a b", a=8))))(),
             waits=[ev_bt], inc=f"dbt{bi}")
        btasm_free[bi] = P.now(f"dbt{bi}")

    for sl in range(NSL):
        bi = sl % 2
        P.op("sync", (lambda s_=sl, b=bi: (lambda e: e.dma_start(
            out=xrb[b][:], in_=x_d[128 * s_:128 * (s_ + 1), :])))(),
             waits=[xrb_free[bi]], inc=f"dxr{bi}")
        ev_xr = P.now(f"dxr{bi}")
        qcsl = sl // 4
        for e2 in range(2):
            pi = (sl % 2) * 2 + e2
            wts = [ev_owt, ev_owt0, ev_ost0, ev_browb, pat_free[pi]] + [ost_evt[(h, qcsl)] for h in range(NH)]
            for h in range(NH):
                P.op("tensor", (lambda h_=h, e_=e2, s_=sl, p_=pi: (lambda e: e.matmul(
                    pat[p_][:],
                    Ost[0:128, S * h_ + 128 * s_:S * h_ + 128 * (s_ + 1)],
                    owt[0:128, 1024 * h_ + 512 * e_:1024 * h_ + 512 * (e_ + 1)],
                    start=(h_ == 0), stop=False)))(),
                     waits=wts if h == 0 else (), inc="t")
            P.op("tensor", (lambda e_=e2, p_=pi: (lambda e: e.matmul(
                pat[p_][:], onesB[0:1, 0:128], browb[0:1, 512 * e_:512 * (e_ + 1)],
                start=False, stop=True)))(), inc="t")
        if bt_pend[0] is not None:
            emit_bt(*bt_pend[0])
            bt_pend[0] = None
        ev_op = P.now("t")
        for e2 in range(2):
            pi = (sl % 2) * 2 + e2
            P.op("vector", (lambda b=bi, e_=e2, p_=pi: (lambda e: e.tensor_add(
                xnw[b][:, 512 * e_:512 * (e_ + 1)], pat[p_][:],
                xrb[b][:, 512 * e_:512 * (e_ + 1)])))(),
                 waits=[ev_op, ev_xr] + (xnw_free[bi] if e2 == 0 else []), inc="v")
            pat_free[pi] = P.now("v")
        ev_xn = P.now("v")
        xrb_free[bi] = ev_xn
        P.op("scalar", (lambda s_=sl, b=bi: (lambda e: e.dma_start(
            out=xres_d[128 * s_:128 * (s_ + 1), :], in_=xnw[b][:])))(),
             waits=[ev_xn], inc=f"dsp{bi}")
        ev_spill = P.now(f"dsp{bi}")
        ev_b = emit_ln(xnw[bi], btm[bi], [ev_xn, btm_free[bi]], 8 * bi)
        xnw_free[bi] = [ev_spill, ev_b]
        bt_pend[0] = (sl, bi, ev_b)
    if bt_pend[0] is not None:
        emit_bt(*bt_pend[0])
        bt_pend[0] = None

    ev_spills = [P.now("dsp0"), P.now("dsp1")]
    ev_btspill = [P.now("dbt0"), P.now("dbt1")]

    if phase_limit <= 5:
        return finish()

    p3b_es.close()
    phase_stacks.remove(p3b_es)
    p3_es.close()
    phase_stacks.remove(p3_es)
    P.barrier()
    p4_es = ExitStack()
    phase_stacks.append(p4_es)
    SB2 = 256
    NB2 = S // SB2
    Hb2 = [p4_es.enter_context(nc.sbuf_tensor(f"Hb2{i}", [128, 128 * SB2], BF16))
           for i in range(2)]
    btb = [p4_es.enter_context(nc.sbuf_tensor(f"btb{i}", [128, 8 * SB2], BF16))
           for i in range(2)]
    w2b = [p4_es.enter_context(nc.sbuf_tensor(f"w2b{i}", [128, 512], BF16)) for i in range(2)]
    wcs = [p4_es.enter_context(nc.sbuf_tensor(f"wcs{i}", [128, D], BF16)) for i in range(10)]
    xr2 = [p4_es.enter_context(nc.sbuf_tensor(f"xr2{i}", [128, D], F32)) for i in range(2)]
    ob2 = [p4_es.enter_context(nc.sbuf_tensor(f"ob2{i}", [128, D], F32)) for i in range(2)]
    pf = [p4_es.enter_context(nc.psum_tensor(f"pf{i}", [128, 512], F32)) for i in range(8)]

    wcs_free = [None] * 10
    w2b_free = [None, None]
    xr2_free = [None, None]
    ob2_free = [None, None]
    pf_free = [None] * 8
    btb_free = [None, None]
    hb2_free = [None, None]
    rails_done = {}
    cp_done = {-1: None}

    def load_btb(bk):
        if bk >= NB2:
            return
        p = bk % 2
        for oc in range(8):
            P.op("sync", (lambda o_=oc, b_=bk, p_=p: (lambda e: e.dma_start(
                out=btb[p_][:, SB2 * o_:SB2 * (o_ + 1)].rearrange("p (a b) -> p a b", a=2),
                in_=btscr_d[2 * b_:2 * (b_ + 1), :, o_, :].rearrange("a p b -> p a b"))))(),
                 waits=(ev_btspill + [btb_free[p]]) if oc == 0 else (), inc=f"dtb{p}")
        btb_free[p] = None
        return P.now(f"dtb{p}")

    btb_evt = {}
    btb_evt[0] = load_btb(0)

    def rails_unit(bk, u, standalone=False):
        p = bk % 2
        h4, rem = divmod(u, 32)
        oc, j = divmod(rem, 4)
        i = (u // 4) % 2
        if j == 0:
            P.op("sync", (lambda h_=h4, o_=oc, i_=i: (lambda e: e.dma_start(
                out=w2b[i_][:], in_=w2scr_d[h_, o_])))(),
                 waits=[ev_w2s[0], ev_w2s[1], w2b_free[i]], inc=f"dw2{i}")
        ev_wl = P.now(f"dw2{i}")
        zb = 4 + (u % 4)
        mwts = [ev_wl, btb_evt[bk], pf_free[zb]]
        P.op("tensor", (lambda i_=i, j_=j, o_=oc, z_=zb, p_=p: (lambda e: e.matmul(
            pf[z_][:, 0:SB2], w2b[i_][:, 128 * j_:128 * (j_ + 1)],
            btb[p_][:, SB2 * o_:SB2 * (o_ + 1)], start=True, stop=True)))(),
             waits=mwts, inc="t")
        ev_z = P.now("t")
        if j == 3:
            w2b_free[i] = ev_z
        if u == 127:
            btb_free[p] = ev_z
        mc = u
        gw = [ev_z]
        if u == 0:
            gw.append(hb2_free[p])
        P.op("scalar", (lambda m_=mc, z_=zb, p_=p: (lambda e: e.activation(
            Hb2[p_][:, SB2 * m_:SB2 * (m_ + 1)], pf[z_][:, 0:SB2], AF.Gelu,
            bias=0.0, scale=1.0)))(),
             waits=gw, inc="s")
        pf_free[zb] = P.now("s")
        if u == 127:
            rails_done[bk] = P.now("s")

    btb_evt[1] = load_btb(1)
    for u in range(128):
        rails_unit(0, u)

    for bk in range(NB2):
        p = bk % 2
        if bk + 2 < NB2:
            btb_evt[bk + 2] = load_btb(bk + 2)
        xr_evts = {}
        for sl2 in (1, 0):
            q4 = sl2
            sl = 2 * bk + sl2
            P.op("sync", (lambda s_=sl, q_=q4: (lambda e: e.dma_start(
                out=xr2[q_][:], in_=xres_d[128 * s_:128 * (s_ + 1), :])))(),
                 waits=[xr2_free[q4], ev_spills[0], ev_spills[1]], inc=f"dr2{q4}")
            xr_evts[sl2] = P.now(f"dr2{q4}")
        for mc in range(128):
            i = mc % 10
            P.op("sync", (lambda m_=mc, i_=i: (lambda e: e.dma_start(
                out=wcs[i_][:], in_=wcb_d[128 * m_:128 * (m_ + 1), :])))(),
                 waits=[wcs_free[i]] + (ev_wcb_done if (bk == 0 and mc < 10) else []),
                 inc=f"dwc{i}")
            ev_wc = P.now(f"dwc{i}")
            for sl2 in range(2):
                for e2 in range(2):
                    pb_i = sl2 * 2 + e2
                    mwts = [ev_wc, rails_done[bk]]
                    if mc == 0:
                        mwts.append(pf_free[pb_i])
                    P.op("tensor", (lambda m_=mc, s_=sl2, e_=e2, pi=pb_i, i_=i, p_=p: (lambda e: e.matmul(
                        pf[pi][:], Hb2[p_][:, SB2 * m_ + 128 * s_:SB2 * m_ + 128 * (s_ + 1)],
                        wcs[i_][:, 512 * e_:512 * (e_ + 1)],
                        start=(m_ == 0), stop=(m_ == 127))))(),
                         waits=mwts, inc="t")
            wcs_free[i] = P.now("t")
            if bk + 1 < NB2:
                rails_unit(bk + 1, mc)
        ev_cp = P.now("t")
        cp_done[bk] = ev_cp
        hb2_free[(bk + 2) % 2] = ev_cp
        for sl2 in (1, 0):
            sl = 2 * bk + sl2
            q4 = sl2
            ev_xr = xr_evts[sl2]
            for e2 in range(2):
                pb_i = sl2 * 2 + e2
                P.op("vector", (lambda q_=q4, e_=e2, pi=pb_i: (lambda e: e.tensor_add(
                    ob2[q_][:, 512 * e_:512 * (e_ + 1)], pf[pi][:],
                    xr2[q_][:, 512 * e_:512 * (e_ + 1)])))(),
                     waits=[ev_cp, ev_xr] + ([ob2_free[q4]] if e2 == 0 else []), inc="v")
                pf_free[pb_i] = P.now("v")
            ev_ot = P.now("v")
            xr2_free[q4] = ev_ot
            P.op("scalar", (lambda s_=sl, q_=q4: (lambda e: e.dma_start(
                out=out_d[128 * s_:128 * (s_ + 1), :], in_=ob2[q_][:])))(),
                 waits=[ev_ot], inc=f"dot{q4}")
            ob2_free[q4] = P.now(f"dot{q4}")

    return finish()



_NC_CACHE = {}


def _get_nc(phase_limit=99):
    if phase_limit not in _NC_CACHE:
        _NC_CACHE[phase_limit] = build_nc(phase_limit)
    return _NC_CACHE[phase_limit]


def kernel(L0_attn_x, L0_attn_wq, L0_attn_wk, L0_attn_wv, L0_attn_out_w,
           L0_attn_out_b, L0_ffn_rail_maker, L0_ffn_rail_up_proj, L0_ffn_w_cproj,
           _phase_limit=99, _cores=8):
    x = np.ascontiguousarray(np.asarray(L0_attn_x, dtype=np.float32))
    wq = np.asarray(L0_attn_wq, dtype=np.float32)
    wk = np.asarray(L0_attn_wk, dtype=np.float32)
    wv = np.asarray(L0_attn_wv, dtype=np.float32)
    ow = np.asarray(L0_attn_out_w, dtype=np.float32)
    ob = np.ascontiguousarray(np.asarray(L0_attn_out_b, dtype=np.float32).reshape(1, D))
    rm = np.asarray(L0_ffn_rail_maker, dtype=np.float32)
    ru = np.asarray(L0_ffn_rail_up_proj, dtype=np.float32)
    wc = np.asarray(L0_ffn_w_cproj, dtype=np.float32)

    wqT = np.ascontiguousarray(wq.T)
    wkT = np.ascontiguousarray(wk.T)
    wvT = np.ascontiguousarray(wv.T)
    owT = np.ascontiguousarray(ow.T)
    wcT = np.ascontiguousarray(wc.T)

    blkA = np.zeros((4, 8, 128, 128), dtype=np.float32)
    blkB = np.zeros((4, 8, 128, 512), dtype=np.float32)
    for h in range(4):
        for oc in range(8):
            for r in range(8):
                Mb = rm[h, 8 * oc + r]
                Ub = ru[h, 8 * oc + r]
                blkA[h, oc, 16 * r:16 * (r + 1), 16 * r:16 * (r + 1)] = Mb.T
                blkB[h, oc, 16 * r:16 * (r + 1), 64 * r:64 * (r + 1)] = Ub.T

    nc = _get_nc(_phase_limit)
    shared = dict(wqT=wqT, wkT=wkT, wvT=wvT, owT=owT, ob=ob,
                  blkA=blkA, blkB=blkB, wcT=wcT)
    in_maps = [dict(shared, x=np.ascontiguousarray(x[i])) for i in range(_cores)]
    res = run_bass_kernel_spmd(nc, in_maps, core_ids=list(range(_cores)))
    if _phase_limit < 99:
        return res
    return np.stack([np.asarray(r["out"]) for r in res.results], axis=0)
